# version 18
# baseline (speedup 1.0000x reference)
import numpy as np
from contextlib import ExitStack

import concourse.bass as bass
import concourse.mybir as mybir
from concourse.bass_utils import run_bass_kernel_spmd

F32 = mybir.dt.float32
BF16 = mybir.dt.bfloat16
AF = mybir.ActivationFunctionType
ALU = mybir.AluOpType
AX = mybir.AxisListType

T = 2048
D = 1024
CH = 512
NCH = T // CH
FT = D // 128
QT = T // 128
INV1024 = 1.0 / 1024.0
NEG = -1.0e30

L0F_HEADS = 8
L0F_DH = 64
L0F_INNER = 512
L0F_WIN = 512
HID = 2730
HIDP = 2816
NHT = HIDP // 128


def _host_consts():
    c = {}
    c["ident"] = np.eye(128, dtype=np.float32)
    half = D // 2
    inv = 1.0 / (10000.0 ** (np.arange(half, dtype=np.float64) / half))
    ang = np.arange(T, dtype=np.float64)[:, None] * inv[None, :]
    cosv = np.cos(ang).T
    sinv = np.sin(ang).T
    ropeC = np.repeat(cosv, 2, axis=0).astype(np.float32)
    sgn = np.where(np.arange(D) % 2 == 0, -1.0, 1.0)[:, None]
    ropeS = (np.repeat(sinv, 2, axis=0) * sgn).astype(np.float32)
    c["ropeC"] = np.ascontiguousarray(ropeC)
    c["ropeS"] = np.ascontiguousarray(ropeS)
    P = np.zeros((128, 128), dtype=np.float32)
    i2 = np.arange(0, 128, 2)
    P[i2, i2 + 1] = 1.0
    P[i2 + 1, i2] = 1.0
    c["JTB"] = P
    r = np.arange(128)[:, None]
    cc = np.arange(128)[None, :]
    c["M0"] = np.where(cc > r, NEG, 0.0).astype(np.float32)
    c["M4"] = np.where(cc < r, NEG, 0.0).astype(np.float32)
    dh = L0F_DH
    hi = np.arange(0, dh, 2, dtype=np.float64)
    invf = 1.0 / (10000.0 ** (hi / dh))
    tt = np.arange(T, dtype=np.float64)
    fr32 = tt[:, None] * invf[None, :]
    fr = np.concatenate([fr32, fr32], axis=1)
    base = (hi + 0.4 * dh) / (1.4 * dh)
    power = (tt - T // 2) / float(L0F_WIN // 2)
    sc32 = base[None, :] ** power[:, None]
    sc = np.concatenate([sc32, sc32], axis=1)
    cosf = np.cos(fr)
    sinf = np.sin(fr)
    sgn64 = np.where(np.arange(dh) < dh // 2, -1.0, 1.0)[None, :]
    qc = (cosf * sc * 8.0).T
    qs = (sinf * sc * 8.0 * sgn64).T
    kc = (cosf / sc).T
    ks = (sinf / sc * sgn64).T
    c["xposQC"] = np.ascontiguousarray(np.tile(qc, (2, 1)).astype(np.float32))
    c["xposQS"] = np.ascontiguousarray(np.tile(qs, (2, 1)).astype(np.float32))
    c["xposKC"] = np.ascontiguousarray(np.tile(kc, (2, 1)).astype(np.float32))
    c["xposKS"] = np.ascontiguousarray(np.tile(ks, (2, 1)).astype(np.float32))
    R = np.zeros((128, 128), dtype=np.float32)
    for hh in range(2):
        o = hh * 64
        for rr in range(32):
            R[o + rr, o + rr + 32] = 1.0
            R[o + rr + 32, o + rr] = 1.0
    c["RTB"] = R.T.copy()
    HS = np.zeros((128, 2), dtype=np.float32)
    HS[0:64, 0] = 1.0
    HS[64:128, 1] = 1.0
    c["HS"] = HS
    c["HST"] = np.ascontiguousarray(HS.T)
    SELS = np.zeros((8, 8 * 128), dtype=np.float32)
    for e in range(8):
        SELS[e, e * 128:(e + 1) * 128] = 1.0
    c["SELS"] = SELS
    return c


_CONSTS = None

def _consts():
    global _CONSTS
    if _CONSTS is None:
        _CONSTS = _host_consts()
    return _CONSTS



def _timg(W):
    R, Cc = W.shape
    return np.ascontiguousarray(
        W.reshape(R // 128, 128, Cc // 128, 128).transpose(2, 1, 0, 3).reshape(Cc // 128, 128, R))


def _prep_weights(inp):
    w = {}
    for nm, key in (("wqbig", "L0_attn_w_q"), ("wkbig", "L0_attn_w_k"), ("wvbig", "L0_attn_w_v")):
        src = np.asarray(inp[key])
        big = np.zeros((64, 16, 64, 16), dtype=np.float32)
        hh = np.arange(16)
        big[:, hh, :, hh] = src.transpose(2, 0, 1)
        w[nm] = np.ascontiguousarray(big.reshape(1024, 1024))
    w["l0_wqkv"] = np.ascontiguousarray(np.asarray(inp["L0_ffn_wqkv"], np.float32))
    w["l0_wo"] = np.ascontiguousarray(np.asarray(inp["L0_ffn_wo"], np.float32))
    w1 = np.asarray(inp["L0_ffn_w1"], np.float32)
    b1 = np.asarray(inp["L0_ffn_b1"], np.float32)
    w1re = np.zeros((2, 1024, 2 * HIDP), dtype=np.float32)
    b1re = np.zeros((2, 2 * HIDP), dtype=np.float32)
    for l in range(2):
        a = np.zeros((1024, HIDP), np.float32)
        g = np.zeros((1024, HIDP), np.float32)
        a[:, :HID] = w1[l][:, :HID]
        g[:, :HID] = w1[l][:, HID:]
        inter = np.stack([a.reshape(1024, NHT, 128), g.reshape(1024, NHT, 128)], axis=2)
        w1re[l] = inter.reshape(1024, 2 * HIDP)
        ab = np.zeros(HIDP, np.float32)
        gb = np.zeros(HIDP, np.float32)
        ab[:HID] = b1[l][:HID]
        gb[:HID] = b1[l][HID:]
        interb = np.stack([ab.reshape(NHT, 128), gb.reshape(NHT, 128)], axis=1)
        b1re[l] = interb.reshape(2 * HIDP)
    w["l0_w1re"] = w1re
    w["l0_b1c"] = np.ascontiguousarray(
        np.concatenate([b1re[l].reshape(2 * NHT, 128).T for l in range(2)], axis=1))
    w2 = np.asarray(inp["L0_ffn_w2"], np.float32)
    w2p = np.zeros((2, HIDP, 1024), np.float32)
    w2p[:, :HID, :] = w2
    w["l0_w2p"] = w2p
    w["l0_b2c"] = np.ascontiguousarray(
        np.stack([np.asarray(inp["L0_ffn_b2"], np.float32)[l].reshape(8, 128).T for l in range(2)], axis=1).reshape(128, 16))
    qs = np.asarray(inp["L0_ffn_q_scale"], np.float32)
    ks = np.asarray(inp["L0_ffn_k_scale"], np.float32)
    cols = []
    for l in range(2):
        q1 = np.tile(qs[l], 2)
        q2 = np.tile(np.roll(qs[l], 32), 2)
        k1 = np.tile(ks[l], 2)
        k2 = np.tile(np.roll(ks[l], 32), 2)
        cols += [q1, q2, k1, k2]
    w["l0_qksc"] = np.ascontiguousarray(np.stack(cols, axis=1))
    lng = np.asarray(inp["L0_ffn_ln_g"], np.float32)
    lnb = np.asarray(inp["L0_ffn_ln_b"], np.float32)
    cols = []
    for l in range(2):
        cols.append(lng[l].reshape(8, 128).T)
        cols.append(lnb[l].reshape(8, 128).T)
    w["l0_lnc"] = np.ascontiguousarray(np.concatenate(cols, axis=1))
    w["l1_wqkv"] = np.ascontiguousarray(np.asarray(inp["L1_attn_Wqkv"], np.float32))
    w["l1_wout"] = np.ascontiguousarray(np.asarray(inp["L1_attn_Wout"], np.float32))
    w["l1_bqc"] = np.ascontiguousarray(np.asarray(inp["L1_attn_bqkv"], np.float32).reshape(24, 128).T)
    w["l1_boc"] = np.ascontiguousarray(np.asarray(inp["L1_attn_bout"], np.float32).reshape(8, 128).T)
    w["gate_wT"] = np.ascontiguousarray(np.asarray(inp["L1_ffn_gate_w"], np.float32).T)
    ew = np.asarray(inp["L1_ffn_expert_w"], np.float32).transpose(0, 2, 1)
    for nm in ("wqbig", "wkbig", "wvbig"):
        w[nm] = _timg(w[nm])
    w["l0_wqkv"] = np.stack([_timg(w["l0_wqkv"][l]) for l in range(2)])
    w["l0_wo"] = np.stack([_timg(w["l0_wo"][l]) for l in range(2)])
    w["l0_w1re"] = np.stack([_timg(w["l0_w1re"][l]) for l in range(2)])
    w["l0_w2p"] = np.stack([_timg(w["l0_w2p"][l]) for l in range(2)])
    w["l1_wqkv"] = _timg(w["l1_wqkv"])
    w["l1_wout"] = _timg(w["l1_wout"])
    w["ew_T"] = np.stack([_timg(ew[e]) for e in range(8)])
    return w



class KB:
    EKEYS = ("T", "V", "S", "P", "Y")

    def __init__(self, nc):
        self.nc = nc
        self.q = {k: [] for k in self.EKEYS}
        self.c = dict(T=0, V=0, S=0, P=0, dW0=0, dW1=0, dY0=0, dY1=0, dZ0=0, dZ1=0, dT0=0, dT1=0, dP0=0, dP1=0)
        self.sem = {}

    def dma_begin(self, ek, dk):
        self.wait(ek, dk, self.c[dk])

    def raw(self, ek, fn):
        self.q[ek].append(fn)

    def wait(self, ek, sk, val):
        if val <= 0:
            return
        self.q[ek].append(lambda e, sk=sk, v=val: e.wait_ge(self.sem[sk], v))

    def op(self, ek, fn):
        self.c[ek] += 1
        def run(e, fn=fn, ek=ek):
            fn(e).then_inc(self.sem[ek], 1)
        self.q[ek].append(run)
        return self.c[ek]

    def sop(self, ek, fn):
        self.q[ek].append(lambda e, fn=fn: fn(e))
        return self.c[ek]

    def dma(self, ek, dk, fn):
        self.c[dk] += 16
        def run(e, fn=fn, dk=dk):
            fn(e).then_inc(self.sem[dk], 16)
        self.q[ek].append(run)
        return self.c[dk]

    def snap(self):
        return dict(self.c)

    def barrier_from(self, ek, snap, keys=None):
        for k in (keys or ("T", "V", "S", "P", "dW0", "dW1", "dY0", "dY1", "dZ0", "dZ1", "dT0", "dT1", "dP0", "dP1")):
            self.wait(ek, k, snap[k])

    def full_barrier(self, keys=("T", "V", "S", "P", "Y")):
        s = self.snap()
        for ek in keys:
            self.barrier_from(ek, s)
        return s



def softmax_row(kb, E, psrow, width, pout, exp_scale_key=None, skip_max=False):
    nmx = E["NMX"]
    nms = E["NMS"]
    sms = E["SMS"]
    rcp = E["RCP"]
    if skip_max:
        kb.wait("S", "V", kb.c["V"])
        nS = kb.op("S", lambda s: s.activation(
            out=pout, in_=psrow, func=AF.Exp, accum_out=sms[:, 0:1]))
        kb.wait("V", "S", nS)
        nV = kb.op("V", lambda v: v.reciprocal(out=rcp[:, :], in_=sms[:, :]))
        kb.wait("V", "V", nV)
        nV = kb.op("V", lambda v: v.tensor_scalar(out=pout, in0=pout, scalar1=rcp[:, 0:1],
                                                  scalar2=None, op0=ALU.mult))
        return nV
    nV = kb.op("V", lambda v: v.reduce_max(out=nmx[:, 0:1], in_=psrow, axis=AX.X, negate=True))
    kb.wait("S", "V", nV)
    if exp_scale_key is not None:
        scale = E["consts_f"][exp_scale_key]
        nS = kb.op("S", lambda s, sc=scale: s.mul(out=nms[:, 0:1], in_=nmx[:, 0:1], mul=sc))
        kb.wait("S", "S", nS)
        bias_ap = nms[:, 0:1]
    else:
        scale = 1.0
        bias_ap = nmx[:, 0:1]
    nS = kb.op("S", lambda s, b=bias_ap, sc=scale: s.activation(
        out=pout, in_=psrow, func=AF.Exp, bias=b, scale=sc, accum_out=sms[:, 0:1]))
    kb.wait("V", "S", nS)
    nV = kb.op("V", lambda v: v.reciprocal(out=rcp[:, 0:1], in_=sms[:, 0:1]))
    kb.wait("V", "V", nV)
    nV = kb.op("V", lambda v: v.tensor_scalar(out=pout, in0=pout, scalar1=rcp[:, 0:1],
                                              scalar2=None, op0=ALU.mult))
    return nV


def ln_stats_apply(kb, E, xslices, width, pp=0, gb_cols=None, bf=False, defer=False):
    SCB = E["SCB"]
    SQB = E["SQB"]
    SR = E["SR"]
    PS_S = E["PS_S"]
    ONEC = E["ONEC"]
    ONE1F = E["ONE1F"]
    bA = PS_S[:, 2 * pp * width:(2 * pp + 1) * width]
    bB = PS_S[:, (2 * pp + 1) * width:(2 * pp + 2) * width]
    st = E.setdefault("_lnst", {"apply_v": [0, 0], "bcast_t": [0, 0]})
    tdone = []
    for ft in range(8):
        h = (ft % 2) * width
        if ft >= 2:
            kb.wait("S", "T", tdone[ft - 2])
        if not bf:
            kb.op("S", lambda s, ft=ft, h=h: s.copy(out=SCB[:, h:h + width], in_=xslices[ft]))
            vsrc = SCB[:, h:h + width]
        else:
            vsrc = xslices[ft]
        nS2 = kb.op("S", lambda s, ft=ft, h=h: s.activation(
            out=SQB[:, h:h + width], in_=xslices[ft], func=AF.Square))
        kb.wait("T", "S", nS2)
        if ft == 0:
            kb.wait("T", "V", st["apply_v"][pp])
        kb.op("T", lambda t, ft=ft, vsrc=vsrc: t.matmul(
            bA[0:1, 0:width], ONEC[:, :], vsrc, start=(ft == 0), stop=(ft == 7)))
        n2 = kb.op("T", lambda t, ft=ft, h=h: t.matmul(
            bB[0:1, 0:width], ONEC[:, :], SQB[:, h:h + width], start=(ft == 0), stop=(ft == 7)))
        tdone.append(n2)
    mu = SR[0:1, pp * 1024:pp * 1024 + width]
    tmp = SR[0:1, pp * 1024 + 512:pp * 1024 + 512 + width]
    kb.wait("S", "T", tdone[-1])
    kb.wait("S", "T", st["bcast_t"][pp])
    s_mu = kb.op("S", lambda s: s.activation(out=mu, in_=bA[0:1, 0:width],
                                             func=AF.Copy, scale=E["consts_f"]["inv1024"]))
    kb.wait("S", "S", s_mu)
    s_m2 = kb.op("S", lambda s: s.activation(out=tmp, in_=mu, func=AF.Square))
    kb.wait("V", "S", s_m2)
    v_var = kb.op("V", lambda v: v.scalar_tensor_tensor(
        out=tmp, in0=bB[0:1, 0:width], scalar=E["consts_f"]["inv1024"],
        in1=tmp, op0=ALU.mult, op1=ALU.subtract))
    kb.wait("V", "V", v_var)
    v_ve = kb.op("V", lambda v: v.tensor_scalar(out=tmp, in0=tmp, scalar1=1e-5,
                                                scalar2=None, op0=ALU.add))
    kb.wait("V", "V", v_ve)
    v_rc = kb.op("V", lambda v: v.reciprocal(out=tmp, in_=tmp))
    kb.wait("S", "V", v_rc)
    s_rs = kb.op("S", lambda s: s.activation(out=tmp, in_=tmp, func=AF.Sqrt))

    def finish():
        kb.wait("T", "S", s_rs)
        kb.op("T", lambda t: t.matmul(bA, ONE1F[:, :], mu, start=True, stop=True))
        brs = kb.op("T", lambda t: t.matmul(bB, ONE1F[:, :], tmp, start=True, stop=True))
        st["bcast_t"][pp] = brs
        kb.wait("V", "T", brs)
        nsub = 0
        for ft in range(8):
            nsub = kb.op("V", lambda v, ft=ft: v.tensor_tensor(out=xslices[ft], in0=xslices[ft],
                                                               in1=bA, op=ALU.subtract))
        kb.wait("V", "V", nsub)
        nV = 0
        for ft in range(8):
            nV = kb.op("V", lambda v, ft=ft: v.tensor_tensor(out=xslices[ft], in0=xslices[ft],
                                                             in1=bB, op=ALU.mult))
        if gb_cols is not None:
            kb.wait("V", "V", nV)
            for ft in range(8):
                gcols, bcols = gb_cols
                nV = kb.op("V", lambda v, ft=ft, gcols=gcols, bcols=bcols: v.tensor_scalar(
                    out=xslices[ft], in0=xslices[ft], scalar1=gcols[ft], scalar2=bcols[ft],
                    op0=ALU.mult, op1=ALU.add))
        st["apply_v"][pp] = nV
        return nV

    if defer:
        return finish
    return finish()


def _load_xt(kb, E, b, also_abf=True, also_x=True):
    X, ABF, XR, PS_S = E["X"], E["ABF"], E["XR"], E["PS_S"]
    x_dram = E["x"]
    vdone = [0] * QT
    sdone = [0] * QT
    for tt in range(QT):
        dk = "dY%d" % (tt % 2)
        if tt >= 2:
            kb.wait("Y", "V", vdone[tt - 2])
            kb.wait("Y", "S", sdone[tt - 2])
        kb.dma_begin("Y", dk)
        n = kb.dma("Y", dk, lambda sy, tt=tt: sy.dma_start(
            XR[tt % 2][:, :], x_dram[b, tt * 128:(tt + 1) * 128, :]))
        kb.wait("T", dk, n)
        if tt >= 2:
            kb.wait("T", "V", vdone[tt - 2])
            kb.wait("T", "S", sdone[tt - 2])
        ttr = 0
        for ft in range(8):
            ttr = kb.op("T", lambda t, tt=tt, ft=ft: t.transpose(
                PS_S[:, ft * 128 + (tt % 2) * 1024: ft * 128 + (tt % 2) * 1024 + 128],
                XR[tt % 2][:, ft * 128:(ft + 1) * 128], E["IDF"][:, :]))
        kb.wait("V", "T", ttr)
        kb.wait("S", "T", ttr)
        for ft in range(8):
            src = PS_S[:, ft * 128 + (tt % 2) * 1024: ft * 128 + (tt % 2) * 1024 + 128]
            if ft < 4:
                if also_x:
                    vdone[tt] = kb.op("V", lambda v, ft=ft, tt=tt, src=src: v.tensor_copy(
                        out=X[ft][:, tt * 128:(tt + 1) * 128], in_=src))
                if also_abf:
                    vdone[tt] = kb.op("V", lambda v, ft=ft, tt=tt, src=src: v.tensor_copy(
                        out=ABF[ft][:, tt * 128:(tt + 1) * 128], in_=src))
            else:
                if also_x:
                    sdone[tt] = kb.op("S", lambda s, ft=ft, tt=tt, src=src: s.copy(
                        out=X[ft][:, tt * 128:(tt + 1) * 128], in_=src))
                if also_abf:
                    sdone[tt] = kb.op("S", lambda s, ft=ft, tt=tt, src=src: s.copy(
                        out=ABF[ft][:, tt * 128:(tt + 1) * 128], in_=src))
    kb.full_barrier()


def _wtile_ap(base_ap, j0, rows, ntile=1):
    return bass.AP(base_ap.tensor, base_ap.offset + j0 * 128 * rows,
                   [[rows, 128], [128 * rows, ntile], [1, rows]])


def _stream_w_cols(kb, E, base_ap, col0, rows, ctot, half, dest=None, ncols=128):
    re = _wtile_ap(base_ap, col0 // 128, rows)
    if dest is None:
        dest = E["WB"][:, half * 1024:half * 1024 + rows]
    return kb.dma("Y", "dZ%d" % half, lambda sy, re=re, dest=dest: sy.dma_start(dest, re))


def _wstream_begin(kb, half):
    kb.dma_begin("Y", "dZ%d" % half)



def _rope_pair(kb, E, ps, p, tblC, tblS, out_ap, swap_lhsT):
    SCB = E["SCB"]
    PBF = E["PBF"]
    PS_M = E["PS_M"]
    o = p * CH
    nS = kb.op("S", lambda s: s.copy(out=SCB[:, o:o + CH], in_=ps))
    kb.wait("T", "S", nS)
    nJ = kb.op("T", lambda t: t.matmul(PS_M[:, p * CH:(p + 1) * CH], swap_lhsT, SCB[:, o:o + CH],
                                       start=True, stop=True))
    kb.wait("V", "S", nS)
    nV = kb.op("V", lambda v: v.tensor_tensor(out=PBF[:, o:o + CH], in0=ps, in1=tblC,
                                              op=ALU.mult))
    kb.wait("V", "T", nJ)
    kb.wait("V", "V", nV)
    nV = kb.op("V", lambda v: v.tensor_tensor(out=SCB[:, o:o + CH], in0=PS_M[:, p * CH:(p + 1) * CH],
                                              in1=tblS, op=ALU.mult))
    kb.wait("V", "V", nV)
    nV = kb.op("V", lambda v: v.tensor_tensor(out=out_ap, in0=SCB[:, o:o + CH],
                                              in1=PBF[:, o:o + CH], op=ALU.add))
    return nV, nJ, nS


def ph_l0a(kb, E, b):
    X, ABF, KT, VT, QJ, WB = E["X"], E["ABF"], E["KT"], E["VT"], E["QJ"], E["WB"]
    PBF, PT = E["PBF"], E["PT"]
    TB0, TB1 = E["TB0"], E["TB1"]
    PS_S, PS_T, PS_M = E["PS_S"], E["PS_T"], E["PS_M"]

    kb.full_barrier()
    if b == 0:
        fa = E.get("pc_fenceA")
        if fa:
            kb.wait("Y", "dP0", fa[0])
            kb.wait("Y", "dP1", fa[1])
    _load_xt(kb, E, b)
    if L0A_STAGES < 2:
        return

    rope_v = [0, 0]
    rope_s = [0, 0]
    wb_mm = [0, 0]
    for ft in range(8):
        half = ft % 2
        dk = "dZ%d" % half
        kb.wait("Y", "T", wb_mm[half])
        _wstream_begin(kb, half)
        nW = _stream_w_cols(kb, E, E["wkbig"], ft * 128, D, D, half)
        kb.wait("T", dk, nW)
        for c in range(NCH):
            p = c % 2
            tk = "dT%d" % p
            kb.wait("Y", "V", rope_v[p])
            kb.dma_begin("Y", tk)
            kb.dma("Y", tk, lambda sy, ft=ft, c=c, p=p: sy.dma_start(
                TB0[:, p * CH:(p + 1) * CH],
                E["ropeCb"][ft * 128:(ft + 1) * 128, c * CH:(c + 1) * CH]))
            nTb = kb.dma("Y", tk, lambda sy, ft=ft, c=c, p=p: sy.dma_start(
                TB1[:, p * CH:(p + 1) * CH],
                E["ropeSb"][ft * 128:(ft + 1) * 128, c * CH:(c + 1) * CH]))
            ps = PS_S[:, p * CH:(p + 1) * CH]
            kb.wait("T", "V", rope_v[p])
            kb.wait("T", "S", rope_s[p])
            nT = 0
            for kt in range(8):
                nT = kb.op("T", lambda t, kt=kt, c=c, half=half, ps=ps: t.matmul(
                    ps, WB[:, half * 1024 + kt * 128: half * 1024 + kt * 128 + 128],
                    ABF[kt][:, c * CH:(c + 1) * CH], start=(kt == 0), stop=(kt == 7)))
            wb_mm[half] = nT
            kb.wait("S", "T", nT)
            kb.wait("S", "V", rope_v[p])
            kb.wait("V", tk, nTb)
            nV, nJ, nSc = _rope_pair(kb, E, ps, p, TB0[:, p * CH:(p + 1) * CH],
                                     TB1[:, p * CH:(p + 1) * CH],
                                     KT[ft][:, c * CH:(c + 1) * CH], E["JTBb"][:, :])
            rope_v[p] = nV
            rope_s[p] = nSc
    kb.full_barrier()

    wb_mm = [0, 0]
    ps_ev = [0, 0]
    for ft in range(8):
        half = ft % 2
        dk = "dZ%d" % half
        kb.wait("Y", "T", wb_mm[half])
        _wstream_begin(kb, half)
        nW = _stream_w_cols(kb, E, E["wvbig"], ft * 128, D, D, half)
        kb.wait("T", dk, nW)
        for tt4 in range(4):
            pb = tt4 % 2
            ps = PS_S[:, pb * CH:pb * CH + CH]
            kb.wait("T", "V", ps_ev[pb])
            nT = 0
            for sub in range(4):
                tt = tt4 * 4 + sub
                for kt in range(8):
                    nT = kb.op("T", lambda t, kt=kt, tt=tt, sub=sub, half=half, ps=ps: t.matmul(
                        ps[:, sub * 128:(sub + 1) * 128],
                        ABF[kt][:, tt * 128:(tt + 1) * 128],
                        WB[:, half * 1024 + kt * 128: half * 1024 + kt * 128 + 128],
                        start=(kt == 0), stop=(kt == 7)))
            wb_mm[half] = nT
            kb.wait("V", "T", nT)
            nE = 0
            for sub in range(4):
                tt = tt4 * 4 + sub
                nE = kb.op("V", lambda v, tt=tt, sub=sub, ft=ft, ps=ps: v.tensor_copy(
                    out=VT[tt][:, ft * 128:(ft + 1) * 128], in_=ps[:, sub * 128:(sub + 1) * 128]))
            ps_ev[pb] = nE
    kb.full_barrier()
    if L0A_STAGES < 4:
        return

    for g in range(4):
        jbmax = 4 * (g + 1)
        rope_v = [0, 0]
        rope_s = [0, 0]
        for ft in range(8):
            half = ft % 2
            p = half
            dk = "dZ%d" % half
            kb.wait("Y", "V", rope_v[p])
            _wstream_begin(kb, half)
            nW = _stream_w_cols(kb, E, E["wqbig"], ft * 128, D, D, half)
            nC = kb.dma("Y", dk, lambda sy, ft=ft, g=g, p=p: sy.dma_start(
                TB0[:, p * CH:(p + 1) * CH],
                E["ropeCb"][ft * 128:(ft + 1) * 128, g * CH:(g + 1) * CH]))
            nS_ = kb.dma("Y", dk, lambda sy, ft=ft, g=g, p=p: sy.dma_start(
                TB1[:, p * CH:(p + 1) * CH],
                E["ropeSb"][ft * 128:(ft + 1) * 128, g * CH:(g + 1) * CH]))
            kb.wait("T", dk, nS_)
            kb.wait("T", "V", rope_v[p])
            kb.wait("T", "S", rope_s[p])
            ps = PS_S[:, p * CH:(p + 1) * CH]
            nT = 0
            for kt in range(8):
                nT = kb.op("T", lambda t, kt=kt, g=g, half=half, ps=ps: t.matmul(
                    ps, WB[:, half * 1024 + kt * 128: half * 1024 + kt * 128 + 128],
                    ABF[kt][:, g * CH:(g + 1) * CH], start=(kt == 0), stop=(kt == 7)))
            kb.wait("S", "T", nT)
            kb.wait("S", "V", rope_v[p])
            nV, nJ, nSc = _rope_pair(kb, E, ps, p, TB0[:, p * CH:(p + 1) * CH],
                                     TB1[:, p * CH:(p + 1) * CH], QJ[ft], E["JTBb"][:, :])
            rope_v[p] = nV
            rope_s[p] = nSc
        kb.full_barrier()
        if L0A_S4 < 2:
            continue

        for iq in range(4 * g, 4 * g + 4):
            jext = (iq + 1) * 128
            nchq = (jext + CH - 1) // CH
            kb.wait("T", "S", kb.c["S"])
            kb.wait("T", "V", kb.c["V"])
            nT = 0
            chunks = []
            c0 = 0
            while c0 < jext - 128:
                w_ = min(CH, jext - 128 - c0)
                chunks.append((c0, w_, False))
                c0 += w_
            chunks.append((jext - 128, 128, True))
            for (c0, wdt, diag) in chunks:
                for dt in range(8):
                    nT = kb.op("T", lambda t, c0=c0, dt=dt, iq=iq, g=g, wdt=wdt, diag=diag: t.matmul(
                        PS_S[:, c0:c0 + wdt],
                        QJ[dt][:, (iq - 4 * g) * 128:(iq - 4 * g) * 128 + 128],
                        KT[dt][:, c0 // 128 * 128:c0 + wdt] if False else KT[dt][:, c0:c0 + wdt],
                        start=(dt == 0), stop=(dt == 7 and not diag)))
                if diag:
                    nT = kb.op("T", lambda t, c0=c0: t.matmul(
                        PS_S[:, c0:c0 + 128], E["M4B"][:, :], E["IDB"][:, :],
                        start=False, stop=True))
            kb.wait("V", "T", nT)
            if L0A_S4 < 3:
                continue
            nV = softmax_row(kb, E, PS_S[:, 0:jext], jext, PBF[:, 0:jext], exp_scale_key="inv32")
            if L0A_S4 < 4:
                continue
            kb.wait("T", "V", nV)
            kb.wait("T", "S", kb.c["S"])
            scopy = [0, 0]
            ngrp = (iq + 1 + 3) // 4
            for gb in range(ngrp):
                bank = gb % 2
                j0 = gb * 4
                j1 = min(iq + 1, j0 + 4)
                if gb >= 2:
                    kb.wait("T", "S", scopy[bank])
                nTt = 0
                for jb in range(j0, j1):
                    nTt = kb.op("T", lambda t, jb=jb, bank=bank, j0=j0: t.transpose(
                        PS_T[:, bank * 1024 + (jb - j0) * 128: bank * 1024 + (jb - j0) * 128 + 128],
                        PBF[:, jb * 128:(jb + 1) * 128], E["IDB"][:, :]))
                kb.wait("S", "T", nTt)
                scopy[bank] = kb.op("S", lambda s, j0=j0, j1=j1, bank=bank: s.copy(
                    out=PT[:, j0 * 128:j1 * 128],
                    in_=PS_T[:, bank * 1024: bank * 1024 + (j1 - j0) * 128]))
            if L0A_S4 < 5:
                continue
            kb.wait("T", "S", kb.c["S"])
            aav = []
            for ft in range(8):
                ps = PS_S[:, ft * 128:(ft + 1) * 128]
                nT = 0
                for jb in range(iq + 1):
                    nT = kb.op("T", lambda t, jb=jb, ft=ft, iq=iq, ps=ps: t.matmul(
                        ps, VT[jb][:, ft * 128:(ft + 1) * 128], PT[:, jb * 128:(jb + 1) * 128],
                        start=(jb == 0), stop=(jb == iq)))
                aav.append(nT)
            kb.wait("V", "T", aav[-1])
            for ft in range(8):
                kb.op("V", lambda v, ft=ft, iq=iq: v.tensor_tensor(
                    out=X[ft][:, iq * 128:(iq + 1) * 128], in0=X[ft][:, iq * 128:(iq + 1) * 128],
                    in1=PS_S[:, ft * 128:(ft + 1) * 128], op=ALU.add))
        kb.full_barrier()
        if L0A_S4 < 6:
            continue
        xs = [X[ft][:, g * CH:(g + 1) * CH] for ft in range(8)]
        ln_stats_apply(kb, E, xs, CH, pp=g % 2)
        kb.full_barrier()


def ph_store(kb, E, b):
    X = E["X"]
    PS_S = E["PS_S"]
    XR = E["XR"]
    y = E["y"]
    kb.full_barrier()
    for tt in range(QT):
        nT = 0
        buf = tt % 2
        dk = "dY%d" % buf
        if tt >= 2:
            kb.wait("T", dk, kb.c[dk])
        for ft in range(8):
            nT = kb.op("T", lambda t, tt=tt, ft=ft, buf=buf: t.transpose(
                PS_S[:, ft * 128 + buf * 1024: ft * 128 + buf * 1024 + 128],
                X[ft][:, tt * 128:(tt + 1) * 128], E["IDF"][:, :]))
        kb.wait("V", "T", nT)
        kb.wait("S", "T", nT)
        nV = kb.op("V", lambda v, buf=buf: v.tensor_copy(
            out=XR[buf][:, 0:CH], in_=PS_S[:, buf * 1024:buf * 1024 + CH]))
        nS = kb.op("S", lambda s, buf=buf: s.copy(
            out=XR[buf][:, CH:1024], in_=PS_S[:, buf * 1024 + CH:buf * 1024 + 1024]))
        kb.wait("Y", "V", nV)
        kb.wait("Y", "S", nS)
        kb.dma_begin("Y", dk)
        kb.dma("Y", dk, lambda sy, tt=tt, buf=buf: sy.dma_start(
            y[b, tt * 128:(tt + 1) * 128, :], XR[buf][:, :]))
    kb.full_barrier()


def ph_load(kb, E, b):
    kb.full_barrier()
    _load_xt(kb, E, b)



PH_FUNCS = {}

def build_core_program(phases):
    nc = bass.Bass("TRN2", target_bir_lowering=False)
    C = _consts()

    dr = {}
    dr["x"] = nc.dram_tensor("x", [2, T, D], F32, kind="ExternalInput").ap()
    dr["y"] = nc.dram_tensor("y", [2, T, D], F32, kind="ExternalOutput").ap()
    for nm, shp in [("wqbig", [8, 128, D]), ("wkbig", [8, 128, D]), ("wvbig", [8, 128, D]),
                    ("ropeC", [D, T]), ("ropeS", [D, T]),
                    ("ident", [128, 128]), ("JTB", [128, 128]), ("RTB", [128, 128]),
                    ("M0", [128, 128]), ("M4", [128, 128]),
                    ("HS", [128, 2]), ("HST", [2, 128]), ("SELS", [8, 1024]),
                    ("xposQC", [128, T]), ("xposQS", [128, T]),
                    ("xposKC", [128, T]), ("xposKS", [128, T]),
                    ("l0_wqkv", [2, 12, 128, D]), ("l0_wo", [2, 8, 128, 512]),
                    ("l0_w1re", [2, 44, 128, D]), ("l0_w2p", [2, 8, 128, HIDP]),
                    ("l0_b1c", [128, 4 * NHT]), ("l0_b2c", [128, 16]),
                    ("l0_qksc", [128, 8]), ("l0_lnc", [128, 32]),
                    ("l1_wqkv", [24, 128, D]), ("l1_wout", [8, 128, D]),
                    ("l1_bqc", [128, 24]), ("l1_boc", [128, 8]),
                    ("gate_wT", [D, 8]), ("ew_T", [8, 8, 128, D])]:
        dr[nm] = nc.dram_tensor(nm, shp, F32, kind="ExternalInput").ap()
    dr["xscr"] = nc.dram_tensor("xscr", [8, 128, T], F32).ap()
    MIR = [("wkbig", [8, 128, D]), ("ropeC", [D, T]), ("ropeS", [D, T]),
           ("wvbig", [8, 128, D]), ("wqbig", [8, 128, D]),
           ("l0_wqkv", [2, 12, 128, D]), ("l0_wo", [2, 8, 128, 512]),
           ("l0_w1re", [2, 44, 128, D]), ("l0_w2p", [2, 8, 128, HIDP]),
           ("l1_wqkv", [24, 128, D]), ("l1_wout", [8, 128, D]), ("ew_T", [8, 8, 128, D])]
    mir = {}
    for nm, shp in MIR:
        mir[nm] = nc.dram_tensor(nm + "_bf", shp, BF16).ap()

    ctx = ExitStack()
    sb = lambda n, s, d: ctx.enter_context(nc.sbuf_tensor(n, s, d))
    psm = lambda n, s, d: ctx.enter_context(nc.psum_tensor(n, s, d))

    E = dict(nc=nc)
    E.update(dr)

    XU = sb("XU", [128, 8 * T], F32)
    ABFU = sb("ABFU", [128, 8 * T], BF16)
    KTU = sb("KTU", [128, 8 * T], BF16)
    VTU = sb("VTU", [128, 16 * D], BF16)
    TB0 = sb("TB0", [128, T], BF16)
    TB1 = sb("TB1", [128, T], BF16)
    WBt = sb("WBt", [128, 2048], BF16)
    QJt = sb("QJt", [128, 4096], BF16)
    PBFt = sb("PBFt", [128, T], BF16)
    PTt = sb("PTt", [128, T], BF16)
    SCBt = sb("SCBt", [128, 1024], BF16)
    SQBt = sb("SQBt", [128, 1024], BF16)
    FSCR = sb("FSCR", [128, T], F32)
    OEt = FSCR.bitcast(BF16)

    XUa, ABFa, KTa, VTa = XU.ap(), ABFU.ap(), KTU.ap(), VTU.ap()
    E["X"] = [XUa[:, i * T:(i + 1) * T] for i in range(8)]
    E["ABF"] = [ABFa[:, i * T:(i + 1) * T] for i in range(8)]
    E["KT"] = [KTa[:, i * T:(i + 1) * T] for i in range(8)]
    E["VT"] = [VTa[:, i * D:(i + 1) * D] for i in range(QT)]
    E["KTU"] = KTa
    E["ABFU"] = ABFa
    E["VTU"] = VTa
    E["TB0"], E["TB1"] = TB0.ap(), TB1.ap()
    E["TSF"] = SQBt.bitcast(F32).ap()
    E["WB"] = WBt.ap()
    qa = QJt.ap()
    E["QJ"] = [qa[:, i * CH:(i + 1) * CH] for i in range(8)]
    E["QJU"] = qa
    E["PBF"] = PBFt.ap()
    E["PT"] = PTt.ap()
    oa = OEt.ap()
    E["OE"] = [oa[:, i * CH:(i + 1) * CH] for i in range(8)]
    E["OEU"] = oa
    E["SCB"] = SCBt.ap()
    E["SQB"] = SQBt.ap()
    fa = FSCR.ap()
    E["XR"] = [fa[:, 0:1024], fa[:, 1024:2048]]
    E["SR"] = fa

    for nm, shp, d in [("IDF", [128, 128], F32), ("IDB", [128, 128], BF16),
                       ("JTBb", [128, 128], BF16), ("RTBb", [128, 128], BF16),
                       ("M0c", [128, 128], F32), ("M4c", [128, 128], F32),
                       ("HSb", [128, 2], BF16), ("HSTb", [2, 128], BF16),
                       ("SELSb", [8, 1024], BF16),
                       ("M0B", [128, 128], BF16), ("M4B", [128, 128], BF16),
                       ("BVB", [128, 8], BF16), ("BOC2", [128, 8], F32),
                       ("ONEC", [128, 1], BF16), ("ONE1F", [1, 128], F32),
                       ("ONEB", [1, 128], BF16),
                       ("QKSC", [128, 8], F32), ("LNC", [128, 32], F32),
                       ("B1C", [128, 4 * NHT], F32), ("B2C", [128, 16], F32),
                       ("BQC", [128, 24], F32), ("BOC", [128, 8], F32),
                       ("NMX", [128, 2], F32), ("NMS", [128, 2], F32),
                       ("SMS", [128, 2], F32), ("RCP", [128, 2], F32),
                       ("CP", [128, 4], F32)]:
        E[nm] = sb(nm, shp, d).ap()
    E["M0"], E["M4"] = E["M0c"], E["M4c"]

    E["PS_S"] = psm("PS_S", [128, 2048], F32).ap()
    _pst = psm("PS_T", [128, 2048], BF16)
    E["PS_T"] = _pst.ap()
    E["PS_TF"] = _pst.bitcast(F32).ap()
    E["PS_M"] = psm("PS_M", [128, 1024], F32).ap()

    E["consts_f"] = {"inv32": 1.0 / 32.0, "inv1024": 1.0 / 1024.0}
    nc.const_aps.register(F32, 1.0 / 32.0, E["CP"][:, 0:1])
    nc.const_aps.register(F32, 1.0 / 1024.0, E["CP"][:, 1:2])

    sems = {}
    for k in ("T", "V", "S", "P", "dW0", "dW1", "dY0", "dY1", "dZ0", "dZ1", "dT0", "dT1", "dP0", "dP1"):
        sems[k] = ctx.enter_context(nc.semaphore(f"sem{k}"))
    blk = ctx.enter_context(nc.Block())

    kb = KB(nc)
    kb.sem = sems

    nconst = 0
    for dst, src in [(E["IDF"], dr["ident"]), (E["M0c"], dr["M0"]), (E["M4c"], dr["M4"]),
                     (E["QKSC"], dr["l0_qksc"]), (E["LNC"], dr["l0_lnc"]),
                     (E["B1C"], dr["l0_b1c"]), (E["B2C"], dr["l0_b2c"]),
                     (E["BQC"], dr["l1_bqc"]), (E["BOC"], dr["l1_boc"])]:
        nconst = kb.dma("Y", "dY0", lambda sy, dst=dst, src=src: sy.dma_start(dst[:, :], src))
    nconst2 = 0
    for dst, src in [(E["IDB"], dr["ident"]), (E["JTBb"], dr["JTB"]), (E["RTBb"], dr["RTB"]),
                     (E["HSb"], dr["HS"]), (E["HSTb"], dr["HST"]), (E["SELSb"], dr["SELS"]),
                     (E["M0B"], dr["M0"]), (E["M4B"], dr["M4"])]:
        nconst2 = kb.dma("P", "dW0", lambda g, dst=dst, src=src: g.dma_start(dst[:, :], src))
    kb.sop("P", lambda g: g.memset(E["ONEC"][:, :], 1.0))
    kb.sop("P", lambda g: g.memset(E["ONEB"][:, :], 1.0))
    kb.sop("P", lambda g: g.memset(E["ONE1F"][:, :], 1.0))
    kb.sop("P", lambda g: g.memset(E["CP"][:, 0:1], 1.0 / 32.0))
    nP = kb.op("P", lambda g: g.memset(E["CP"][:, 1:2], 1.0 / 1024.0))
    for ek in ("T", "V", "S"):
        kb.wait(ek, "dY0", nconst)
        kb.wait(ek, "dW0", nconst2)
        kb.wait(ek, "P", nP)
    kb.wait("Y", "dY0", nconst)
    kb.wait("P", "dW0", nconst2)


    KTUa = E["KTU"]
    pc_half = [0, 0]
    hcur = 0
    pc_fence = {}
    for nm, shp in MIR:
        nel = 1
        for s in shp:
            nel *= s
        ctot = nel // 128
        c0 = 0
        while c0 < ctot:
            w = min(8192, ctot - c0)
            h = hcur
            hcur ^= 1
            kb.wait("P", "dP%d" % h, pc_half[h])
            kb.dma_begin("P", "dW%d" % h)
            fin = bass.AP(dr[nm].tensor, c0, [[ctot, 128], [1, w]])
            nin = kb.dma("P", "dW%d" % h, lambda g, fin=fin, h=h, w=w: g.dma_start(
                KTUa[:, h * 8192:h * 8192 + w], fin))
            kb.wait("P", "dW%d" % h, nin)
            fout = bass.AP(mir[nm].tensor, c0, [[ctot, 128], [1, w]])
            pc_half[h] = kb.dma("P", "dP%d" % h, lambda g, fout=fout, h=h, w=w: g.dma_start(
                fout, KTUa[:, h * 8192:h * 8192 + w]))
            c0 += w
        pc_fence[nm] = (kb.c["dP0"], kb.c["dP1"])
    E["pc_fenceA"] = pc_fence["wqbig"]
    E["pc_fenceB"] = pc_fence[MIR[-1][0]]
    for nm, shp in MIR:
        E[nm if nm not in ("ropeC", "ropeS") else nm + "b"] = mir[nm]

    for b in range(2):
        for pi, ph in enumerate(phases):
            if pi == 0 and ph != "l0a":
                ph_load(kb, E, b)
            PH_FUNCS[ph](kb, E, b)
        ph_store(kb, E, b)

    kb.wait("Y", "dY0", kb.c["dY0"])
    kb.wait("Y", "dY1", kb.c["dY1"])

    def mk(ek):
        def sec(eng):
            for fn in kb.q[ek]:
                fn(eng)
        return sec
    blk.tensor(mk("T"))
    blk.vector(mk("V"))
    blk.scalar(mk("S"))
    blk.gpsimd(mk("P"))
    blk.sync(mk("Y"))

    ctx.close()
    return nc


PH_FUNCS["l0a"] = ph_l0a
PH_FUNCS["nop"] = lambda kb, E, b: None


L0A_STAGES = 99
L0A_S4 = 99





def _cast_shift_abf(kb, E, shift):
    X, ABF = E["X"], E["ABF"]
    for ft in range(8):
        ek = "V" if ft < 4 else "S"
        if shift and ft >= 4:
            nz = kb.op("V", lambda v, ft=ft: v.memset(ABF[ft][:, 0:2], 0.0))
            if ek == "V":
                kb.wait("V", "V", nz)
                kb.op("V", lambda v, ft=ft: v.tensor_copy(out=ABF[ft][:, 1:T],
                                                          in_=X[ft][:, 0:T - 1]))
            else:
                kb.wait("S", "V", nz)
                kb.op("S", lambda s, ft=ft: s.copy(out=ABF[ft][:, 1:T], in_=X[ft][:, 0:T - 1]))
        else:
            if ek == "V":
                kb.op("V", lambda v, ft=ft: v.tensor_copy(out=ABF[ft][:, :], in_=X[ft][:, :]))
            else:
                kb.op("S", lambda s, ft=ft: s.copy(out=ABF[ft][:, :], in_=X[ft][:, :]))
    kb.full_barrier()


def _fold_xpos(kb, E, cos_dram, sin_dram, scol, rcol):
    TSF, TB0, TB1 = E["TSF"], E["TB0"], E["TB1"]
    QKSC = E["QKSC"]
    for c in range(NCH):
        for (dram, tb, col) in ((cos_dram, TB0, scol), (sin_dram, TB1, rcol)):
            kb.wait("Y", "V", kb.c["V"])
            kb.dma_begin("Y", "dY0")
            n = kb.dma("Y", "dY0", lambda sy, dram=dram, c=c: sy.dma_start(
                TSF[:, :], dram[:, c * CH:(c + 1) * CH]))
            kb.wait("V", "dY0", n)
            kb.op("V", lambda v, tb=tb, c=c, col=col: v.tensor_scalar(
                out=tb[:, c * CH:(c + 1) * CH], in0=TSF[:, :], scalar1=QKSC[:, col:col + 1],
                scalar2=None, op0=ALU.mult))
    kb.full_barrier()


def _l0f_qk_tiles(kb, E, l, which):
    X, ABF, KT, WB = E["X"], E["ABF"], E["KT"], E["WB"]
    TB0, TB1 = E["TB0"], E["TB1"]
    SCB, SQB, SR, PBF = E["SCB"], E["SQB"], E["SR"], E["PBF"]
    PS_S, PS_M = E["PS_S"], E["PS_M"]
    wbase = E["l0_wqkv"][l]
    units = [(ft, c) for ft in range(4) for c in range(NCH)]
    wb_mm = [0, 0]
    sq_n = {}
    out_n = {}
    qn_n = {}
    t2_n = {}
    recb_n = {}
    st_w = {}

    def emit_tail(u):
        ft, c = units[u]
        p = u % 2
        kb.wait("T", "S", sq_n[u])
        if u >= 2 and (u - 2) in qn_n:
            kb.wait("T", "V", qn_n[u - 2])
        nrm = kb.op("T", lambda t, p=p: t.matmul(
            PS_M[0:2, p * CH:(p + 1) * CH], E["HSb"][:, :], SQB[:, p * CH:(p + 1) * CH],
            start=True, stop=True))
        kb.wait("S", "T", nrm)
        nsq = kb.op("S", lambda s, p=p: s.activation(
            out=SR[0:2, p * CH:(p + 1) * CH], in_=PS_M[0:2, p * CH:(p + 1) * CH],
            func=AF.Sqrt))
        kb.wait("V", "S", nsq)
        nv = kb.op("V", lambda v, p=p: v.tensor_scalar(
            out=SR[0:2, p * CH:(p + 1) * CH], in0=SR[0:2, p * CH:(p + 1) * CH],
            scalar1=1e-12, scalar2=None, op0=ALU.max))
        kb.wait("V", "V", nv)
        nv = kb.op("V", lambda v, p=p: v.reciprocal(
            out=SR[0:2, p * CH:(p + 1) * CH], in_=SR[0:2, p * CH:(p + 1) * CH]))
        kb.wait("V", "V", nv)
        nrc = kb.op("V", lambda v, p=p: v.tensor_copy(
            out=SQB[0:2, p * CH:p * CH + CH], in_=SR[0:2, p * CH:(p + 1) * CH]))
        recb_n[u] = nrc
        kb.wait("T", "V", nrc)
        nbc = kb.op("T", lambda t, p=p: t.matmul(
            PS_M[:, p * CH:(p + 1) * CH], E["HSTb"][:, :], SQB[0:2, p * CH:p * CH + CH],
            start=True, stop=True))
        kb.wait("V", "T", nbc)
        if u >= 2 and (u - 2) in t2_n:
            kb.wait("V", "V", t2_n[u - 2])
        nqn = kb.op("V", lambda v, p=p: v.tensor_tensor(
            out=PBF[:, p * CH:(p + 1) * CH], in0=SCB[:, p * CH:(p + 1) * CH],
            in1=PS_M[:, p * CH:(p + 1) * CH], op=ALU.mult))
        qn_n[u] = nqn
        kb.wait("T", "V", nqn)
        if u >= 2 and (u - 2) in t2_n:
            kb.wait("T", "V", t2_n[u - 2])
        nrot = kb.op("T", lambda t, p=p: t.matmul(
            PS_S[:, (2 + p) * CH:(3 + p) * CH], E["RTBb"][:, :], PBF[:, p * CH:(p + 1) * CH],
            start=True, stop=True))
        kb.wait("V", "V", nqn)
        nt1 = kb.op("V", lambda v, c=c, p=p: v.tensor_tensor(
            out=PBF[:, 1024 + p * CH:1024 + (p + 1) * CH], in0=PBF[:, p * CH:(p + 1) * CH],
            in1=TB0[:, c * CH:(c + 1) * CH], op=ALU.mult))
        kb.wait("V", "T", nrot)
        kb.wait("V", "V", nt1)
        nt2 = kb.op("V", lambda v, c=c, p=p: v.tensor_tensor(
            out=SQB[:, p * CH:(p + 1) * CH], in0=PS_S[:, (2 + p) * CH:(3 + p) * CH],
            in1=TB1[:, c * CH:(c + 1) * CH], op=ALU.mult))
        t2_n[u] = nt2
        kb.wait("V", "V", nt2)
        out_n[u] = kb.op("V", lambda v, ft=ft, c=c, which=which, p=p: v.tensor_tensor(
            out=KT[which * 4 + ft][:, c * CH:(c + 1) * CH],
            in0=SQB[:, p * CH:(p + 1) * CH],
            in1=PBF[:, 1024 + p * CH:1024 + (p + 1) * CH], op=ALU.add))

    for u, (ft, c) in enumerate(units):
        p = u % 2
        if c == 0:
            half = ft % 2
            kb.wait("Y", "T", wb_mm[half])
            _wstream_begin(kb, half)
            nW = _stream_w_cols(kb, E, wbase, which * 512 + ft * 128, D, 1536, half)
            st_w[ft] = (half, nW)
        half, nW = st_w[ft]
        if c == 0:
            kb.wait("T", "dZ%d" % half, nW)
        ps = PS_S[:, p * CH:(p + 1) * CH]
        if u >= 2:
            kb.wait("T", "S", sq_n[u - 2])
        nT = 0
        for kt in range(8):
            nT = kb.op("T", lambda t, kt=kt, c=c, half=half, ps=ps: t.matmul(
                ps, WB[:, half * 1024 + kt * 128: half * 1024 + kt * 128 + 128],
                ABF[kt][:, c * CH:(c + 1) * CH], start=(kt == 0), stop=(kt == 7)))
        wb_mm[half] = nT
        kb.wait("S", "T", nT)
        if u >= 2 and (u - 2) in qn_n:
            kb.wait("S", "V", qn_n[u - 2])
        kb.op("S", lambda s, ps=ps, p=p: s.copy(out=SCB[:, p * CH:(p + 1) * CH], in_=ps))
        if u >= 2 and (u - 2) in out_n:
            kb.wait("S", "V", out_n[u - 2])
        sq_n[u] = kb.op("S", lambda s, ps=ps, p=p: s.activation(
            out=SQB[:, p * CH:(p + 1) * CH], in_=ps, func=AF.Square))
        if u >= 1:
            emit_tail(u - 1)
    emit_tail(len(units) - 1)
    kb.full_barrier()

def ln_stats_apply_bf(kb, E, bfslices, width, gcols, bcols, pp=0, defer=False):
    return ln_stats_apply(kb, E, bfslices, width, pp=pp, gb_cols=(gcols, bcols), bf=True,
                          defer=defer)

def ph_l0fs(kb, E, b):
    kb.full_barrier()
    if b == 0:
        fb = E.get("pc_fenceB")
        if fb:
            kb.wait("Y", "dP0", fb[0])
            kb.wait("Y", "dP1", fb[1])
    n = 0
    for ft in range(8):
        n = kb.dma("Y", "dY0", lambda sy, ft=ft: sy.dma_start(
            E["xscr"][ft, :, :], E["X"][ft][:, :]))
    kb.wait("Y", "dY0", n)
    kb.full_barrier()


def ph_l0fr(kb, E, b):
    X, XR = E["X"], E["XR"]
    kb.full_barrier()
    nv = [0, 0]
    for i, (ft, hf) in enumerate([(f, h) for f in range(8) for h in range(2)]):
        par = i % 2
        if i >= 2:
            kb.wait("Y", "V", nv[par])
        kb.dma_begin("Y", "dY%d" % par)
        n = kb.dma("Y", "dY%d" % par, lambda sy, ft=ft, hf=hf, par=par: sy.dma_start(
            XR[par][:, :], E["xscr"][ft, :, hf * 1024:(hf + 1) * 1024]))
        kb.wait("V", "dY%d" % par, n)
        nv[par] = kb.op("V", lambda v, ft=ft, hf=hf, par=par: v.tensor_tensor(
            out=X[ft][:, hf * 1024:(hf + 1) * 1024], in0=X[ft][:, hf * 1024:(hf + 1) * 1024],
            in1=XR[par][:, :], op=ALU.add))
    kb.full_barrier()
    prev = None
    for c in range(NCH):
        xs = [X[ft][:, c * CH:(c + 1) * CH] for ft in range(8)]
        f = ln_stats_apply(kb, E, xs, CH, pp=c % 2, defer=True)
        if prev is not None:
            prev()
        prev = f
    prev()
    kb.full_barrier()


def _mk_l0f(l):
    def ph(kb, E, b):
        X, ABF, KT, VT, WB = E["X"], E["ABF"], E["KT"], E["VT"], E["WB"]
        PBF, PT = E["PBF"], E["PT"]
        PS_S, PS_T, PS_M = E["PS_S"], E["PS_T"], E["PS_M"]
        KTU, VTU = E["KTU"], E["VTU"]
        kb.full_barrier()
        _cast_shift_abf(kb, E, shift=True)
        _fold_xpos(kb, E, E["xposQC"], E["xposQS"], 4 * l + 0, 4 * l + 1)
        _l0f_qk_tiles(kb, E, l, 0)
        _fold_xpos(kb, E, E["xposKC"], E["xposKS"], 4 * l + 2, 4 * l + 3)
        _l0f_qk_tiles(kb, E, l, 1)
        wbase = E["l0_wqkv"][l]
        wb_mm = [0, 0]
        ps_ev = [0, 0]
        for f4 in range(4):
            half = f4 % 2
            kb.wait("Y", "T", wb_mm[half])
            _wstream_begin(kb, half)
            nW = _stream_w_cols(kb, E, wbase, 1024 + f4 * 128, D, 1536, half)
            kb.wait("T", "dZ%d" % half, nW)
            for tt4 in range(4):
                pb = tt4 % 2
                ps = PS_S[:, pb * CH:pb * CH + CH]
                kb.wait("T", "V", ps_ev[pb])
                nT = 0
                for sub in range(4):
                    tt = tt4 * 4 + sub
                    for kt in range(8):
                        nT = kb.op("T", lambda t, kt=kt, tt=tt, sub=sub, half=half, ps=ps: t.matmul(
                            ps[:, sub * 128:(sub + 1) * 128],
                            ABF[kt][:, tt * 128:(tt + 1) * 128],
                            WB[:, half * 1024 + kt * 128: half * 1024 + kt * 128 + 128],
                            start=(kt == 0), stop=(kt == 7)))
                wb_mm[half] = nT
                kb.wait("V", "T", nT)
                nE = 0
                for sub in range(4):
                    tt = tt4 * 4 + sub
                    nE = kb.op("V", lambda v, tt=tt, sub=sub, f4=f4, ps=ps: v.tensor_copy(
                        out=VT[tt][:, f4 * 128:(f4 + 1) * 128], in_=ps[:, sub * 128:(sub + 1) * 128]))
                ps_ev[pb] = nE
        kb.full_barrier()
        OT = ABF
        units = [(iq, h) for iq in range(QT) for h in range(L0F_HEADS)]
        sc_n = {}
        exp_n = {}
        aav_n = {}
        evac_n = {}
        ttr_n = {}
        tcp_n = {}
        ABFUa = E["ABFU"]

        def emit_unit_tail(u):
            iq, h = units[u]
            p = u % 2
            tp = iq % 2
            jb0 = max(0, iq - 4)
            nj = iq - jb0 + 1
            boff = p * 1024
            po = PS_M[:, p * 512:p * 512 + 65]
            kb.wait("T", "S", exp_n[u])
            if u >= 2:
                kb.wait("T", "V", evac_n[u - 2])
            nA = 0
            for j in range(nj):
                jb = jb0 + j
                nA = kb.op("T", lambda t, j=j, jb=jb, h=h, boff=boff, po=po: t.matmul(
                    po[:, 0:64], PBF[:, boff + j * 128:boff + j * 128 + 128],
                    VT[jb][:, h * 64:h * 64 + 64], start=(j == 0), stop=False))
                nA = kb.op("T", lambda t, j=j, nj=nj, boff=boff, po=po: t.matmul(
                    po[:, 64:65], PBF[:, boff + j * 128:boff + j * 128 + 128],
                    E["ONEC"][:, :], start=False, stop=(j == nj - 1)))
            aav_n[u] = nA
            kb.wait("V", "T", nA)
            nv = kb.op("V", lambda v, po=po, p=p: v.reciprocal(
                out=E["RCP"][:, p:p + 1], in_=po[:, 64:65]))
            kb.wait("V", "V", nv)
            if h == 0 and iq >= 2:
                kb.wait("V", "T", ttr_n[iq - 2])
            evac_n[u] = kb.op("V", lambda v, po=po, tp=tp, h=h, p=p: v.tensor_scalar(
                out=PT[:, tp * 512 + h * 64:tp * 512 + h * 64 + 64],
                in0=po[:, 0:64], scalar1=E["RCP"][:, p:p + 1], scalar2=None, op0=ALU.mult))

        def emit_tile_tail(iq):
            tp = iq % 2
            kb.wait("T", "V", evac_n[iq * L0F_HEADS + L0F_HEADS - 1])
            if iq >= 2:
                kb.wait("T", "V", tcp_n[iq - 2])
            nTt = 0
            for bqd in range(4):
                nTt = kb.op("T", lambda t, bqd=bqd, tp=tp: t.transpose(
                    PS_T[:, tp * 1024 + bqd * 128:tp * 1024 + bqd * 128 + 128],
                    PT[:, tp * 512 + bqd * 128:tp * 512 + bqd * 128 + 128], E["IDB"][:, :]))
            ttr_n[iq] = nTt
            kb.wait("V", "T", nTt)
            dst = bass.AP(ABFUa.tensor, ABFUa.offset + iq * 128,
                          [[8 * T, 128], [T, 4], [1, 128]])
            tcp_n[iq] = kb.op("V", lambda v, dst=dst, tp=tp: v.tensor_copy(
                out=dst, in_=PS_T[:, tp * 1024:tp * 1024 + 512]))

        for u, (iq, h) in enumerate(units):
            p = u % 2
            jb0 = max(0, iq - 4)
            nj = iq - jb0 + 1
            wj = nj * 128
            boff = p * 1024
            ft = h // 2
            rb = (h % 2) * 64
            if u >= 2:
                kb.wait("T", "S", exp_n[u - 2])
            nT = 0
            for j in range(nj):
                jb = jb0 + j
                need_diag = (jb == iq)
                need_far = (j == 0 and iq >= 4)
                nT = kb.op("T", lambda t, j=j, jb=jb, ft=ft, rb=rb, iq=iq, boff=boff,
                           stop=not (need_diag or need_far): t.matmul(
                    PS_S[:, boff + j * 128:boff + j * 128 + 128],
                    KT[4 + ft][rb:rb + 64, jb * 128:(jb + 1) * 128],
                    KT[ft][rb:rb + 64, iq * 128:(iq + 1) * 128],
                    start=True, stop=stop))
                if need_diag:
                    nT = kb.op("T", lambda t, j=j, boff=boff: t.matmul(
                        PS_S[:, boff + j * 128:boff + j * 128 + 128],
                        E["M0B"][:, :], E["IDB"][:, :], start=False, stop=True))
                if need_far:
                    nT = kb.op("T", lambda t, j=j, boff=boff: t.matmul(
                        PS_S[:, boff + j * 128:boff + j * 128 + 128],
                        E["M4B"][:, :], E["IDB"][:, :], start=False, stop=True))
            sc_n[u] = nT
            kb.wait("S", "T", sc_n[u])
            if u >= 2:
                kb.wait("S", "T", aav_n[u - 2])
            exp_n[u] = kb.op("S", lambda s, wj=wj, boff=boff: s.activation(
                out=PBF[:, boff:boff + wj], in_=PS_S[:, boff:boff + wj], func=AF.Exp))
            if u >= 1:
                emit_unit_tail(u - 1)
            if h == 1 and iq >= 1:
                emit_tile_tail(iq - 1)
        emit_unit_tail(len(units) - 1)
        emit_tile_tail(QT - 1)
        kb.full_barrier()
        wbase = E["l0_wo"][l]
        wb_mm = [0, 0]
        ps_ev = [0, 0]
        for ft in range(8):
            half = ft % 2
            kb.wait("Y", "T", wb_mm[half])
            _wstream_begin(kb, half)
            nW = _stream_w_cols(kb, E, wbase, ft * 128, 512, D, half)
            kb.wait("T", "dZ%d" % half, nW)
            for c in range(NCH):
                pb = c % 2
                ps = PS_S[:, pb * CH:pb * CH + CH]
                kb.wait("T", "V", ps_ev[pb])
                nT = 0
                for kt in range(4):
                    nT = kb.op("T", lambda t, kt=kt, c=c, half=half, ps=ps: t.matmul(
                        ps, WB[:, half * 1024 + kt * 128: half * 1024 + kt * 128 + 128],
                        OT[kt][:, c * CH:(c + 1) * CH], start=(kt == 0), stop=(kt == 3)))
                wb_mm[half] = nT
                kb.wait("V", "T", nT)
                ps_ev[pb] = kb.op("V", lambda v, ft=ft, c=c, ps=ps: v.tensor_tensor(
                    out=X[ft][:, c * CH:(c + 1) * CH], in0=X[ft][:, c * CH:(c + 1) * CH],
                    in1=ps, op=ALU.add))
        kb.full_barrier()
        _cast_shift_abf(kb, E, shift=True)
        gcols = [E["LNC"][:, l * 16 + ft:l * 16 + ft + 1] for ft in range(8)]
        bcols = [E["LNC"][:, l * 16 + 8 + ft:l * 16 + 8 + ft + 1] for ft in range(8)]
        prev = None
        for c in range(NCH):
            bfs = [ABF[ft][:, c * CH:(c + 1) * CH] for ft in range(8)]
            f = ln_stats_apply_bf(kb, E, bfs, CH, gcols, bcols, pp=c % 2, defer=True)
            if prev is not None:
                prev()
            prev = f
        prev()
        kb.full_barrier()
        w1b = E["l0_w1re"][l]
        w2b = E["l0_w2p"][l]
        W1S = [KTU[:, 0:1024], KTU[:, 1024:2048]]
        W2S = [KTU[:, 2048:2048 + 2816], KTU[:, 4864:4864 + 2816]]
        for rnd in range(3):
            fp0 = rnd * 8
            fp1 = min(NHT, fp0 + 8)
            wb_mm = [0, 0]
            h4_v = [0, 0]
            gel_s = [0, 0]
            for fp in range(fp0, fp1):
                half = fp % 2
                kb.wait("Y", "T", wb_mm[half])
                _wstream_begin(kb, half)
                nWa = _stream_w_cols(kb, E, w1b, (2 * fp) * 128, D, 2 * HIDP, half,
                                     dest=W1S[half][:, 0:1024])
                nWg = _stream_w_cols(kb, E, w1b, (2 * fp + 1) * 128, D, 2 * HIDP, half)
                kb.wait("T", "dZ%d" % half, nWg)
                acol = E["B1C"][:, l * 2 * NHT + 2 * fp: l * 2 * NHT + 2 * fp + 1]
                gcol = E["B1C"][:, l * 2 * NHT + 2 * fp + 1: l * 2 * NHT + 2 * fp + 2]
                for c in range(NCH):
                    p = c % 2
                    psa = PS_S[:, 2 * p * CH:(2 * p + 1) * CH]
                    psg = PS_S[:, (2 * p + 1) * CH:(2 * p + 2) * CH]
                    kb.wait("T", "V", h4_v[p])
                    kb.wait("T", "S", gel_s[p])
                    nTa = 0
                    for kt in range(8):
                        nTa = kb.op("T", lambda t, kt=kt, c=c, half=half, psa=psa: t.matmul(
                            psa, W1S[half][:, kt * 128:kt * 128 + 128],
                            ABF[kt][:, c * CH:(c + 1) * CH], start=(kt == 0), stop=(kt == 7)))
                    nTg = 0
                    for kt in range(8):
                        nTg = kb.op("T", lambda t, kt=kt, c=c, half=half, psg=psg: t.matmul(
                            psg, WB[:, half * 1024 + kt * 128: half * 1024 + kt * 128 + 128],
                            ABF[kt][:, c * CH:(c + 1) * CH], start=(kt == 0), stop=(kt == 7)))
                    wb_mm[half] = nTg
                    kb.wait("S", "T", nTg)
                    kb.wait("S", "V", h4_v[p])
                    nG = kb.op("S", lambda s, psg=psg, gcol=gcol, p=p: s.activation(
                        out=E["SQB"][:, p * CH:(p + 1) * CH], in_=psg, func=AF.Gelu,
                        bias=gcol[0:128, 0:1]))
                    gel_s[p] = nG
                    kb.wait("V", "T", nTa)
                    nA = kb.op("V", lambda v, psa=psa, acol=acol, p=p: v.tensor_scalar(
                        out=E["SCB"][:, p * CH:(p + 1) * CH], in0=psa, scalar1=acol[0:128, 0:1],
                        scalar2=None, op0=ALU.add))
                    kb.wait("V", "S", nG)
                    kb.wait("V", "V", nA)
                    h4_v[p] = kb.op("V", lambda v, fp=fp, fp0=fp0, c=c, p=p: v.tensor_tensor(
                        out=VTU[:, ((fp - fp0) * 4 + c) * CH:((fp - fp0) * 4 + c) * CH + CH],
                        in0=E["SCB"][:, p * CH:(p + 1) * CH],
                        in1=E["SQB"][:, p * CH:(p + 1) * CH], op=ALU.mult))
            kb.full_barrier()
            nkt = fp1 - fp0
            wb_mm = [0, 0]
            ps_ev = [0, 0]
            for ft in range(8):
                half = ft % 2
                kb.wait("Y", "T", wb_mm[half])
                _wstream_begin(kb, half)
                w2rows = bass.AP(w2b.tensor, w2b.offset + ft * 128 * HIDP + fp0 * 128,
                                 [[HIDP, 128], [1, nkt * 128]])
                nW = kb.dma("Y", "dZ%d" % half, lambda sy, re=w2rows, half=half, nkt=nkt: sy.dma_start(
                    W2S[half][:, 0:nkt * 128], re))
                kb.wait("T", "dZ%d" % half, nW)
                for c in range(NCH):
                    pb = c % 2
                    ps = PS_S[:, 2 * CH + pb * CH:2 * CH + pb * CH + CH]
                    kb.wait("T", "V", ps_ev[pb])
                    nT = 0
                    for kk in range(nkt):
                        nT = kb.op("T", lambda t, kk=kk, c=c, half=half, ps=ps, nkt=nkt: t.matmul(
                            ps, W2S[half][:, kk * 128:kk * 128 + 128],
                            VTU[:, (kk * 4 + c) * CH:(kk * 4 + c) * CH + CH],
                            start=(kk == 0), stop=(kk == nkt - 1)))
                    wb_mm[half] = nT
                    kb.wait("V", "T", nT)
                    if rnd < 2:
                        ps_ev[pb] = kb.op("V", lambda v, ft=ft, c=c, ps=ps: v.tensor_tensor(
                            out=X[ft][:, c * CH:(c + 1) * CH], in0=X[ft][:, c * CH:(c + 1) * CH],
                            in1=ps, op=ALU.add))
                    else:
                        b2col = E["B2C"][:, l * 8 + ft:l * 8 + ft + 1]
                        ps_ev[pb] = kb.op("V", lambda v, ft=ft, c=c, ps=ps, b2col=b2col: v.scalar_tensor_tensor(
                            out=X[ft][:, c * CH:(c + 1) * CH], in0=ps, scalar=b2col,
                            in1=X[ft][:, c * CH:(c + 1) * CH], op0=ALU.add, op1=ALU.add))
            kb.full_barrier()
    return ph


PH_FUNCS["l0fs"] = ph_l0fs
PH_FUNCS["l0fr"] = ph_l0fr
PH_FUNCS["l0f0"] = _mk_l0f(0)
PH_FUNCS["l0f1"] = _mk_l0f(1)



def ph_l1a(kb, E, b):
    X, ABF, KT, VT, QJ, WB, OE = E["X"], E["ABF"], E["KT"], E["VT"], E["QJ"], E["WB"], E["OE"]
    PBF, PT = E["PBF"], E["PT"]
    PS_S, PS_T, PS_M = E["PS_S"], E["PS_T"], E["PS_M"]
    BQC, BOC = E["BQC"], E["BOC"]
    wqkv = E["l1_wqkv"]
    kb.full_barrier()
    _cast_shift_abf(kb, E, shift=False)
    wb_mm = [0, 0]
    ps_ev = [0, 0]
    for ft in range(8):
        half = ft % 2
        kb.wait("Y", "T", wb_mm[half])
        _wstream_begin(kb, half)
        nW = _stream_w_cols(kb, E, wqkv, 1024 + ft * 128, D, 3 * D, half)
        kb.wait("T", "dZ%d" % half, nW)
        for c in range(NCH):
            pb = c % 2
            ps = PS_S[:, pb * CH:pb * CH + CH]
            kb.wait("T", "V", ps_ev[pb])
            nT = 0
            for kt in range(8):
                nT = kb.op("T", lambda t, kt=kt, c=c, half=half, ps=ps: t.matmul(
                    ps, WB[:, half * 1024 + kt * 128: half * 1024 + kt * 128 + 128],
                    ABF[kt][:, c * CH:(c + 1) * CH], start=(kt == 0), stop=(kt == 7)))
            wb_mm[half] = nT
            kb.wait("V", "T", nT)
            ps_ev[pb] = kb.op("V", lambda v, ft=ft, c=c, ps=ps: v.tensor_scalar(
                out=KT[ft][:, c * CH:(c + 1) * CH], in0=ps,
                scalar1=BQC[:, 8 + ft:9 + ft], scalar2=0.125, op0=ALU.add, op1=ALU.mult))
    kb.full_barrier()
    wb_mm = [0, 0]
    ps_ev = [0, 0]
    for f4 in range(8):
        half = f4 % 2
        kb.wait("Y", "T", wb_mm[half])
        _wstream_begin(kb, half)
        nW = _stream_w_cols(kb, E, wqkv, 2048 + f4 * 128, D, 3 * D, half)
        kb.wait("T", "dZ%d" % half, nW)
        for tt4 in range(4):
            pb = tt4 % 2
            ps = PS_S[:, pb * CH:pb * CH + CH]
            kb.wait("T", "V", ps_ev[pb])
            nT = 0
            for sub in range(4):
                tt = tt4 * 4 + sub
                for kt in range(8):
                    nT = kb.op("T", lambda t, kt=kt, tt=tt, sub=sub, half=half, ps=ps: t.matmul(
                        ps[:, sub * 128:(sub + 1) * 128],
                        ABF[kt][:, tt * 128:(tt + 1) * 128],
                        WB[:, half * 1024 + kt * 128: half * 1024 + kt * 128 + 128],
                        start=(kt == 0), stop=(kt == 7)))
            wb_mm[half] = nT
            kb.wait("V", "T", nT)
            nE = 0
            for sub in range(4):
                tt = tt4 * 4 + sub
                nE = kb.op("V", lambda v, tt=tt, sub=sub, f4=f4, ps=ps: v.tensor_copy(
                    out=VT[tt][:, f4 * 128:(f4 + 1) * 128], in_=ps[:, sub * 128:(sub + 1) * 128]))
            ps_ev[pb] = nE
    kb.full_barrier()
    for g in range(4):
        wb_mm = [0, 0]
        ps_ev = [0, 0]
        for ft in range(8):
            half = ft % 2
            kb.wait("Y", "T", wb_mm[half])
            _wstream_begin(kb, half)
            nW = _stream_w_cols(kb, E, wqkv, ft * 128, D, 3 * D, half)
            kb.wait("T", "dZ%d" % half, nW)
            kb.wait("T", "V", ps_ev[half])
            ps = PS_S[:, (ft % 2) * CH:(ft % 2) * CH + CH]
            nT = 0
            for kt in range(8):
                nT = kb.op("T", lambda t, kt=kt, g=g, half=half, ps=ps: t.matmul(
                    ps, WB[:, half * 1024 + kt * 128: half * 1024 + kt * 128 + 128],
                    ABF[kt][:, g * CH:(g + 1) * CH], start=(kt == 0), stop=(kt == 7)))
            wb_mm[half] = nT
            kb.wait("V", "T", nT)
            ps_ev[half] = kb.op("V", lambda v, ft=ft, ps=ps: v.tensor_scalar(
                out=QJ[ft][:, :], in0=ps, scalar1=BQC[:, ft:ft + 1], scalar2=None, op0=ALU.add))
        kb.full_barrier()
        units = [(iq, h) for iq in range(4 * g, 4 * g + 4) for h in range(16)]
        exp_n = {}
        expa_n = {}
        aav_n = {}
        evac_n = {}
        for u, (iq, h) in enumerate(units):
            if L1A_SERIAL:
                kb.full_barrier()
            jext = (iq + 1) * 128
            dbl = jext <= 1024
            boff = (u % 2) * 1024 if dbl else 0
            ft = h // 2
            rb = (h % 2) * 64
            il = (iq - 4 * g) * 128
            dd = 1 if (L1A_V1 or not dbl) else 2
            if u >= dd:
                key = expa_n.get(u - dd, exp_n[u - dd]) if not dbl else exp_n[u - dd]
                kb.wait("T", "S", key)
            nT = 0
            sc_a = 0
            for jb in range(iq + 1):
                last = (jb == iq)
                if not dbl and jb == 8:
                    sc_a = nT
                    kb.wait("T", "S", exp_n[u - 1] if u >= 1 else 0)
                nT = kb.op("T", lambda t, jb=jb, ft=ft, rb=rb, il=il, boff=boff, last=last: t.matmul(
                    PS_S[:, boff + jb * 128:boff + jb * 128 + 128],
                    KT[ft][rb:rb + 64, jb * 128:(jb + 1) * 128],
                    QJ[ft][rb:rb + 64, il:il + 128],
                    start=True, stop=not last))
            nT = kb.op("T", lambda t, jext=jext, boff=boff: t.matmul(
                PS_S[:, boff + jext - 128:boff + jext],
                E["M0B"][:, :], E["IDB"][:, :], start=False, stop=True))
            sc_n = nT
            if not dbl:
                slot = u % 2
                PTb = E["PT"] if slot == 0 else E["PBF"]
                kb.wait("S", "T", sc_a)
                dd2 = 1 if L1A_V2 else 2
                if u >= dd2 and (u - dd2) in aav_n:
                    kb.wait("S", "T", aav_n[u - dd2])
                expa_n[u] = kb.op("S", lambda s, PTb=PTb: s.activation(
                    out=PTb[:, 0:1024], in_=PS_S[:, 0:1024], func=AF.Exp))
            if u >= 1:
                pq, ph_ = units[u - 1]
                pjext = (pq + 1) * 128
                pslot = (u - 1) % 2
                PTp = E["PT"] if pslot == 0 else E["PBF"]
                po = PS_M[:, pslot * CH:pslot * CH + 65]
                kb.wait("T", "S", exp_n[u - 1])
                dd3 = 2 if L1A_V3 else 3
                if u >= dd3 and (u - dd3) in evac_n:
                    kb.wait("T", "V", evac_n[u - dd3])
                nA = 0
                for jb in range(pq + 1):
                    nA = kb.op("T", lambda t, jb=jb, ph_=ph_, po=po, PTp=PTp, pq=pq: t.matmul(
                        po[:, 0:64], PTp[:, jb * 128:(jb + 1) * 128],
                        VT[jb][:, ph_ * 64:ph_ * 64 + 64],
                        start=(jb == 0), stop=False))
                    nA = kb.op("T", lambda t, jb=jb, po=po, PTp=PTp, pq=pq: t.matmul(
                        po[:, 64:65], PTp[:, jb * 128:(jb + 1) * 128],
                        E["ONEC"][:, :], start=False, stop=(jb == pq)))
                aav_n[u - 1] = nA
                kb.wait("V", "T", nA)
                rp = (u - 1) % 2
                nv = kb.op("V", lambda v, po=po, rp=rp: v.reciprocal(
                    out=E["RCP"][:, rp:rp + 1], in_=po[:, 64:65]))
                kb.wait("V", "V", nv)
                piq = pq
                evac_n[u - 1] = kb.op("V", lambda v, po=po, piq=piq, ph_=ph_, g=g, rp=rp: v.tensor_scalar(
                    out=E["OEU"][:, (piq - 4 * g) * 1024 + ph_ * 64:(piq - 4 * g) * 1024 + ph_ * 64 + 64],
                    in0=po[:, 0:64], scalar1=E["RCP"][:, rp:rp + 1], scalar2=None, op0=ALU.mult))
            slot = u % 2
            PTb = E["PT"] if slot == 0 else E["PBF"]
            kb.wait("S", "T", sc_n)
            dd2 = 1 if L1A_V2 else 2
            if dbl and u >= dd2 and (u - dd2) in aav_n:
                kb.wait("S", "T", aav_n[u - dd2])
            if dbl:
                exp_n[u] = kb.op("S", lambda s, jext=jext, boff=boff, PTb=PTb: s.activation(
                    out=PTb[:, 0:jext], in_=PS_S[:, boff:boff + jext], func=AF.Exp))
            else:
                exp_n[u] = kb.op("S", lambda s, jext=jext, PTb=PTb: s.activation(
                    out=PTb[:, 1024:jext], in_=PS_S[:, 1024:jext], func=AF.Exp))
        u = len(units) - 1
        iq, h = units[u]
        jext = (iq + 1) * 128
        slot = u % 2
        PTb = E["PT"] if slot == 0 else E["PBF"]
        po = PS_M[:, slot * CH:slot * CH + 65]
        kb.wait("T", "S", exp_n[u])
        kb.wait("T", "V", evac_n.get(u - 2, 0))
        nA = 0
        for jb in range(iq + 1):
            nA = kb.op("T", lambda t, jb=jb, h=h, po=po, PTb=PTb, iq=iq: t.matmul(
                po[:, 0:64], PTb[:, jb * 128:(jb + 1) * 128],
                VT[jb][:, h * 64:h * 64 + 64], start=(jb == 0), stop=False))
            nA = kb.op("T", lambda t, jb=jb, po=po, PTb=PTb, iq=iq: t.matmul(
                po[:, 64:65], PTb[:, jb * 128:(jb + 1) * 128],
                E["ONEC"][:, :], start=False, stop=(jb == iq)))
        kb.wait("V", "T", nA)
        rp = u % 2
        nv = kb.op("V", lambda v, po=po, rp=rp: v.reciprocal(
            out=E["RCP"][:, rp:rp + 1], in_=po[:, 64:65]))
        kb.wait("V", "V", nv)
        kb.op("V", lambda v, po=po, iq=iq, h=h, g=g, rp=rp: v.tensor_scalar(
            out=E["OEU"][:, (iq - 4 * g) * 1024 + h * 64:(iq - 4 * g) * 1024 + h * 64 + 64],
            in0=po[:, 0:64], scalar1=E["RCP"][:, rp:rp + 1], scalar2=None, op0=ALU.mult))
        kb.full_barrier()
        QJU = E["QJU"]
        scp = [0, 0]
        for it in range(4):
            for grp in range(2):
                bank = (it * 2 + grp) % 2
                if it * 2 + grp >= 2:
                    kb.wait("T", "S", scp[bank])
                nTt = 0
                for k in range(4):
                    ft = grp * 4 + k
                    nTt = kb.op("T", lambda t, it=it, ft=ft, k=k, bank=bank: t.transpose(
                        PS_T[:, bank * 1024 + k * 128: bank * 1024 + k * 128 + 128],
                        E["OEU"][:, it * 1024 + ft * 128:it * 1024 + ft * 128 + 128], E["IDB"][:, :]))
                kb.wait("S", "T", nTt)
                dst = bass.AP(QJU.tensor, QJU.offset + grp * 4 * CH + it * 128,
                              [[4096, 128], [CH, 4], [1, 128]])
                scp[bank] = kb.op("S", lambda s, dst=dst, bank=bank: s.copy(
                    out=dst, in_=PS_T[:, bank * 1024: bank * 1024 + 512]))
        kb.full_barrier()
        wout = E["l1_wout"]
        if g == 0:
            nbv = kb.op("V", lambda v: v.tensor_copy(out=E["BVB"][:, :], in_=E["BQC"][:, 16:24]))
            kb.wait("T", "V", nbv)
        wb_mm2 = [0, 0]
        ps_ev2 = [0, 0]
        for ft in range(8):
            half = ft % 2
            kb.wait("Y", "T", wb_mm2[half])
            _wstream_begin(kb, half)
            nW = _stream_w_cols(kb, E, wout, ft * 128, D, D, half)
            kb.wait("T", "dZ%d" % half, nW)
            kb.wait("T", "V", ps_ev2[half])
            ps = PS_S[:, (ft % 2) * CH:(ft % 2) * CH + CH]
            nT = 0
            for kt in range(8):
                nT = kb.op("T", lambda t, kt=kt, half=half, ps=ps: t.matmul(
                    ps, WB[:, half * 1024 + kt * 128: half * 1024 + kt * 128 + 128],
                    QJ[kt][:, :], start=(kt == 0), stop=(kt == 7)))
            wb_mm2[half] = nT
            if g == 0:
                kb.wait("T", "V", kb.c["V"])
                nb2 = 0
                for kt in range(8):
                    nb2 = kb.op("T", lambda t, kt=kt, half=half: t.matmul(
                        PS_M[:, 65:66], WB[:, half * 1024 + kt * 128: half * 1024 + kt * 128 + 128],
                        E["BVB"][:, kt:kt + 1],
                        start=(kt == 0), stop=(kt == 7)))
                wb_mm2[half] = nb2
                kb.wait("V", "T", nb2)
                nbb = kb.op("V", lambda v, ft=ft: v.tensor_tensor(
                    out=E["BOC2"][:, ft:ft + 1], in0=PS_M[:, 65:66], in1=E["BOC"][:, ft:ft + 1],
                    op=ALU.add))
                kb.wait("V", "V", nbb)
            kb.wait("V", "T", nT)
            ps_ev2[half] = kb.op("V", lambda v, ft=ft, g=g, ps=ps: v.scalar_tensor_tensor(
                out=X[ft][:, g * CH:(g + 1) * CH], in0=ps, scalar=E["BOC2"][:, ft:ft + 1],
                in1=X[ft][:, g * CH:(g + 1) * CH], op0=ALU.add, op1=ALU.add))
        kb.full_barrier()
        xs = [X[ft][:, g * CH:(g + 1) * CH] for ft in range(8)]
        ln_stats_apply(kb, E, xs, CH, pp=g % 2)
        kb.full_barrier()


def ph_l1f(kb, E, b):
    X, ABF, WB = E["X"], E["ABF"], E["WB"]
    PS_S, PS_T, PS_M = E["PS_S"], E["PS_T"], E["PS_M"]
    QJ = E["QJ"]
    OE = E["OE"]
    MS = E["TB0"]
    BQC = E["BQC"]
    kb.full_barrier()
    _cast_shift_abf(kb, E, shift=False)

    def bank(k):
        if k < 4:
            return PS_S[:, k * CH:(k + 1) * CH]
        return PS_M[:, (k - 4) * CH:(k - 3) * CH]

    GW = E["SR"]
    gw_ap = bass.AP(E["gate_wT"].tensor, E["gate_wT"].offset,
                    [[8, 128], [1024, 8], [1, 8]])
    kb.dma_begin("Y", "dY0")
    ngw = kb.dma("Y", "dY0", lambda sy: sy.dma_start(GW[:, 0:64], gw_ap))
    kb.wait("T", "dY0", ngw)
    for tt in range(QT):
        kb.wait("T", "V", kb.c["V"])
        kb.wait("T", "S", kb.c["S"])
        nT = 0
        for kt in range(8):
            nT = kb.op("T", lambda t, kt=kt, tt=tt: t.matmul(
                PS_M[:, 0:8], X[kt][:, tt * 128:(tt + 1) * 128],
                GW[:, kt * 8:(kt + 1) * 8], start=(kt == 0), stop=(kt == 7)))
        kb.wait("V", "T", nT)
        nV = kb.op("V", lambda v: v.reduce_max(out=E["NMX"][:, 0:1], in_=PS_M[:, 0:8], axis=AX.X))
        kb.wait("V", "V", nV)
        nV = kb.op("V", lambda v: v.tensor_scalar(
            out=E["SCB"][:, 0:8], in0=PS_M[:, 0:8], scalar1=E["NMX"][:, 0:1],
            scalar2=None, op0=ALU.is_ge))
        kb.wait("T", "V", nV)
        nTt = kb.op("T", lambda t, tt=tt: t.transpose(
            PS_T[0:8, (tt % 2) * 1024:(tt % 2) * 1024 + 128], E["SCB"][:, 0:8], E["IDB"][:, :]))
        kb.wait("S", "T", nTt)
        kb.op("S", lambda s, tt=tt: s.copy(
            out=MS[0:8, tt * 128:(tt + 1) * 128],
            in_=PS_T[0:8, (tt % 2) * 1024:(tt % 2) * 1024 + 128]))
    kb.full_barrier()

    ew = E["ew_T"]
    for c in range(NCH):
        prevS = 0
        mask_s = {}
        for e in range(8):
            kb.wait("T", "S", prevS)
            nb = kb.op("T", lambda t, e=e, c=c: t.matmul(
                PS_M[:, CH:2 * CH], E["SELSb"][:, e * 128:(e + 1) * 128],
                MS[0:8, c * CH:(c + 1) * CH], start=True, stop=True))
            kb.wait("S", "T", nb)
            prevS = kb.op("S", lambda s, e=e: s.copy(out=QJ[e][:, :], in_=PS_M[:, CH:2 * CH]))
            mask_s[e] = prevS
        kb.wait("T", "S", prevS)
        PTa, PBFa = E["PT"], E["PBF"]

        def xe_ap(s_, kt):
            if s_ == 0:
                return OE[kt][:, :]
            return (PTa if kt < 4 else PBFa)[:, (kt % 4) * CH:(kt % 4 + 1) * CH]
        ev_n = 0
        lastT = 0
        xe_mm = [0, 0]
        wb_cons = [0, 0]
        for (f0, f1) in ((0, 6), (6, 8)):
            for e in range(8):
                xs = e % 2
                kb.wait("V", "T", xe_mm[xs])
                kb.wait("V", "S", mask_s[e])
                nxe = 0
                for kt in range(8):
                    nxe = kb.op("V", lambda v, kt=kt, c=c, e=e, xs=xs: v.tensor_tensor(
                        out=xe_ap(xs, kt), in0=ABF[kt][:, c * CH:(c + 1) * CH], in1=QJ[e][:, :],
                        op=ALU.mult))
                nq = (f1 - f0 + 3) // 4
                fts = list(range(f0, f1))
                qgroups = [fts[i:i + 4] for i in range(0, len(fts), 4)]
                for qg in qgroups:
                    half = (e + qg[0]) % 2
                    WQ = E["VTU"][:, half * 8192:half * 8192 + len(qg) * 1024]
                    kb.wait("Y", "T", wb_cons[half])
                    _wstream_begin(kb, half)
                    re = _wtile_ap(ew[e], qg[0], D, ntile=len(qg))
                    nW = kb.dma("Y", "dZ%d" % half, lambda sy, re=re, WQ=WQ: sy.dma_start(WQ, re))
                    kb.wait("T", "dZ%d" % half, nW)
                    kb.wait("T", "V", nxe)
                    if e == 0 and qg[0] == f0 and f0 == 0:
                        kb.wait("T", "V", ev_n)
                    for fi, ft in enumerate(qg):
                        for kt in range(8):
                            lastT = kb.op("T", lambda t, kt=kt, ft=ft, fi=fi, e=e, xs=xs, f0=f0, WQ=WQ: t.matmul(
                                bank(ft - f0),
                                WQ[:, fi * 1024 + kt * 128: fi * 1024 + kt * 128 + 128],
                                xe_ap(xs, kt), start=(e == 0 and kt == 0), stop=(e == 7 and kt == 7)))
                    wb_cons[half] = lastT
                xe_mm[xs] = lastT
            kb.wait("V", "T", lastT)
            for ft in range(f0, f1):
                ev_n = kb.op("V", lambda v, ft=ft, c=c, f0=f0: v.tensor_tensor(
                    out=X[ft][:, c * CH:(c + 1) * CH], in0=X[ft][:, c * CH:(c + 1) * CH],
                    in1=bank(ft - f0), op=ALU.add))
        kb.full_barrier()
        xs = [X[ft][:, c * CH:(c + 1) * CH] for ft in range(8)]
        ln_stats_apply(kb, E, xs, CH, pp=c % 2)
        kb.full_barrier()


PH_FUNCS["l1a"] = ph_l1a
L1A_SERIAL = False
L1A_V1 = False
L1A_V2 = False
L1A_V3 = False
PH_FUNCS["l1f"] = ph_l1f


def _in_maps_for(w, C, xs):
    m = dict(w)
    m.update({k: C[k] for k in ("ropeC", "ropeS", "ident", "JTB", "RTB", "M0", "M4",
                                 "HS", "HST", "SELS", "xposQC", "xposQS", "xposKC", "xposKS")})
    m["x"] = xs
    return m


def kernel(**inputs):
    C = _consts()
    w = _prep_weights(inputs)
    x = np.ascontiguousarray(np.asarray(inputs["L0_attn_x"], np.float32))
    nc = build_core_program(["l0a", "l0fs", "l0f0", "l0f1", "l0fr", "l1a", "l1f"])
    in_maps = []
    for c in range(8):
        in_maps.append(_in_maps_for(w, C, np.ascontiguousarray(x[2 * c:2 * c + 2])))
    res = run_bass_kernel_spmd(nc, in_maps, core_ids=list(range(8)))
    out = np.concatenate([r["y"] for r in res.results], axis=0)
    return out.astype(np.float32)


# version 19
# speedup vs baseline: 1.0267x; 1.0267x over previous
import numpy as np
from contextlib import ExitStack

import concourse.bass as bass
import concourse.mybir as mybir
from concourse.bass_utils import run_bass_kernel_spmd

F32 = mybir.dt.float32
BF16 = mybir.dt.bfloat16
AF = mybir.ActivationFunctionType
ALU = mybir.AluOpType
AX = mybir.AxisListType

T = 2048
D = 1024
CH = 512
NCH = T // CH
FT = D // 128
QT = T // 128
INV1024 = 1.0 / 1024.0
NEG = -1.0e30

L0F_HEADS = 8
L0F_DH = 64
L0F_INNER = 512
L0F_WIN = 512
HID = 2730
HIDP = 2816
NHT = HIDP // 128


def _host_consts():
    c = {}
    c["ident"] = np.eye(128, dtype=np.float32)
    half = D // 2
    inv = 1.0 / (10000.0 ** (np.arange(half, dtype=np.float64) / half))
    ang = np.arange(T, dtype=np.float64)[:, None] * inv[None, :]
    cosv = np.cos(ang).T
    sinv = np.sin(ang).T
    ropeC = np.repeat(cosv, 2, axis=0).astype(np.float32)
    sgn = np.where(np.arange(D) % 2 == 0, -1.0, 1.0)[:, None]
    ropeS = (np.repeat(sinv, 2, axis=0) * sgn).astype(np.float32)
    c["ropeC"] = np.ascontiguousarray(ropeC)
    c["ropeS"] = np.ascontiguousarray(ropeS)
    P = np.zeros((128, 128), dtype=np.float32)
    i2 = np.arange(0, 128, 2)
    P[i2, i2 + 1] = 1.0
    P[i2 + 1, i2] = 1.0
    c["JTB"] = P
    r = np.arange(128)[:, None]
    cc = np.arange(128)[None, :]
    c["M0"] = np.where(cc > r, NEG, 0.0).astype(np.float32)
    c["M4"] = np.where(cc < r, NEG, 0.0).astype(np.float32)
    dh = L0F_DH
    hi = np.arange(0, dh, 2, dtype=np.float64)
    invf = 1.0 / (10000.0 ** (hi / dh))
    tt = np.arange(T, dtype=np.float64)
    fr32 = tt[:, None] * invf[None, :]
    fr = np.concatenate([fr32, fr32], axis=1)
    base = (hi + 0.4 * dh) / (1.4 * dh)
    power = (tt - T // 2) / float(L0F_WIN // 2)
    sc32 = base[None, :] ** power[:, None]
    sc = np.concatenate([sc32, sc32], axis=1)
    cosf = np.cos(fr)
    sinf = np.sin(fr)
    sgn64 = np.where(np.arange(dh) < dh // 2, -1.0, 1.0)[None, :]
    qc = (cosf * sc * 8.0).T
    qs = (sinf * sc * 8.0 * sgn64).T
    kc = (cosf / sc).T
    ks = (sinf / sc * sgn64).T
    c["xposQC"] = np.ascontiguousarray(np.tile(qc, (2, 1)).astype(np.float32))
    c["xposQS"] = np.ascontiguousarray(np.tile(qs, (2, 1)).astype(np.float32))
    c["xposKC"] = np.ascontiguousarray(np.tile(kc, (2, 1)).astype(np.float32))
    c["xposKS"] = np.ascontiguousarray(np.tile(ks, (2, 1)).astype(np.float32))
    R = np.zeros((128, 128), dtype=np.float32)
    for hh in range(2):
        o = hh * 64
        for rr in range(32):
            R[o + rr, o + rr + 32] = 1.0
            R[o + rr + 32, o + rr] = 1.0
    c["RTB"] = R.T.copy()
    HS = np.zeros((128, 2), dtype=np.float32)
    HS[0:64, 0] = 1.0
    HS[64:128, 1] = 1.0
    c["HS"] = HS
    c["HST"] = np.ascontiguousarray(HS.T)
    SELS = np.zeros((8, 8 * 128), dtype=np.float32)
    for e in range(8):
        SELS[e, e * 128:(e + 1) * 128] = 1.0
    c["SELS"] = SELS
    return c


_CONSTS = None

def _consts():
    global _CONSTS
    if _CONSTS is None:
        _CONSTS = _host_consts()
    return _CONSTS



def _timg(W):
    R, Cc = W.shape
    return np.ascontiguousarray(
        W.reshape(R // 128, 128, Cc // 128, 128).transpose(2, 1, 0, 3).reshape(Cc // 128, 128, R))


def _prep_weights(inp):
    w = {}
    for nm, key in (("wqbig", "L0_attn_w_q"), ("wkbig", "L0_attn_w_k"), ("wvbig", "L0_attn_w_v")):
        src = np.asarray(inp[key])
        big = np.zeros((64, 16, 64, 16), dtype=np.float32)
        hh = np.arange(16)
        big[:, hh, :, hh] = src.transpose(2, 0, 1)
        w[nm] = np.ascontiguousarray(big.reshape(1024, 1024))
    w["l0_wqkv"] = np.ascontiguousarray(np.asarray(inp["L0_ffn_wqkv"], np.float32))
    w["l0_wo"] = np.ascontiguousarray(np.asarray(inp["L0_ffn_wo"], np.float32))
    w1 = np.asarray(inp["L0_ffn_w1"], np.float32)
    b1 = np.asarray(inp["L0_ffn_b1"], np.float32)
    w1re = np.zeros((2, 1024, 2 * HIDP), dtype=np.float32)
    b1re = np.zeros((2, 2 * HIDP), dtype=np.float32)
    for l in range(2):
        a = np.zeros((1024, HIDP), np.float32)
        g = np.zeros((1024, HIDP), np.float32)
        a[:, :HID] = w1[l][:, :HID]
        g[:, :HID] = w1[l][:, HID:]
        inter = np.stack([a.reshape(1024, NHT, 128), g.reshape(1024, NHT, 128)], axis=2)
        w1re[l] = inter.reshape(1024, 2 * HIDP)
        ab = np.zeros(HIDP, np.float32)
        gb = np.zeros(HIDP, np.float32)
        ab[:HID] = b1[l][:HID]
        gb[:HID] = b1[l][HID:]
        interb = np.stack([ab.reshape(NHT, 128), gb.reshape(NHT, 128)], axis=1)
        b1re[l] = interb.reshape(2 * HIDP)
    w["l0_w1re"] = w1re
    w["l0_b1c"] = np.ascontiguousarray(
        np.concatenate([b1re[l].reshape(2 * NHT, 128).T for l in range(2)], axis=1))
    w2 = np.asarray(inp["L0_ffn_w2"], np.float32)
    w2p = np.zeros((2, HIDP, 1024), np.float32)
    w2p[:, :HID, :] = w2
    w["l0_w2p"] = w2p
    w["l0_b2c"] = np.ascontiguousarray(
        np.stack([np.asarray(inp["L0_ffn_b2"], np.float32)[l].reshape(8, 128).T for l in range(2)], axis=1).reshape(128, 16))
    qs = np.asarray(inp["L0_ffn_q_scale"], np.float32)
    ks = np.asarray(inp["L0_ffn_k_scale"], np.float32)
    cols = []
    for l in range(2):
        q1 = np.tile(qs[l], 2)
        q2 = np.tile(np.roll(qs[l], 32), 2)
        k1 = np.tile(ks[l], 2)
        k2 = np.tile(np.roll(ks[l], 32), 2)
        cols += [q1, q2, k1, k2]
    w["l0_qksc"] = np.ascontiguousarray(np.stack(cols, axis=1))
    lng = np.asarray(inp["L0_ffn_ln_g"], np.float32)
    lnb = np.asarray(inp["L0_ffn_ln_b"], np.float32)
    cols = []
    for l in range(2):
        cols.append(lng[l].reshape(8, 128).T)
        cols.append(lnb[l].reshape(8, 128).T)
    w["l0_lnc"] = np.ascontiguousarray(np.concatenate(cols, axis=1))
    w["l1_wqkv"] = np.ascontiguousarray(np.asarray(inp["L1_attn_Wqkv"], np.float32))
    w["l1_wout"] = np.ascontiguousarray(np.asarray(inp["L1_attn_Wout"], np.float32))
    w["l1_bqc"] = np.ascontiguousarray(np.asarray(inp["L1_attn_bqkv"], np.float32).reshape(24, 128).T)
    w["l1_boc"] = np.ascontiguousarray(np.asarray(inp["L1_attn_bout"], np.float32).reshape(8, 128).T)
    w["gate_wT"] = np.ascontiguousarray(np.asarray(inp["L1_ffn_gate_w"], np.float32).T)
    ew = np.asarray(inp["L1_ffn_expert_w"], np.float32).transpose(0, 2, 1)
    for nm in ("wqbig", "wkbig", "wvbig"):
        w[nm] = _timg(w[nm])
    w["l0_wqkv"] = np.stack([_timg(w["l0_wqkv"][l]) for l in range(2)])
    w["l0_wo"] = np.stack([_timg(w["l0_wo"][l]) for l in range(2)])
    w["l0_w1re"] = np.stack([_timg(w["l0_w1re"][l]) for l in range(2)])
    w["l0_w2p"] = np.stack([_timg(w["l0_w2p"][l]) for l in range(2)])
    w["l1_wqkv"] = _timg(w["l1_wqkv"])
    w["l1_wout"] = _timg(w["l1_wout"])
    w["ew_T"] = np.stack([_timg(ew[e]) for e in range(8)])
    return w



class KB:
    EKEYS = ("T", "V", "S", "P", "Y")

    def __init__(self, nc):
        self.nc = nc
        self.q = {k: [] for k in self.EKEYS}
        self.c = dict(T=0, V=0, S=0, P=0, dW0=0, dW1=0, dY0=0, dY1=0, dZ0=0, dZ1=0, dT0=0, dT1=0, dP0=0, dP1=0)
        self.sem = {}

    def dma_begin(self, ek, dk):
        self.wait(ek, dk, self.c[dk])

    def raw(self, ek, fn):
        self.q[ek].append(fn)

    def wait(self, ek, sk, val):
        if val <= 0:
            return
        self.q[ek].append(lambda e, sk=sk, v=val: e.wait_ge(self.sem[sk], v))

    def op(self, ek, fn):
        self.c[ek] += 1
        def run(e, fn=fn, ek=ek):
            fn(e).then_inc(self.sem[ek], 1)
        self.q[ek].append(run)
        return self.c[ek]

    def sop(self, ek, fn):
        self.q[ek].append(lambda e, fn=fn: fn(e))
        return self.c[ek]

    def dma(self, ek, dk, fn):
        self.c[dk] += 16
        def run(e, fn=fn, dk=dk):
            fn(e).then_inc(self.sem[dk], 16)
        self.q[ek].append(run)
        return self.c[dk]

    def snap(self):
        return dict(self.c)

    def barrier_from(self, ek, snap, keys=None):
        for k in (keys or ("T", "V", "S", "P", "dW0", "dW1", "dY0", "dY1", "dZ0", "dZ1", "dT0", "dT1", "dP0", "dP1")):
            self.wait(ek, k, snap[k])

    def full_barrier(self, keys=("T", "V", "S", "P", "Y")):
        s = self.snap()
        for ek in keys:
            self.barrier_from(ek, s)
        return s



def softmax_row(kb, E, psrow, width, pout, exp_scale_key=None, skip_max=False):
    nmx = E["NMX"]
    nms = E["NMS"]
    sms = E["SMS"]
    rcp = E["RCP"]
    if skip_max:
        kb.wait("S", "V", kb.c["V"])
        nS = kb.op("S", lambda s: s.activation(
            out=pout, in_=psrow, func=AF.Exp, accum_out=sms[:, 0:1]))
        kb.wait("V", "S", nS)
        nV = kb.op("V", lambda v: v.reciprocal(out=rcp[:, :], in_=sms[:, :]))
        kb.wait("V", "V", nV)
        nV = kb.op("V", lambda v: v.tensor_scalar(out=pout, in0=pout, scalar1=rcp[:, 0:1],
                                                  scalar2=None, op0=ALU.mult))
        return nV
    nV = kb.op("V", lambda v: v.reduce_max(out=nmx[:, 0:1], in_=psrow, axis=AX.X, negate=True))
    kb.wait("S", "V", nV)
    if exp_scale_key is not None:
        scale = E["consts_f"][exp_scale_key]
        nS = kb.op("S", lambda s, sc=scale: s.mul(out=nms[:, 0:1], in_=nmx[:, 0:1], mul=sc))
        kb.wait("S", "S", nS)
        bias_ap = nms[:, 0:1]
    else:
        scale = 1.0
        bias_ap = nmx[:, 0:1]
    nS = kb.op("S", lambda s, b=bias_ap, sc=scale: s.activation(
        out=pout, in_=psrow, func=AF.Exp, bias=b, scale=sc, accum_out=sms[:, 0:1]))
    kb.wait("V", "S", nS)
    nV = kb.op("V", lambda v: v.reciprocal(out=rcp[:, 0:1], in_=sms[:, 0:1]))
    kb.wait("V", "V", nV)
    nV = kb.op("V", lambda v: v.tensor_scalar(out=pout, in0=pout, scalar1=rcp[:, 0:1],
                                              scalar2=None, op0=ALU.mult))
    return nV


def ln_stats_apply(kb, E, xslices, width, pp=0, gb_cols=None, bf=False, defer=False):
    SCB = E["SCB"]
    SQB = E["SQB"]
    SR = E["SR"]
    PS_S = E["PS_S"]
    ONEC = E["ONEC"]
    ONE1F = E["ONE1F"]
    bA = PS_S[:, 2 * pp * width:(2 * pp + 1) * width]
    bB = PS_S[:, (2 * pp + 1) * width:(2 * pp + 2) * width]
    st = E.setdefault("_lnst", {"apply_v": [0, 0], "bcast_t": [0, 0]})
    tdone = []
    for ft in range(8):
        h = (ft % 2) * width
        if ft >= 2:
            kb.wait("S", "T", tdone[ft - 2])
        if not bf:
            kb.op("S", lambda s, ft=ft, h=h: s.copy(out=SCB[:, h:h + width], in_=xslices[ft]))
            vsrc = SCB[:, h:h + width]
        else:
            vsrc = xslices[ft]
        nS2 = kb.op("S", lambda s, ft=ft, h=h: s.activation(
            out=SQB[:, h:h + width], in_=xslices[ft], func=AF.Square))
        kb.wait("T", "S", nS2)
        if ft == 0:
            kb.wait("T", "V", st["apply_v"][pp])
        kb.op("T", lambda t, ft=ft, vsrc=vsrc: t.matmul(
            bA[0:1, 0:width], ONEC[:, :], vsrc, start=(ft == 0), stop=(ft == 7)))
        n2 = kb.op("T", lambda t, ft=ft, h=h: t.matmul(
            bB[0:1, 0:width], ONEC[:, :], SQB[:, h:h + width], start=(ft == 0), stop=(ft == 7)))
        tdone.append(n2)
    mu = SR[0:1, pp * 1024:pp * 1024 + width]
    tmp = SR[0:1, pp * 1024 + 512:pp * 1024 + 512 + width]
    kb.wait("S", "T", tdone[-1])
    kb.wait("S", "T", st["bcast_t"][pp])
    s_mu = kb.op("S", lambda s: s.activation(out=mu, in_=bA[0:1, 0:width],
                                             func=AF.Copy, scale=E["consts_f"]["inv1024"]))
    kb.wait("S", "S", s_mu)
    s_m2 = kb.op("S", lambda s: s.activation(out=tmp, in_=mu, func=AF.Square))
    kb.wait("V", "S", s_m2)
    v_var = kb.op("V", lambda v: v.scalar_tensor_tensor(
        out=tmp, in0=bB[0:1, 0:width], scalar=E["consts_f"]["inv1024"],
        in1=tmp, op0=ALU.mult, op1=ALU.subtract))
    kb.wait("V", "V", v_var)
    v_ve = kb.op("V", lambda v: v.tensor_scalar(out=tmp, in0=tmp, scalar1=1e-5,
                                                scalar2=None, op0=ALU.add))
    kb.wait("V", "V", v_ve)
    v_rc = kb.op("V", lambda v: v.reciprocal(out=tmp, in_=tmp))
    kb.wait("S", "V", v_rc)
    s_rs = kb.op("S", lambda s: s.activation(out=tmp, in_=tmp, func=AF.Sqrt))

    def finish():
        kb.wait("T", "S", s_rs)
        kb.op("T", lambda t: t.matmul(bA, ONE1F[:, :], mu, start=True, stop=True))
        brs = kb.op("T", lambda t: t.matmul(bB, ONE1F[:, :], tmp, start=True, stop=True))
        st["bcast_t"][pp] = brs
        kb.wait("V", "T", brs)
        nsub = 0
        for ft in range(8):
            nsub = kb.op("V", lambda v, ft=ft: v.tensor_tensor(out=xslices[ft], in0=xslices[ft],
                                                               in1=bA, op=ALU.subtract))
        kb.wait("V", "V", nsub)
        nV = 0
        for ft in range(8):
            nV = kb.op("V", lambda v, ft=ft: v.tensor_tensor(out=xslices[ft], in0=xslices[ft],
                                                             in1=bB, op=ALU.mult))
        if gb_cols is not None:
            kb.wait("V", "V", nV)
            for ft in range(8):
                gcols, bcols = gb_cols
                nV = kb.op("V", lambda v, ft=ft, gcols=gcols, bcols=bcols: v.tensor_scalar(
                    out=xslices[ft], in0=xslices[ft], scalar1=gcols[ft], scalar2=bcols[ft],
                    op0=ALU.mult, op1=ALU.add))
        st["apply_v"][pp] = nV
        return nV

    if defer:
        return finish
    return finish()


def _load_xt(kb, E, b, also_abf=True, also_x=True):
    X, ABF, XR, PS_S = E["X"], E["ABF"], E["XR"], E["PS_S"]
    x_dram = E["x"]
    vdone = [0] * QT
    sdone = [0] * QT
    for tt in range(QT):
        dk = "dY%d" % (tt % 2)
        if tt >= 2:
            kb.wait("Y", "V", vdone[tt - 2])
            kb.wait("Y", "S", sdone[tt - 2])
        kb.dma_begin("Y", dk)
        n = kb.dma("Y", dk, lambda sy, tt=tt: sy.dma_start(
            XR[tt % 2][:, :], x_dram[b, tt * 128:(tt + 1) * 128, :]))
        kb.wait("T", dk, n)
        if tt >= 2:
            kb.wait("T", "V", vdone[tt - 2])
            kb.wait("T", "S", sdone[tt - 2])
        ttr = 0
        for ft in range(8):
            ttr = kb.op("T", lambda t, tt=tt, ft=ft: t.transpose(
                PS_S[:, ft * 128 + (tt % 2) * 1024: ft * 128 + (tt % 2) * 1024 + 128],
                XR[tt % 2][:, ft * 128:(ft + 1) * 128], E["IDF"][:, :]))
        kb.wait("V", "T", ttr)
        kb.wait("S", "T", ttr)
        for ft in range(8):
            src = PS_S[:, ft * 128 + (tt % 2) * 1024: ft * 128 + (tt % 2) * 1024 + 128]
            if ft < 4:
                if also_x:
                    vdone[tt] = kb.op("V", lambda v, ft=ft, tt=tt, src=src: v.tensor_copy(
                        out=X[ft][:, tt * 128:(tt + 1) * 128], in_=src))
                if also_abf:
                    vdone[tt] = kb.op("V", lambda v, ft=ft, tt=tt, src=src: v.tensor_copy(
                        out=ABF[ft][:, tt * 128:(tt + 1) * 128], in_=src))
            else:
                if also_x:
                    sdone[tt] = kb.op("S", lambda s, ft=ft, tt=tt, src=src: s.copy(
                        out=X[ft][:, tt * 128:(tt + 1) * 128], in_=src))
                if also_abf:
                    sdone[tt] = kb.op("S", lambda s, ft=ft, tt=tt, src=src: s.copy(
                        out=ABF[ft][:, tt * 128:(tt + 1) * 128], in_=src))
    kb.full_barrier()


def _wtile_ap(base_ap, j0, rows, ntile=1):
    return bass.AP(base_ap.tensor, base_ap.offset + j0 * 128 * rows,
                   [[rows, 128], [128 * rows, ntile], [1, rows]])


def _stream_w_cols(kb, E, base_ap, col0, rows, ctot, half, dest=None, ncols=128):
    re = _wtile_ap(base_ap, col0 // 128, rows)
    if dest is None:
        dest = E["WB"][:, half * 1024:half * 1024 + rows]
    return kb.dma("Y", "dZ%d" % half, lambda sy, re=re, dest=dest: sy.dma_start(dest, re))


def _wstream_begin(kb, half):
    kb.dma_begin("Y", "dZ%d" % half)



def _rope_pair(kb, E, ps, p, tblC, tblS, out_ap, swap_lhsT):
    SCB = E["SCB"]
    PBF = E["PBF"]
    PS_M = E["PS_M"]
    o = p * CH
    nS = kb.op("S", lambda s: s.copy(out=SCB[:, o:o + CH], in_=ps))
    kb.wait("T", "S", nS)
    nJ = kb.op("T", lambda t: t.matmul(PS_M[:, p * CH:(p + 1) * CH], swap_lhsT, SCB[:, o:o + CH],
                                       start=True, stop=True))
    kb.wait("V", "S", nS)
    nV = kb.op("V", lambda v: v.tensor_tensor(out=PBF[:, o:o + CH], in0=ps, in1=tblC,
                                              op=ALU.mult))
    kb.wait("V", "T", nJ)
    kb.wait("V", "V", nV)
    nV = kb.op("V", lambda v: v.tensor_tensor(out=SCB[:, o:o + CH], in0=PS_M[:, p * CH:(p + 1) * CH],
                                              in1=tblS, op=ALU.mult))
    kb.wait("V", "V", nV)
    nV = kb.op("V", lambda v: v.tensor_tensor(out=out_ap, in0=SCB[:, o:o + CH],
                                              in1=PBF[:, o:o + CH], op=ALU.add))
    return nV, nJ, nS


def ph_l0a(kb, E, b):
    X, ABF, KT, VT, QJ, WB = E["X"], E["ABF"], E["KT"], E["VT"], E["QJ"], E["WB"]
    PBF, PT = E["PBF"], E["PT"]
    TB0, TB1 = E["TB0"], E["TB1"]
    PS_S, PS_T, PS_M = E["PS_S"], E["PS_T"], E["PS_M"]

    kb.full_barrier()
    if b == 0:
        fa = E.get("pc_fenceA")
        if fa:
            kb.wait("Y", "dP0", fa[0])
            kb.wait("Y", "dP1", fa[1])
    _load_xt(kb, E, b)
    if L0A_STAGES < 2:
        return

    rope_v = [0, 0]
    rope_s = [0, 0]
    wb_mm = [0, 0]
    for ft in range(8):
        half = ft % 2
        dk = "dZ%d" % half
        kb.wait("Y", "T", wb_mm[half])
        _wstream_begin(kb, half)
        nW = _stream_w_cols(kb, E, E["wkbig"], ft * 128, D, D, half)
        kb.wait("T", dk, nW)
        for c in range(NCH):
            p = c % 2
            tk = "dT%d" % p
            kb.wait("Y", "V", rope_v[p])
            kb.dma_begin("Y", tk)
            kb.dma("Y", tk, lambda sy, ft=ft, c=c, p=p: sy.dma_start(
                TB0[:, p * CH:(p + 1) * CH],
                E["ropeCb"][ft * 128:(ft + 1) * 128, c * CH:(c + 1) * CH]))
            nTb = kb.dma("Y", tk, lambda sy, ft=ft, c=c, p=p: sy.dma_start(
                TB1[:, p * CH:(p + 1) * CH],
                E["ropeSb"][ft * 128:(ft + 1) * 128, c * CH:(c + 1) * CH]))
            ps = PS_S[:, p * CH:(p + 1) * CH]
            kb.wait("T", "V", rope_v[p])
            kb.wait("T", "S", rope_s[p])
            nT = 0
            for kt in range(8):
                nT = kb.op("T", lambda t, kt=kt, c=c, half=half, ps=ps: t.matmul(
                    ps, WB[:, half * 1024 + kt * 128: half * 1024 + kt * 128 + 128],
                    ABF[kt][:, c * CH:(c + 1) * CH], start=(kt == 0), stop=(kt == 7)))
            wb_mm[half] = nT
            kb.wait("S", "T", nT)
            kb.wait("S", "V", rope_v[p])
            kb.wait("V", tk, nTb)
            nV, nJ, nSc = _rope_pair(kb, E, ps, p, TB0[:, p * CH:(p + 1) * CH],
                                     TB1[:, p * CH:(p + 1) * CH],
                                     KT[ft][:, c * CH:(c + 1) * CH], E["JTBb"][:, :])
            rope_v[p] = nV
            rope_s[p] = nSc
    kb.full_barrier()

    wb_mm = [0, 0]
    ps_ev = [0, 0]
    for ft in range(8):
        half = ft % 2
        dk = "dZ%d" % half
        kb.wait("Y", "T", wb_mm[half])
        _wstream_begin(kb, half)
        nW = _stream_w_cols(kb, E, E["wvbig"], ft * 128, D, D, half)
        kb.wait("T", dk, nW)
        for tt4 in range(4):
            pb = tt4 % 2
            ps = PS_S[:, pb * CH:pb * CH + CH]
            kb.wait("T", "V", ps_ev[pb])
            nT = 0
            for sub in range(4):
                tt = tt4 * 4 + sub
                for kt in range(8):
                    nT = kb.op("T", lambda t, kt=kt, tt=tt, sub=sub, half=half, ps=ps: t.matmul(
                        ps[:, sub * 128:(sub + 1) * 128],
                        ABF[kt][:, tt * 128:(tt + 1) * 128],
                        WB[:, half * 1024 + kt * 128: half * 1024 + kt * 128 + 128],
                        start=(kt == 0), stop=(kt == 7)))
            wb_mm[half] = nT
            kb.wait("V", "T", nT)
            nE = 0
            for sub in range(4):
                tt = tt4 * 4 + sub
                nE = kb.op("V", lambda v, tt=tt, sub=sub, ft=ft, ps=ps: v.tensor_copy(
                    out=VT[tt][:, ft * 128:(ft + 1) * 128], in_=ps[:, sub * 128:(sub + 1) * 128]))
            ps_ev[pb] = nE
    kb.full_barrier()
    if L0A_STAGES < 4:
        return

    for g in range(4):
        jbmax = 4 * (g + 1)
        rope_v = [0, 0]
        rope_s = [0, 0]
        for ft in range(8):
            half = ft % 2
            p = half
            dk = "dZ%d" % half
            kb.wait("Y", "V", rope_v[p])
            _wstream_begin(kb, half)
            nW = _stream_w_cols(kb, E, E["wqbig"], ft * 128, D, D, half)
            nC = kb.dma("Y", dk, lambda sy, ft=ft, g=g, p=p: sy.dma_start(
                TB0[:, p * CH:(p + 1) * CH],
                E["ropeCb"][ft * 128:(ft + 1) * 128, g * CH:(g + 1) * CH]))
            nS_ = kb.dma("Y", dk, lambda sy, ft=ft, g=g, p=p: sy.dma_start(
                TB1[:, p * CH:(p + 1) * CH],
                E["ropeSb"][ft * 128:(ft + 1) * 128, g * CH:(g + 1) * CH]))
            kb.wait("T", dk, nS_)
            kb.wait("T", "V", rope_v[p])
            kb.wait("T", "S", rope_s[p])
            ps = PS_S[:, p * CH:(p + 1) * CH]
            nT = 0
            for kt in range(8):
                nT = kb.op("T", lambda t, kt=kt, g=g, half=half, ps=ps: t.matmul(
                    ps, WB[:, half * 1024 + kt * 128: half * 1024 + kt * 128 + 128],
                    ABF[kt][:, g * CH:(g + 1) * CH], start=(kt == 0), stop=(kt == 7)))
            kb.wait("S", "T", nT)
            kb.wait("S", "V", rope_v[p])
            nV, nJ, nSc = _rope_pair(kb, E, ps, p, TB0[:, p * CH:(p + 1) * CH],
                                     TB1[:, p * CH:(p + 1) * CH], QJ[ft], E["JTBb"][:, :])
            rope_v[p] = nV
            rope_s[p] = nSc
        kb.full_barrier()
        if L0A_S4 < 2:
            continue

        for iq in range(4 * g, 4 * g + 4):
            jext = (iq + 1) * 128
            nchq = (jext + CH - 1) // CH
            kb.wait("T", "S", kb.c["S"])
            kb.wait("T", "V", kb.c["V"])
            nT = 0
            chunks = []
            c0 = 0
            while c0 < jext - 128:
                w_ = min(CH, jext - 128 - c0)
                chunks.append((c0, w_, False))
                c0 += w_
            chunks.append((jext - 128, 128, True))
            for (c0, wdt, diag) in chunks:
                for dt in range(8):
                    nT = kb.op("T", lambda t, c0=c0, dt=dt, iq=iq, g=g, wdt=wdt, diag=diag: t.matmul(
                        PS_S[:, c0:c0 + wdt],
                        QJ[dt][:, (iq - 4 * g) * 128:(iq - 4 * g) * 128 + 128],
                        KT[dt][:, c0 // 128 * 128:c0 + wdt] if False else KT[dt][:, c0:c0 + wdt],
                        start=(dt == 0), stop=(dt == 7 and not diag)))
                if diag:
                    nT = kb.op("T", lambda t, c0=c0: t.matmul(
                        PS_S[:, c0:c0 + 128], E["M4B"][:, :], E["IDB"][:, :],
                        start=False, stop=True))
            kb.wait("V", "T", nT)
            if L0A_S4 < 3:
                continue
            nV = softmax_row(kb, E, PS_S[:, 0:jext], jext, PBF[:, 0:jext], exp_scale_key="inv32")
            if L0A_S4 < 4:
                continue
            kb.wait("T", "V", nV)
            kb.wait("T", "S", kb.c["S"])
            scopy = [0, 0]
            ngrp = (iq + 1 + 3) // 4
            for gb in range(ngrp):
                bank = gb % 2
                j0 = gb * 4
                j1 = min(iq + 1, j0 + 4)
                if gb >= 2:
                    kb.wait("T", "S", scopy[bank])
                nTt = 0
                for jb in range(j0, j1):
                    nTt = kb.op("T", lambda t, jb=jb, bank=bank, j0=j0: t.transpose(
                        PS_T[:, bank * 1024 + (jb - j0) * 128: bank * 1024 + (jb - j0) * 128 + 128],
                        PBF[:, jb * 128:(jb + 1) * 128], E["IDB"][:, :]))
                kb.wait("S", "T", nTt)
                scopy[bank] = kb.op("S", lambda s, j0=j0, j1=j1, bank=bank: s.copy(
                    out=PT[:, j0 * 128:j1 * 128],
                    in_=PS_T[:, bank * 1024: bank * 1024 + (j1 - j0) * 128]))
            if L0A_S4 < 5:
                continue
            kb.wait("T", "S", kb.c["S"])
            aav = []
            for ft in range(8):
                ps = PS_S[:, ft * 128:(ft + 1) * 128]
                nT = 0
                for jb in range(iq + 1):
                    nT = kb.op("T", lambda t, jb=jb, ft=ft, iq=iq, ps=ps: t.matmul(
                        ps, VT[jb][:, ft * 128:(ft + 1) * 128], PT[:, jb * 128:(jb + 1) * 128],
                        start=(jb == 0), stop=(jb == iq)))
                aav.append(nT)
            kb.wait("V", "T", aav[-1])
            for ft in range(8):
                kb.op("V", lambda v, ft=ft, iq=iq: v.tensor_tensor(
                    out=X[ft][:, iq * 128:(iq + 1) * 128], in0=X[ft][:, iq * 128:(iq + 1) * 128],
                    in1=PS_S[:, ft * 128:(ft + 1) * 128], op=ALU.add))
        kb.full_barrier()
        if L0A_S4 < 6:
            continue
        xs = [X[ft][:, g * CH:(g + 1) * CH] for ft in range(8)]
        ln_stats_apply(kb, E, xs, CH, pp=g % 2)
        kb.full_barrier()


def ph_store(kb, E, b):
    X = E["X"]
    PS_S = E["PS_S"]
    XR = E["XR"]
    y = E["y"]
    kb.full_barrier()
    for tt in range(QT):
        nT = 0
        buf = tt % 2
        dk = "dY%d" % buf
        if tt >= 2:
            kb.wait("T", dk, kb.c[dk])
        for ft in range(8):
            nT = kb.op("T", lambda t, tt=tt, ft=ft, buf=buf: t.transpose(
                PS_S[:, ft * 128 + buf * 1024: ft * 128 + buf * 1024 + 128],
                X[ft][:, tt * 128:(tt + 1) * 128], E["IDF"][:, :]))
        kb.wait("V", "T", nT)
        kb.wait("S", "T", nT)
        nV = kb.op("V", lambda v, buf=buf: v.tensor_copy(
            out=XR[buf][:, 0:CH], in_=PS_S[:, buf * 1024:buf * 1024 + CH]))
        nS = kb.op("S", lambda s, buf=buf: s.copy(
            out=XR[buf][:, CH:1024], in_=PS_S[:, buf * 1024 + CH:buf * 1024 + 1024]))
        kb.wait("Y", "V", nV)
        kb.wait("Y", "S", nS)
        kb.dma_begin("Y", dk)
        kb.dma("Y", dk, lambda sy, tt=tt, buf=buf: sy.dma_start(
            y[b, tt * 128:(tt + 1) * 128, :], XR[buf][:, :]))
    kb.full_barrier()


def ph_load(kb, E, b):
    kb.full_barrier()
    _load_xt(kb, E, b)



PH_FUNCS = {}

def build_core_program(phases):
    nc = bass.Bass("TRN2", target_bir_lowering=False)
    C = _consts()

    dr = {}
    dr["x"] = nc.dram_tensor("x", [2, T, D], F32, kind="ExternalInput").ap()
    dr["y"] = nc.dram_tensor("y", [2, T, D], F32, kind="ExternalOutput").ap()
    for nm, shp in [("wqbig", [8, 128, D]), ("wkbig", [8, 128, D]), ("wvbig", [8, 128, D]),
                    ("ropeC", [D, T]), ("ropeS", [D, T]),
                    ("ident", [128, 128]), ("JTB", [128, 128]), ("RTB", [128, 128]),
                    ("M0", [128, 128]), ("M4", [128, 128]),
                    ("HS", [128, 2]), ("HST", [2, 128]), ("SELS", [8, 1024]),
                    ("xposQC", [128, T]), ("xposQS", [128, T]),
                    ("xposKC", [128, T]), ("xposKS", [128, T]),
                    ("l0_wqkv", [2, 12, 128, D]), ("l0_wo", [2, 8, 128, 512]),
                    ("l0_w1re", [2, 44, 128, D]), ("l0_w2p", [2, 8, 128, HIDP]),
                    ("l0_b1c", [128, 4 * NHT]), ("l0_b2c", [128, 16]),
                    ("l0_qksc", [128, 8]), ("l0_lnc", [128, 32]),
                    ("l1_wqkv", [24, 128, D]), ("l1_wout", [8, 128, D]),
                    ("l1_bqc", [128, 24]), ("l1_boc", [128, 8]),
                    ("gate_wT", [D, 8]), ("ew_T", [8, 8, 128, D])]:
        dr[nm] = nc.dram_tensor(nm, shp, F32, kind="ExternalInput").ap()
    dr["xscr"] = nc.dram_tensor("xscr", [8, 128, T], F32).ap()
    MIR = [("wkbig", [8, 128, D]), ("ropeC", [D, T]), ("ropeS", [D, T]),
           ("wvbig", [8, 128, D]), ("wqbig", [8, 128, D]),
           ("l0_wqkv", [2, 12, 128, D]), ("l0_wo", [2, 8, 128, 512]),
           ("l0_w1re", [2, 44, 128, D]), ("l0_w2p", [2, 8, 128, HIDP]),
           ("l1_wqkv", [24, 128, D]), ("l1_wout", [8, 128, D]), ("ew_T", [8, 8, 128, D])]
    mir = {}
    for nm, shp in MIR:
        mir[nm] = nc.dram_tensor(nm + "_bf", shp, BF16).ap()

    ctx = ExitStack()
    sb = lambda n, s, d: ctx.enter_context(nc.sbuf_tensor(n, s, d))
    psm = lambda n, s, d: ctx.enter_context(nc.psum_tensor(n, s, d))

    E = dict(nc=nc)
    E.update(dr)

    XU = sb("XU", [128, 8 * T], F32)
    ABFU = sb("ABFU", [128, 8 * T], BF16)
    KTU = sb("KTU", [128, 8 * T], BF16)
    VTU = sb("VTU", [128, 16 * D], BF16)
    TB0 = sb("TB0", [128, T], BF16)
    TB1 = sb("TB1", [128, T], BF16)
    WBt = sb("WBt", [128, 2048], BF16)
    QJt = sb("QJt", [128, 4096], BF16)
    PBFt = sb("PBFt", [128, T], BF16)
    PTt = sb("PTt", [128, T], BF16)
    SCBt = sb("SCBt", [128, 1024], BF16)
    SQBt = sb("SQBt", [128, 1024], BF16)
    FSCR = sb("FSCR", [128, T], F32)
    OEt = FSCR.bitcast(BF16)

    XUa, ABFa, KTa, VTa = XU.ap(), ABFU.ap(), KTU.ap(), VTU.ap()
    E["X"] = [XUa[:, i * T:(i + 1) * T] for i in range(8)]
    E["ABF"] = [ABFa[:, i * T:(i + 1) * T] for i in range(8)]
    E["KT"] = [KTa[:, i * T:(i + 1) * T] for i in range(8)]
    E["VT"] = [VTa[:, i * D:(i + 1) * D] for i in range(QT)]
    E["KTU"] = KTa
    E["ABFU"] = ABFa
    E["VTU"] = VTa
    E["TB0"], E["TB1"] = TB0.ap(), TB1.ap()
    E["TSF"] = SQBt.bitcast(F32).ap()
    E["WB"] = WBt.ap()
    qa = QJt.ap()
    E["QJ"] = [qa[:, i * CH:(i + 1) * CH] for i in range(8)]
    E["QJU"] = qa
    E["PBF"] = PBFt.ap()
    E["PT"] = PTt.ap()
    oa = OEt.ap()
    E["OE"] = [oa[:, i * CH:(i + 1) * CH] for i in range(8)]
    E["OEU"] = oa
    E["SCB"] = SCBt.ap()
    E["SQB"] = SQBt.ap()
    fa = FSCR.ap()
    E["XR"] = [fa[:, 0:1024], fa[:, 1024:2048]]
    E["SR"] = fa

    for nm, shp, d in [("IDF", [128, 128], F32), ("IDB", [128, 128], BF16),
                       ("JTBb", [128, 128], BF16), ("RTBb", [128, 128], BF16),
                       ("M0c", [128, 128], F32), ("M4c", [128, 128], F32),
                       ("HSb", [128, 2], BF16), ("HSTb", [2, 128], BF16),
                       ("SELSb", [8, 1024], BF16),
                       ("M0B", [128, 128], BF16), ("M4B", [128, 128], BF16),
                       ("BVB", [128, 8], BF16), ("BOC2", [128, 8], F32),
                       ("ONEC", [128, 1], BF16), ("ONE1F", [1, 128], F32),
                       ("ONEB", [1, 128], BF16),
                       ("QKSC", [128, 8], F32), ("LNC", [128, 32], F32),
                       ("B1C", [128, 4 * NHT], F32), ("B2C", [128, 16], F32),
                       ("BQC", [128, 24], F32), ("BOC", [128, 8], F32),
                       ("NMX", [128, 2], F32), ("NMS", [128, 2], F32),
                       ("SMS", [128, 2], F32), ("RCP", [128, 2], F32),
                       ("CP", [128, 4], F32)]:
        E[nm] = sb(nm, shp, d).ap()
    E["M0"], E["M4"] = E["M0c"], E["M4c"]

    E["PS_S"] = psm("PS_S", [128, 2048], F32).ap()
    _pst = psm("PS_T", [128, 2048], BF16)
    E["PS_T"] = _pst.ap()
    E["PS_TF"] = _pst.bitcast(F32).ap()
    E["PS_M"] = psm("PS_M", [128, 1024], F32).ap()

    E["consts_f"] = {"inv32": 1.0 / 32.0, "inv1024": 1.0 / 1024.0}
    nc.const_aps.register(F32, 1.0 / 32.0, E["CP"][:, 0:1])
    nc.const_aps.register(F32, 1.0 / 1024.0, E["CP"][:, 1:2])

    sems = {}
    for k in ("T", "V", "S", "P", "dW0", "dW1", "dY0", "dY1", "dZ0", "dZ1", "dT0", "dT1", "dP0", "dP1"):
        sems[k] = ctx.enter_context(nc.semaphore(f"sem{k}"))
    blk = ctx.enter_context(nc.Block())

    kb = KB(nc)
    kb.sem = sems

    nconst = 0
    for dst, src in [(E["IDF"], dr["ident"]), (E["M0c"], dr["M0"]), (E["M4c"], dr["M4"]),
                     (E["QKSC"], dr["l0_qksc"]), (E["LNC"], dr["l0_lnc"]),
                     (E["B1C"], dr["l0_b1c"]), (E["B2C"], dr["l0_b2c"]),
                     (E["BQC"], dr["l1_bqc"]), (E["BOC"], dr["l1_boc"])]:
        nconst = kb.dma("Y", "dY0", lambda sy, dst=dst, src=src: sy.dma_start(dst[:, :], src))
    nconst2 = 0
    for dst, src in [(E["IDB"], dr["ident"]), (E["JTBb"], dr["JTB"]), (E["RTBb"], dr["RTB"]),
                     (E["HSb"], dr["HS"]), (E["HSTb"], dr["HST"]), (E["SELSb"], dr["SELS"]),
                     (E["M0B"], dr["M0"]), (E["M4B"], dr["M4"])]:
        nconst2 = kb.dma("P", "dW0", lambda g, dst=dst, src=src: g.dma_start(dst[:, :], src))
    kb.sop("P", lambda g: g.memset(E["ONEC"][:, :], 1.0))
    kb.sop("P", lambda g: g.memset(E["ONEB"][:, :], 1.0))
    kb.sop("P", lambda g: g.memset(E["ONE1F"][:, :], 1.0))
    kb.sop("P", lambda g: g.memset(E["CP"][:, 0:1], 1.0 / 32.0))
    nP = kb.op("P", lambda g: g.memset(E["CP"][:, 1:2], 1.0 / 1024.0))
    for ek in ("T", "V", "S"):
        kb.wait(ek, "dY0", nconst)
        kb.wait(ek, "dW0", nconst2)
        kb.wait(ek, "P", nP)
    kb.wait("Y", "dY0", nconst)
    kb.wait("P", "dW0", nconst2)


    KTUa = E["KTU"]
    pc_half = [0, 0]
    hcur = 0
    pc_fence = {}
    for nm, shp in MIR:
        nel = 1
        for s in shp:
            nel *= s
        ctot = nel // 128
        c0 = 0
        while c0 < ctot:
            w = min(8192, ctot - c0)
            h = hcur
            hcur ^= 1
            kb.wait("P", "dP%d" % h, pc_half[h])
            kb.dma_begin("P", "dW%d" % h)
            fin = bass.AP(dr[nm].tensor, c0, [[ctot, 128], [1, w]])
            nin = kb.dma("P", "dW%d" % h, lambda g, fin=fin, h=h, w=w: g.dma_start(
                KTUa[:, h * 8192:h * 8192 + w], fin))
            kb.wait("P", "dW%d" % h, nin)
            fout = bass.AP(mir[nm].tensor, c0, [[ctot, 128], [1, w]])
            pc_half[h] = kb.dma("P", "dP%d" % h, lambda g, fout=fout, h=h, w=w: g.dma_start(
                fout, KTUa[:, h * 8192:h * 8192 + w]))
            c0 += w
        pc_fence[nm] = (kb.c["dP0"], kb.c["dP1"])
    E["pc_fenceA"] = pc_fence["wqbig"]
    E["pc_fenceB"] = pc_fence[MIR[-1][0]]
    for nm, shp in MIR:
        E[nm if nm not in ("ropeC", "ropeS") else nm + "b"] = mir[nm]

    for b in range(2):
        for pi, ph in enumerate(phases):
            if pi == 0 and ph != "l0a":
                ph_load(kb, E, b)
            PH_FUNCS[ph](kb, E, b)
        ph_store(kb, E, b)

    kb.wait("Y", "dY0", kb.c["dY0"])
    kb.wait("Y", "dY1", kb.c["dY1"])

    def mk(ek):
        def sec(eng):
            for fn in kb.q[ek]:
                fn(eng)
        return sec
    blk.tensor(mk("T"))
    blk.vector(mk("V"))
    blk.scalar(mk("S"))
    blk.gpsimd(mk("P"))
    blk.sync(mk("Y"))

    ctx.close()
    return nc


PH_FUNCS["l0a"] = ph_l0a
PH_FUNCS["nop"] = lambda kb, E, b: None


L0A_STAGES = 99
L0A_S4 = 99





def _cast_shift_abf(kb, E, shift):
    X, ABF = E["X"], E["ABF"]
    for ft in range(8):
        ek = "V" if ft < 4 else "S"
        if shift and ft >= 4:
            nz = kb.op("V", lambda v, ft=ft: v.memset(ABF[ft][:, 0:2], 0.0))
            if ek == "V":
                kb.wait("V", "V", nz)
                kb.op("V", lambda v, ft=ft: v.tensor_copy(out=ABF[ft][:, 1:T],
                                                          in_=X[ft][:, 0:T - 1]))
            else:
                kb.wait("S", "V", nz)
                kb.op("S", lambda s, ft=ft: s.copy(out=ABF[ft][:, 1:T], in_=X[ft][:, 0:T - 1]))
        else:
            if ek == "V":
                kb.op("V", lambda v, ft=ft: v.tensor_copy(out=ABF[ft][:, :], in_=X[ft][:, :]))
            else:
                kb.op("S", lambda s, ft=ft: s.copy(out=ABF[ft][:, :], in_=X[ft][:, :]))
    kb.full_barrier()


def _fold_xpos(kb, E, cos_dram, sin_dram, scol, rcol):
    TSF, TB0, TB1 = E["TSF"], E["TB0"], E["TB1"]
    QKSC = E["QKSC"]
    for c in range(NCH):
        for (dram, tb, col) in ((cos_dram, TB0, scol), (sin_dram, TB1, rcol)):
            kb.wait("Y", "V", kb.c["V"])
            kb.dma_begin("Y", "dY0")
            n = kb.dma("Y", "dY0", lambda sy, dram=dram, c=c: sy.dma_start(
                TSF[:, :], dram[:, c * CH:(c + 1) * CH]))
            kb.wait("V", "dY0", n)
            kb.op("V", lambda v, tb=tb, c=c, col=col: v.tensor_scalar(
                out=tb[:, c * CH:(c + 1) * CH], in0=TSF[:, :], scalar1=QKSC[:, col:col + 1],
                scalar2=None, op0=ALU.mult))
    kb.full_barrier()


def _l0f_qk_tiles(kb, E, l, which):
    X, ABF, KT, WB = E["X"], E["ABF"], E["KT"], E["WB"]
    TB0, TB1 = E["TB0"], E["TB1"]
    SCB, SQB, SR, PBF = E["SCB"], E["SQB"], E["SR"], E["PBF"]
    PS_S, PS_M = E["PS_S"], E["PS_M"]
    wbase = E["l0_wqkv"][l]
    units = [(ft, c) for ft in range(4) for c in range(NCH)]
    wb_mm = [0, 0]
    sq_n = {}
    out_n = {}
    qn_n = {}
    t2_n = {}
    recb_n = {}
    st_w = {}

    def emit_tailA(u):
        ft, c = units[u]
        p = u % 2
        kb.wait("T", "S", sq_n[u])
        if u >= 2 and (u - 2) in qn_n:
            kb.wait("T", "V", qn_n[u - 2])
        nrm = kb.op("T", lambda t, p=p: t.matmul(
            PS_M[0:2, p * CH:(p + 1) * CH], E["HSb"][:, :], SQB[:, p * CH:(p + 1) * CH],
            start=True, stop=True))
        kb.wait("S", "T", nrm)
        nsq = kb.op("S", lambda s, p=p: s.activation(
            out=SR[0:2, p * CH:(p + 1) * CH], in_=PS_M[0:2, p * CH:(p + 1) * CH],
            func=AF.Sqrt))
        kb.wait("V", "S", nsq)
        nv = kb.op("V", lambda v, p=p: v.tensor_scalar(
            out=SR[0:2, p * CH:(p + 1) * CH], in0=SR[0:2, p * CH:(p + 1) * CH],
            scalar1=1e-12, scalar2=None, op0=ALU.max))
        kb.wait("V", "V", nv)
        nv = kb.op("V", lambda v, p=p: v.reciprocal(
            out=SR[0:2, p * CH:(p + 1) * CH], in_=SR[0:2, p * CH:(p + 1) * CH]))
        kb.wait("V", "V", nv)
        nrc = kb.op("V", lambda v, p=p: v.tensor_copy(
            out=SQB[0:2, p * CH:p * CH + CH], in_=SR[0:2, p * CH:(p + 1) * CH]))
        recb_n[u] = nrc

    def emit_tailB(u):
        ft, c = units[u]
        p = u % 2
        kb.wait("T", "V", recb_n[u])
        nbc = kb.op("T", lambda t, p=p: t.matmul(
            PS_M[:, p * CH:(p + 1) * CH], E["HSTb"][:, :], SQB[0:2, p * CH:p * CH + CH],
            start=True, stop=True))
        kb.wait("V", "T", nbc)
        if u >= 2 and (u - 2) in t2_n:
            kb.wait("V", "V", t2_n[u - 2])
        nqn = kb.op("V", lambda v, p=p: v.tensor_tensor(
            out=PBF[:, p * CH:(p + 1) * CH], in0=SCB[:, p * CH:(p + 1) * CH],
            in1=PS_M[:, p * CH:(p + 1) * CH], op=ALU.mult))
        qn_n[u] = nqn
        kb.wait("T", "V", nqn)
        if u >= 2 and (u - 2) in t2_n:
            kb.wait("T", "V", t2_n[u - 2])
        nrot = kb.op("T", lambda t, p=p: t.matmul(
            PS_S[:, (2 + p) * CH:(3 + p) * CH], E["RTBb"][:, :], PBF[:, p * CH:(p + 1) * CH],
            start=True, stop=True))
        kb.wait("V", "V", nqn)
        nt1 = kb.op("V", lambda v, c=c, p=p: v.tensor_tensor(
            out=PBF[:, 1024 + p * CH:1024 + (p + 1) * CH], in0=PBF[:, p * CH:(p + 1) * CH],
            in1=TB0[:, c * CH:(c + 1) * CH], op=ALU.mult))
        kb.wait("V", "T", nrot)
        kb.wait("V", "V", nt1)
        nt2 = kb.op("V", lambda v, c=c, p=p: v.tensor_tensor(
            out=SQB[:, p * CH:(p + 1) * CH], in0=PS_S[:, (2 + p) * CH:(3 + p) * CH],
            in1=TB1[:, c * CH:(c + 1) * CH], op=ALU.mult))
        t2_n[u] = nt2
        kb.wait("V", "V", nt2)
        out_n[u] = kb.op("V", lambda v, ft=ft, c=c, which=which, p=p: v.tensor_tensor(
            out=KT[which * 4 + ft][:, c * CH:(c + 1) * CH],
            in0=SQB[:, p * CH:(p + 1) * CH],
            in1=PBF[:, 1024 + p * CH:1024 + (p + 1) * CH], op=ALU.add))

    for u, (ft, c) in enumerate(units):
        p = u % 2
        if c == 0:
            half = ft % 2
            kb.wait("Y", "T", wb_mm[half])
            _wstream_begin(kb, half)
            nW = _stream_w_cols(kb, E, wbase, which * 512 + ft * 128, D, 1536, half)
            st_w[ft] = (half, nW)
        half, nW = st_w[ft]
        if c == 0:
            kb.wait("T", "dZ%d" % half, nW)
        ps = PS_S[:, p * CH:(p + 1) * CH]
        if u >= 2:
            kb.wait("T", "S", sq_n[u - 2])
        nT = 0
        for kt in range(8):
            nT = kb.op("T", lambda t, kt=kt, c=c, half=half, ps=ps: t.matmul(
                ps, WB[:, half * 1024 + kt * 128: half * 1024 + kt * 128 + 128],
                ABF[kt][:, c * CH:(c + 1) * CH], start=(kt == 0), stop=(kt == 7)))
        wb_mm[half] = nT
        kb.wait("S", "T", nT)
        if u >= 2:
            emit_tailB(u - 2)
        if u >= 2 and (u - 2) in qn_n:
            kb.wait("S", "V", qn_n[u - 2])
        kb.op("S", lambda s, ps=ps, p=p: s.copy(out=SCB[:, p * CH:(p + 1) * CH], in_=ps))
        if u >= 2 and (u - 2) in out_n:
            kb.wait("S", "V", out_n[u - 2])
        sq_n[u] = kb.op("S", lambda s, ps=ps, p=p: s.activation(
            out=SQB[:, p * CH:(p + 1) * CH], in_=ps, func=AF.Square))
        if u >= 1:
            emit_tailA(u - 1)
    emit_tailA(len(units) - 1)
    emit_tailB(len(units) - 2)
    emit_tailB(len(units) - 1)
    kb.full_barrier()

def ln_stats_apply_bf(kb, E, bfslices, width, gcols, bcols, pp=0, defer=False):
    return ln_stats_apply(kb, E, bfslices, width, pp=pp, gb_cols=(gcols, bcols), bf=True,
                          defer=defer)

def ph_l0fs(kb, E, b):
    kb.full_barrier()
    if b == 0:
        fb = E.get("pc_fenceB")
        if fb:
            kb.wait("Y", "dP0", fb[0])
            kb.wait("Y", "dP1", fb[1])
    n = 0
    for ft in range(8):
        n = kb.dma("Y", "dY0", lambda sy, ft=ft: sy.dma_start(
            E["xscr"][ft, :, :], E["X"][ft][:, :]))
    kb.wait("Y", "dY0", n)
    kb.full_barrier()


def ph_l0fr(kb, E, b):
    X, XR = E["X"], E["XR"]
    kb.full_barrier()
    nv = [0, 0]
    for i, (ft, hf) in enumerate([(f, h) for f in range(8) for h in range(2)]):
        par = i % 2
        if i >= 2:
            kb.wait("Y", "V", nv[par])
        kb.dma_begin("Y", "dY%d" % par)
        n = kb.dma("Y", "dY%d" % par, lambda sy, ft=ft, hf=hf, par=par: sy.dma_start(
            XR[par][:, :], E["xscr"][ft, :, hf * 1024:(hf + 1) * 1024]))
        kb.wait("V", "dY%d" % par, n)
        nv[par] = kb.op("V", lambda v, ft=ft, hf=hf, par=par: v.tensor_tensor(
            out=X[ft][:, hf * 1024:(hf + 1) * 1024], in0=X[ft][:, hf * 1024:(hf + 1) * 1024],
            in1=XR[par][:, :], op=ALU.add))
    kb.full_barrier()
    prev = None
    for c in range(NCH):
        xs = [X[ft][:, c * CH:(c + 1) * CH] for ft in range(8)]
        f = ln_stats_apply(kb, E, xs, CH, pp=c % 2, defer=True)
        if prev is not None:
            prev()
        prev = f
    prev()
    kb.full_barrier()


def _mk_l0f(l):
    def ph(kb, E, b):
        X, ABF, KT, VT, WB = E["X"], E["ABF"], E["KT"], E["VT"], E["WB"]
        PBF, PT = E["PBF"], E["PT"]
        PS_S, PS_T, PS_M = E["PS_S"], E["PS_T"], E["PS_M"]
        KTU, VTU = E["KTU"], E["VTU"]
        kb.full_barrier()
        _cast_shift_abf(kb, E, shift=True)
        _fold_xpos(kb, E, E["xposQC"], E["xposQS"], 4 * l + 0, 4 * l + 1)
        _l0f_qk_tiles(kb, E, l, 0)
        _fold_xpos(kb, E, E["xposKC"], E["xposKS"], 4 * l + 2, 4 * l + 3)
        _l0f_qk_tiles(kb, E, l, 1)
        wbase = E["l0_wqkv"][l]
        wb_mm = [0, 0]
        ps_ev = [0, 0]
        for f4 in range(4):
            half = f4 % 2
            kb.wait("Y", "T", wb_mm[half])
            _wstream_begin(kb, half)
            nW = _stream_w_cols(kb, E, wbase, 1024 + f4 * 128, D, 1536, half)
            kb.wait("T", "dZ%d" % half, nW)
            for tt4 in range(4):
                pb = tt4 % 2
                ps = PS_S[:, pb * CH:pb * CH + CH]
                kb.wait("T", "V", ps_ev[pb])
                nT = 0
                for sub in range(4):
                    tt = tt4 * 4 + sub
                    for kt in range(8):
                        nT = kb.op("T", lambda t, kt=kt, tt=tt, sub=sub, half=half, ps=ps: t.matmul(
                            ps[:, sub * 128:(sub + 1) * 128],
                            ABF[kt][:, tt * 128:(tt + 1) * 128],
                            WB[:, half * 1024 + kt * 128: half * 1024 + kt * 128 + 128],
                            start=(kt == 0), stop=(kt == 7)))
                wb_mm[half] = nT
                kb.wait("V", "T", nT)
                nE = 0
                for sub in range(4):
                    tt = tt4 * 4 + sub
                    nE = kb.op("V", lambda v, tt=tt, sub=sub, f4=f4, ps=ps: v.tensor_copy(
                        out=VT[tt][:, f4 * 128:(f4 + 1) * 128], in_=ps[:, sub * 128:(sub + 1) * 128]))
                ps_ev[pb] = nE
        kb.full_barrier()
        OT = ABF
        units = [(iq, h) for iq in range(QT) for h in range(L0F_HEADS)]
        sc_n = {}
        exp_n = {}
        aav_n = {}
        evac_n = {}
        ttr_n = {}
        tcp_n = {}
        ABFUa = E["ABFU"]

        def emit_unit_tail(u):
            iq, h = units[u]
            p = u % 2
            tp = iq % 2
            jb0 = max(0, iq - 4)
            nj = iq - jb0 + 1
            boff = p * 1024
            po = PS_M[:, p * 512:p * 512 + 65]
            kb.wait("T", "S", exp_n[u])
            if u >= 2:
                kb.wait("T", "V", evac_n[u - 2])
            nA = 0
            for j in range(nj):
                jb = jb0 + j
                nA = kb.op("T", lambda t, j=j, jb=jb, h=h, boff=boff, po=po: t.matmul(
                    po[:, 0:64], PBF[:, boff + j * 128:boff + j * 128 + 128],
                    VT[jb][:, h * 64:h * 64 + 64], start=(j == 0), stop=False))
                nA = kb.op("T", lambda t, j=j, nj=nj, boff=boff, po=po: t.matmul(
                    po[:, 64:65], PBF[:, boff + j * 128:boff + j * 128 + 128],
                    E["ONEC"][:, :], start=False, stop=(j == nj - 1)))
            aav_n[u] = nA
            kb.wait("V", "T", nA)
            nv = kb.op("V", lambda v, po=po, p=p: v.reciprocal(
                out=E["RCP"][:, p:p + 1], in_=po[:, 64:65]))
            kb.wait("V", "V", nv)
            if h == 0 and iq >= 2:
                kb.wait("V", "T", ttr_n[iq - 2])
            evac_n[u] = kb.op("V", lambda v, po=po, tp=tp, h=h, p=p: v.tensor_scalar(
                out=PT[:, tp * 512 + h * 64:tp * 512 + h * 64 + 64],
                in0=po[:, 0:64], scalar1=E["RCP"][:, p:p + 1], scalar2=None, op0=ALU.mult))

        def emit_tile_tail(iq):
            tp = iq % 2
            kb.wait("T", "V", evac_n[iq * L0F_HEADS + L0F_HEADS - 1])
            if iq >= 2:
                kb.wait("T", "V", tcp_n[iq - 2])
            nTt = 0
            for bqd in range(4):
                nTt = kb.op("T", lambda t, bqd=bqd, tp=tp: t.transpose(
                    PS_T[:, tp * 1024 + bqd * 128:tp * 1024 + bqd * 128 + 128],
                    PT[:, tp * 512 + bqd * 128:tp * 512 + bqd * 128 + 128], E["IDB"][:, :]))
            ttr_n[iq] = nTt
            kb.wait("V", "T", nTt)
            dst = bass.AP(ABFUa.tensor, ABFUa.offset + iq * 128,
                          [[8 * T, 128], [T, 4], [1, 128]])
            tcp_n[iq] = kb.op("V", lambda v, dst=dst, tp=tp: v.tensor_copy(
                out=dst, in_=PS_T[:, tp * 1024:tp * 1024 + 512]))

        for u, (iq, h) in enumerate(units):
            p = u % 2
            jb0 = max(0, iq - 4)
            nj = iq - jb0 + 1
            wj = nj * 128
            boff = p * 1024
            ft = h // 2
            rb = (h % 2) * 64
            if u >= 2:
                kb.wait("T", "S", exp_n[u - 2])
            nT = 0
            for j in range(nj):
                jb = jb0 + j
                need_diag = (jb == iq)
                need_far = (j == 0 and iq >= 4)
                nT = kb.op("T", lambda t, j=j, jb=jb, ft=ft, rb=rb, iq=iq, boff=boff,
                           stop=not (need_diag or need_far): t.matmul(
                    PS_S[:, boff + j * 128:boff + j * 128 + 128],
                    KT[4 + ft][rb:rb + 64, jb * 128:(jb + 1) * 128],
                    KT[ft][rb:rb + 64, iq * 128:(iq + 1) * 128],
                    start=True, stop=stop))
                if need_diag:
                    nT = kb.op("T", lambda t, j=j, boff=boff: t.matmul(
                        PS_S[:, boff + j * 128:boff + j * 128 + 128],
                        E["M0B"][:, :], E["IDB"][:, :], start=False, stop=True))
                if need_far:
                    nT = kb.op("T", lambda t, j=j, boff=boff: t.matmul(
                        PS_S[:, boff + j * 128:boff + j * 128 + 128],
                        E["M4B"][:, :], E["IDB"][:, :], start=False, stop=True))
            sc_n[u] = nT
            kb.wait("S", "T", sc_n[u])
            if u >= 2:
                kb.wait("S", "T", aav_n[u - 2])
            exp_n[u] = kb.op("S", lambda s, wj=wj, boff=boff: s.activation(
                out=PBF[:, boff:boff + wj], in_=PS_S[:, boff:boff + wj], func=AF.Exp))
            if u >= 1:
                emit_unit_tail(u - 1)
            if h == 1 and iq >= 1:
                emit_tile_tail(iq - 1)
        emit_unit_tail(len(units) - 1)
        emit_tile_tail(QT - 1)
        kb.full_barrier()
        wbase = E["l0_wo"][l]
        wb_mm = [0, 0]
        ps_ev = [0, 0]
        for ft in range(8):
            half = ft % 2
            kb.wait("Y", "T", wb_mm[half])
            _wstream_begin(kb, half)
            nW = _stream_w_cols(kb, E, wbase, ft * 128, 512, D, half)
            kb.wait("T", "dZ%d" % half, nW)
            for c in range(NCH):
                pb = c % 2
                ps = PS_S[:, pb * CH:pb * CH + CH]
                kb.wait("T", "V", ps_ev[pb])
                nT = 0
                for kt in range(4):
                    nT = kb.op("T", lambda t, kt=kt, c=c, half=half, ps=ps: t.matmul(
                        ps, WB[:, half * 1024 + kt * 128: half * 1024 + kt * 128 + 128],
                        OT[kt][:, c * CH:(c + 1) * CH], start=(kt == 0), stop=(kt == 3)))
                wb_mm[half] = nT
                kb.wait("V", "T", nT)
                ps_ev[pb] = kb.op("V", lambda v, ft=ft, c=c, ps=ps: v.tensor_tensor(
                    out=X[ft][:, c * CH:(c + 1) * CH], in0=X[ft][:, c * CH:(c + 1) * CH],
                    in1=ps, op=ALU.add))
        kb.full_barrier()
        _cast_shift_abf(kb, E, shift=True)
        gcols = [E["LNC"][:, l * 16 + ft:l * 16 + ft + 1] for ft in range(8)]
        bcols = [E["LNC"][:, l * 16 + 8 + ft:l * 16 + 8 + ft + 1] for ft in range(8)]
        prev = None
        for c in range(NCH):
            bfs = [ABF[ft][:, c * CH:(c + 1) * CH] for ft in range(8)]
            f = ln_stats_apply_bf(kb, E, bfs, CH, gcols, bcols, pp=c % 2, defer=True)
            if prev is not None:
                prev()
            prev = f
        prev()
        kb.full_barrier()
        w1b = E["l0_w1re"][l]
        w2b = E["l0_w2p"][l]
        W1S = [KTU[:, 0:1024], KTU[:, 1024:2048]]
        W2S = [KTU[:, 2048:2048 + 2816], KTU[:, 4864:4864 + 2816]]
        for rnd in range(3):
            fp0 = rnd * 8
            fp1 = min(NHT, fp0 + 8)
            wb_mm = [0, 0]
            h4_v = [0, 0]
            gel_s = [0, 0]
            for fp in range(fp0, fp1):
                half = fp % 2
                kb.wait("Y", "T", wb_mm[half])
                _wstream_begin(kb, half)
                nWa = _stream_w_cols(kb, E, w1b, (2 * fp) * 128, D, 2 * HIDP, half,
                                     dest=W1S[half][:, 0:1024])
                nWg = _stream_w_cols(kb, E, w1b, (2 * fp + 1) * 128, D, 2 * HIDP, half)
                kb.wait("T", "dZ%d" % half, nWg)
                acol = E["B1C"][:, l * 2 * NHT + 2 * fp: l * 2 * NHT + 2 * fp + 1]
                gcol = E["B1C"][:, l * 2 * NHT + 2 * fp + 1: l * 2 * NHT + 2 * fp + 2]
                for c in range(NCH):
                    p = c % 2
                    psa = PS_S[:, 2 * p * CH:(2 * p + 1) * CH]
                    psg = PS_S[:, (2 * p + 1) * CH:(2 * p + 2) * CH]
                    kb.wait("T", "V", h4_v[p])
                    kb.wait("T", "S", gel_s[p])
                    nTa = 0
                    for kt in range(8):
                        nTa = kb.op("T", lambda t, kt=kt, c=c, half=half, psa=psa: t.matmul(
                            psa, W1S[half][:, kt * 128:kt * 128 + 128],
                            ABF[kt][:, c * CH:(c + 1) * CH], start=(kt == 0), stop=(kt == 7)))
                    nTg = 0
                    for kt in range(8):
                        nTg = kb.op("T", lambda t, kt=kt, c=c, half=half, psg=psg: t.matmul(
                            psg, WB[:, half * 1024 + kt * 128: half * 1024 + kt * 128 + 128],
                            ABF[kt][:, c * CH:(c + 1) * CH], start=(kt == 0), stop=(kt == 7)))
                    wb_mm[half] = nTg
                    kb.wait("S", "T", nTg)
                    kb.wait("S", "V", h4_v[p])
                    nG = kb.op("S", lambda s, psg=psg, gcol=gcol, p=p: s.activation(
                        out=E["SQB"][:, p * CH:(p + 1) * CH], in_=psg, func=AF.Gelu,
                        bias=gcol[0:128, 0:1]))
                    gel_s[p] = nG
                    kb.wait("V", "T", nTa)
                    nA = kb.op("V", lambda v, psa=psa, acol=acol, p=p: v.tensor_scalar(
                        out=E["SCB"][:, p * CH:(p + 1) * CH], in0=psa, scalar1=acol[0:128, 0:1],
                        scalar2=None, op0=ALU.add))
                    kb.wait("V", "S", nG)
                    kb.wait("V", "V", nA)
                    h4_v[p] = kb.op("V", lambda v, fp=fp, fp0=fp0, c=c, p=p: v.tensor_tensor(
                        out=VTU[:, ((fp - fp0) * 4 + c) * CH:((fp - fp0) * 4 + c) * CH + CH],
                        in0=E["SCB"][:, p * CH:(p + 1) * CH],
                        in1=E["SQB"][:, p * CH:(p + 1) * CH], op=ALU.mult))
            kb.full_barrier()
            nkt = fp1 - fp0
            wb_mm = [0, 0]
            ps_ev = [0, 0]
            for ft in range(8):
                half = ft % 2
                kb.wait("Y", "T", wb_mm[half])
                _wstream_begin(kb, half)
                w2rows = bass.AP(w2b.tensor, w2b.offset + ft * 128 * HIDP + fp0 * 128,
                                 [[HIDP, 128], [1, nkt * 128]])
                nW = kb.dma("Y", "dZ%d" % half, lambda sy, re=w2rows, half=half, nkt=nkt: sy.dma_start(
                    W2S[half][:, 0:nkt * 128], re))
                kb.wait("T", "dZ%d" % half, nW)
                for c in range(NCH):
                    pb = c % 2
                    ps = PS_S[:, 2 * CH + pb * CH:2 * CH + pb * CH + CH]
                    kb.wait("T", "V", ps_ev[pb])
                    nT = 0
                    for kk in range(nkt):
                        nT = kb.op("T", lambda t, kk=kk, c=c, half=half, ps=ps, nkt=nkt: t.matmul(
                            ps, W2S[half][:, kk * 128:kk * 128 + 128],
                            VTU[:, (kk * 4 + c) * CH:(kk * 4 + c) * CH + CH],
                            start=(kk == 0), stop=(kk == nkt - 1)))
                    wb_mm[half] = nT
                    kb.wait("V", "T", nT)
                    if rnd < 2:
                        ps_ev[pb] = kb.op("V", lambda v, ft=ft, c=c, ps=ps: v.tensor_tensor(
                            out=X[ft][:, c * CH:(c + 1) * CH], in0=X[ft][:, c * CH:(c + 1) * CH],
                            in1=ps, op=ALU.add))
                    else:
                        b2col = E["B2C"][:, l * 8 + ft:l * 8 + ft + 1]
                        ps_ev[pb] = kb.op("V", lambda v, ft=ft, c=c, ps=ps, b2col=b2col: v.scalar_tensor_tensor(
                            out=X[ft][:, c * CH:(c + 1) * CH], in0=ps, scalar=b2col,
                            in1=X[ft][:, c * CH:(c + 1) * CH], op0=ALU.add, op1=ALU.add))
            kb.full_barrier()
    return ph


PH_FUNCS["l0fs"] = ph_l0fs
PH_FUNCS["l0fr"] = ph_l0fr
PH_FUNCS["l0f0"] = _mk_l0f(0)
PH_FUNCS["l0f1"] = _mk_l0f(1)



def ph_l1a(kb, E, b):
    X, ABF, KT, VT, QJ, WB, OE = E["X"], E["ABF"], E["KT"], E["VT"], E["QJ"], E["WB"], E["OE"]
    PBF, PT = E["PBF"], E["PT"]
    PS_S, PS_T, PS_M = E["PS_S"], E["PS_T"], E["PS_M"]
    BQC, BOC = E["BQC"], E["BOC"]
    wqkv = E["l1_wqkv"]
    kb.full_barrier()
    _cast_shift_abf(kb, E, shift=False)
    wb_mm = [0, 0]
    ps_ev = [0, 0]
    for ft in range(8):
        half = ft % 2
        kb.wait("Y", "T", wb_mm[half])
        _wstream_begin(kb, half)
        nW = _stream_w_cols(kb, E, wqkv, 1024 + ft * 128, D, 3 * D, half)
        kb.wait("T", "dZ%d" % half, nW)
        for c in range(NCH):
            pb = c % 2
            ps = PS_S[:, pb * CH:pb * CH + CH]
            kb.wait("T", "V", ps_ev[pb])
            nT = 0
            for kt in range(8):
                nT = kb.op("T", lambda t, kt=kt, c=c, half=half, ps=ps: t.matmul(
                    ps, WB[:, half * 1024 + kt * 128: half * 1024 + kt * 128 + 128],
                    ABF[kt][:, c * CH:(c + 1) * CH], start=(kt == 0), stop=(kt == 7)))
            wb_mm[half] = nT
            kb.wait("V", "T", nT)
            ps_ev[pb] = kb.op("V", lambda v, ft=ft, c=c, ps=ps: v.tensor_scalar(
                out=KT[ft][:, c * CH:(c + 1) * CH], in0=ps,
                scalar1=BQC[:, 8 + ft:9 + ft], scalar2=0.125, op0=ALU.add, op1=ALU.mult))
    kb.full_barrier()
    wb_mm = [0, 0]
    ps_ev = [0, 0]
    for f4 in range(8):
        half = f4 % 2
        kb.wait("Y", "T", wb_mm[half])
        _wstream_begin(kb, half)
        nW = _stream_w_cols(kb, E, wqkv, 2048 + f4 * 128, D, 3 * D, half)
        kb.wait("T", "dZ%d" % half, nW)
        for tt4 in range(4):
            pb = tt4 % 2
            ps = PS_S[:, pb * CH:pb * CH + CH]
            kb.wait("T", "V", ps_ev[pb])
            nT = 0
            for sub in range(4):
                tt = tt4 * 4 + sub
                for kt in range(8):
                    nT = kb.op("T", lambda t, kt=kt, tt=tt, sub=sub, half=half, ps=ps: t.matmul(
                        ps[:, sub * 128:(sub + 1) * 128],
                        ABF[kt][:, tt * 128:(tt + 1) * 128],
                        WB[:, half * 1024 + kt * 128: half * 1024 + kt * 128 + 128],
                        start=(kt == 0), stop=(kt == 7)))
            wb_mm[half] = nT
            kb.wait("V", "T", nT)
            nE = 0
            for sub in range(4):
                tt = tt4 * 4 + sub
                nE = kb.op("V", lambda v, tt=tt, sub=sub, f4=f4, ps=ps: v.tensor_copy(
                    out=VT[tt][:, f4 * 128:(f4 + 1) * 128], in_=ps[:, sub * 128:(sub + 1) * 128]))
            ps_ev[pb] = nE
    kb.full_barrier()
    for g in range(4):
        wb_mm = [0, 0]
        ps_ev = [0, 0]
        for ft in range(8):
            half = ft % 2
            kb.wait("Y", "T", wb_mm[half])
            _wstream_begin(kb, half)
            nW = _stream_w_cols(kb, E, wqkv, ft * 128, D, 3 * D, half)
            kb.wait("T", "dZ%d" % half, nW)
            kb.wait("T", "V", ps_ev[half])
            ps = PS_S[:, (ft % 2) * CH:(ft % 2) * CH + CH]
            nT = 0
            for kt in range(8):
                nT = kb.op("T", lambda t, kt=kt, g=g, half=half, ps=ps: t.matmul(
                    ps, WB[:, half * 1024 + kt * 128: half * 1024 + kt * 128 + 128],
                    ABF[kt][:, g * CH:(g + 1) * CH], start=(kt == 0), stop=(kt == 7)))
            wb_mm[half] = nT
            kb.wait("V", "T", nT)
            ps_ev[half] = kb.op("V", lambda v, ft=ft, ps=ps: v.tensor_scalar(
                out=QJ[ft][:, :], in0=ps, scalar1=BQC[:, ft:ft + 1], scalar2=None, op0=ALU.add))
        kb.full_barrier()
        units = [(iq, h) for iq in range(4 * g, 4 * g + 4) for h in range(16)]
        exp_n = {}
        expa_n = {}
        aav_n = {}
        evac_n = {}
        for u, (iq, h) in enumerate(units):
            if L1A_SERIAL:
                kb.full_barrier()
            jext = (iq + 1) * 128
            dbl = jext <= 1024
            boff = (u % 2) * 1024 if dbl else 0
            ft = h // 2
            rb = (h % 2) * 64
            il = (iq - 4 * g) * 128
            dd = 1 if (L1A_V1 or not dbl) else 2
            if u >= dd:
                key = expa_n.get(u - dd, exp_n[u - dd]) if not dbl else exp_n[u - dd]
                kb.wait("T", "S", key)
            nT = 0
            sc_a = 0
            for jb in range(iq + 1):
                last = (jb == iq)
                if not dbl and jb == 8:
                    sc_a = nT
                    kb.wait("T", "S", exp_n[u - 1] if u >= 1 else 0)
                nT = kb.op("T", lambda t, jb=jb, ft=ft, rb=rb, il=il, boff=boff, last=last: t.matmul(
                    PS_S[:, boff + jb * 128:boff + jb * 128 + 128],
                    KT[ft][rb:rb + 64, jb * 128:(jb + 1) * 128],
                    QJ[ft][rb:rb + 64, il:il + 128],
                    start=True, stop=not last))
            nT = kb.op("T", lambda t, jext=jext, boff=boff: t.matmul(
                PS_S[:, boff + jext - 128:boff + jext],
                E["M0B"][:, :], E["IDB"][:, :], start=False, stop=True))
            sc_n = nT
            if not dbl:
                slot = u % 2
                PTb = E["PT"] if slot == 0 else E["PBF"]
                kb.wait("S", "T", sc_a)
                dd2 = 1 if L1A_V2 else 2
                if u >= dd2 and (u - dd2) in aav_n:
                    kb.wait("S", "T", aav_n[u - dd2])
                expa_n[u] = kb.op("S", lambda s, PTb=PTb: s.activation(
                    out=PTb[:, 0:1024], in_=PS_S[:, 0:1024], func=AF.Exp))
            if u >= 1:
                pq, ph_ = units[u - 1]
                pjext = (pq + 1) * 128
                pslot = (u - 1) % 2
                PTp = E["PT"] if pslot == 0 else E["PBF"]
                po = PS_M[:, pslot * CH:pslot * CH + 65]
                kb.wait("T", "S", exp_n[u - 1])
                dd3 = 2 if L1A_V3 else 3
                if u >= dd3 and (u - dd3) in evac_n:
                    kb.wait("T", "V", evac_n[u - dd3])
                nA = 0
                for jb in range(pq + 1):
                    nA = kb.op("T", lambda t, jb=jb, ph_=ph_, po=po, PTp=PTp, pq=pq: t.matmul(
                        po[:, 0:64], PTp[:, jb * 128:(jb + 1) * 128],
                        VT[jb][:, ph_ * 64:ph_ * 64 + 64],
                        start=(jb == 0), stop=False))
                    nA = kb.op("T", lambda t, jb=jb, po=po, PTp=PTp, pq=pq: t.matmul(
                        po[:, 64:65], PTp[:, jb * 128:(jb + 1) * 128],
                        E["ONEC"][:, :], start=False, stop=(jb == pq)))
                aav_n[u - 1] = nA
                kb.wait("V", "T", nA)
                rp = (u - 1) % 2
                nv = kb.op("V", lambda v, po=po, rp=rp: v.reciprocal(
                    out=E["RCP"][:, rp:rp + 1], in_=po[:, 64:65]))
                kb.wait("V", "V", nv)
                piq = pq
                evac_n[u - 1] = kb.op("V", lambda v, po=po, piq=piq, ph_=ph_, g=g, rp=rp: v.tensor_scalar(
                    out=E["OEU"][:, (piq - 4 * g) * 1024 + ph_ * 64:(piq - 4 * g) * 1024 + ph_ * 64 + 64],
                    in0=po[:, 0:64], scalar1=E["RCP"][:, rp:rp + 1], scalar2=None, op0=ALU.mult))
            slot = u % 2
            PTb = E["PT"] if slot == 0 else E["PBF"]
            kb.wait("S", "T", sc_n)
            dd2 = 1 if L1A_V2 else 2
            if dbl and u >= dd2 and (u - dd2) in aav_n:
                kb.wait("S", "T", aav_n[u - dd2])
            if dbl:
                exp_n[u] = kb.op("S", lambda s, jext=jext, boff=boff, PTb=PTb: s.activation(
                    out=PTb[:, 0:jext], in_=PS_S[:, boff:boff + jext], func=AF.Exp))
            else:
                exp_n[u] = kb.op("S", lambda s, jext=jext, PTb=PTb: s.activation(
                    out=PTb[:, 1024:jext], in_=PS_S[:, 1024:jext], func=AF.Exp))
        u = len(units) - 1
        iq, h = units[u]
        jext = (iq + 1) * 128
        slot = u % 2
        PTb = E["PT"] if slot == 0 else E["PBF"]
        po = PS_M[:, slot * CH:slot * CH + 65]
        kb.wait("T", "S", exp_n[u])
        kb.wait("T", "V", evac_n.get(u - 2, 0))
        nA = 0
        for jb in range(iq + 1):
            nA = kb.op("T", lambda t, jb=jb, h=h, po=po, PTb=PTb, iq=iq: t.matmul(
                po[:, 0:64], PTb[:, jb * 128:(jb + 1) * 128],
                VT[jb][:, h * 64:h * 64 + 64], start=(jb == 0), stop=False))
            nA = kb.op("T", lambda t, jb=jb, po=po, PTb=PTb, iq=iq: t.matmul(
                po[:, 64:65], PTb[:, jb * 128:(jb + 1) * 128],
                E["ONEC"][:, :], start=False, stop=(jb == iq)))
        kb.wait("V", "T", nA)
        rp = u % 2
        nv = kb.op("V", lambda v, po=po, rp=rp: v.reciprocal(
            out=E["RCP"][:, rp:rp + 1], in_=po[:, 64:65]))
        kb.wait("V", "V", nv)
        kb.op("V", lambda v, po=po, iq=iq, h=h, g=g, rp=rp: v.tensor_scalar(
            out=E["OEU"][:, (iq - 4 * g) * 1024 + h * 64:(iq - 4 * g) * 1024 + h * 64 + 64],
            in0=po[:, 0:64], scalar1=E["RCP"][:, rp:rp + 1], scalar2=None, op0=ALU.mult))
        kb.full_barrier()
        QJU = E["QJU"]
        scp = [0, 0]
        for it in range(4):
            for grp in range(2):
                bank = (it * 2 + grp) % 2
                if it * 2 + grp >= 2:
                    kb.wait("T", "S", scp[bank])
                nTt = 0
                for k in range(4):
                    ft = grp * 4 + k
                    nTt = kb.op("T", lambda t, it=it, ft=ft, k=k, bank=bank: t.transpose(
                        PS_T[:, bank * 1024 + k * 128: bank * 1024 + k * 128 + 128],
                        E["OEU"][:, it * 1024 + ft * 128:it * 1024 + ft * 128 + 128], E["IDB"][:, :]))
                kb.wait("S", "T", nTt)
                dst = bass.AP(QJU.tensor, QJU.offset + grp * 4 * CH + it * 128,
                              [[4096, 128], [CH, 4], [1, 128]])
                scp[bank] = kb.op("S", lambda s, dst=dst, bank=bank: s.copy(
                    out=dst, in_=PS_T[:, bank * 1024: bank * 1024 + 512]))
        kb.full_barrier()
        wout = E["l1_wout"]
        if g == 0:
            nbv = kb.op("V", lambda v: v.tensor_copy(out=E["BVB"][:, :], in_=E["BQC"][:, 16:24]))
            kb.wait("T", "V", nbv)
        wb_mm2 = [0, 0]
        ps_ev2 = [0, 0]
        for ft in range(8):
            half = ft % 2
            kb.wait("Y", "T", wb_mm2[half])
            _wstream_begin(kb, half)
            nW = _stream_w_cols(kb, E, wout, ft * 128, D, D, half)
            kb.wait("T", "dZ%d" % half, nW)
            kb.wait("T", "V", ps_ev2[half])
            ps = PS_S[:, (ft % 2) * CH:(ft % 2) * CH + CH]
            nT = 0
            for kt in range(8):
                nT = kb.op("T", lambda t, kt=kt, half=half, ps=ps: t.matmul(
                    ps, WB[:, half * 1024 + kt * 128: half * 1024 + kt * 128 + 128],
                    QJ[kt][:, :], start=(kt == 0), stop=(kt == 7)))
            wb_mm2[half] = nT
            if g == 0:
                kb.wait("T", "V", kb.c["V"])
                nb2 = 0
                for kt in range(8):
                    nb2 = kb.op("T", lambda t, kt=kt, half=half: t.matmul(
                        PS_M[:, 65:66], WB[:, half * 1024 + kt * 128: half * 1024 + kt * 128 + 128],
                        E["BVB"][:, kt:kt + 1],
                        start=(kt == 0), stop=(kt == 7)))
                wb_mm2[half] = nb2
                kb.wait("V", "T", nb2)
                nbb = kb.op("V", lambda v, ft=ft: v.tensor_tensor(
                    out=E["BOC2"][:, ft:ft + 1], in0=PS_M[:, 65:66], in1=E["BOC"][:, ft:ft + 1],
                    op=ALU.add))
                kb.wait("V", "V", nbb)
            kb.wait("V", "T", nT)
            ps_ev2[half] = kb.op("V", lambda v, ft=ft, g=g, ps=ps: v.scalar_tensor_tensor(
                out=X[ft][:, g * CH:(g + 1) * CH], in0=ps, scalar=E["BOC2"][:, ft:ft + 1],
                in1=X[ft][:, g * CH:(g + 1) * CH], op0=ALU.add, op1=ALU.add))
        kb.full_barrier()
        xs = [X[ft][:, g * CH:(g + 1) * CH] for ft in range(8)]
        ln_stats_apply(kb, E, xs, CH, pp=g % 2)
        kb.full_barrier()


def ph_l1f(kb, E, b):
    X, ABF, WB = E["X"], E["ABF"], E["WB"]
    PS_S, PS_T, PS_M = E["PS_S"], E["PS_T"], E["PS_M"]
    QJ = E["QJ"]
    OE = E["OE"]
    MS = E["TB0"]
    BQC = E["BQC"]
    kb.full_barrier()
    _cast_shift_abf(kb, E, shift=False)

    def bank(k):
        if k < 4:
            return PS_S[:, k * CH:(k + 1) * CH]
        return PS_M[:, (k - 4) * CH:(k - 3) * CH]

    GW = E["SR"]
    gw_ap = bass.AP(E["gate_wT"].tensor, E["gate_wT"].offset,
                    [[8, 128], [1024, 8], [1, 8]])
    kb.dma_begin("Y", "dY0")
    ngw = kb.dma("Y", "dY0", lambda sy: sy.dma_start(GW[:, 0:64], gw_ap))
    kb.wait("T", "dY0", ngw)
    for tt in range(QT):
        kb.wait("T", "V", kb.c["V"])
        kb.wait("T", "S", kb.c["S"])
        nT = 0
        for kt in range(8):
            nT = kb.op("T", lambda t, kt=kt, tt=tt: t.matmul(
                PS_M[:, 0:8], X[kt][:, tt * 128:(tt + 1) * 128],
                GW[:, kt * 8:(kt + 1) * 8], start=(kt == 0), stop=(kt == 7)))
        kb.wait("V", "T", nT)
        nV = kb.op("V", lambda v: v.reduce_max(out=E["NMX"][:, 0:1], in_=PS_M[:, 0:8], axis=AX.X))
        kb.wait("V", "V", nV)
        nV = kb.op("V", lambda v: v.tensor_scalar(
            out=E["SCB"][:, 0:8], in0=PS_M[:, 0:8], scalar1=E["NMX"][:, 0:1],
            scalar2=None, op0=ALU.is_ge))
        kb.wait("T", "V", nV)
        nTt = kb.op("T", lambda t, tt=tt: t.transpose(
            PS_T[0:8, (tt % 2) * 1024:(tt % 2) * 1024 + 128], E["SCB"][:, 0:8], E["IDB"][:, :]))
        kb.wait("S", "T", nTt)
        kb.op("S", lambda s, tt=tt: s.copy(
            out=MS[0:8, tt * 128:(tt + 1) * 128],
            in_=PS_T[0:8, (tt % 2) * 1024:(tt % 2) * 1024 + 128]))
    kb.full_barrier()

    ew = E["ew_T"]
    for c in range(NCH):
        prevS = 0
        mask_s = {}
        for e in range(8):
            kb.wait("T", "S", prevS)
            nb = kb.op("T", lambda t, e=e, c=c: t.matmul(
                PS_M[:, CH:2 * CH], E["SELSb"][:, e * 128:(e + 1) * 128],
                MS[0:8, c * CH:(c + 1) * CH], start=True, stop=True))
            kb.wait("S", "T", nb)
            prevS = kb.op("S", lambda s, e=e: s.copy(out=QJ[e][:, :], in_=PS_M[:, CH:2 * CH]))
            mask_s[e] = prevS
        kb.wait("T", "S", prevS)
        PTa, PBFa = E["PT"], E["PBF"]

        def xe_ap(s_, kt):
            if s_ == 0:
                return OE[kt][:, :]
            return (PTa if kt < 4 else PBFa)[:, (kt % 4) * CH:(kt % 4 + 1) * CH]
        ev_n = 0
        lastT = 0
        xe_mm = [0, 0]
        wb_cons = [0, 0]
        for (f0, f1) in ((0, 6), (6, 8)):
            for e in range(8):
                xs = e % 2
                kb.wait("V", "T", xe_mm[xs])
                kb.wait("V", "S", mask_s[e])
                nxe = 0
                for kt in range(8):
                    nxe = kb.op("V", lambda v, kt=kt, c=c, e=e, xs=xs: v.tensor_tensor(
                        out=xe_ap(xs, kt), in0=ABF[kt][:, c * CH:(c + 1) * CH], in1=QJ[e][:, :],
                        op=ALU.mult))
                nq = (f1 - f0 + 3) // 4
                fts = list(range(f0, f1))
                qgroups = [fts[i:i + 4] for i in range(0, len(fts), 4)]
                for qg in qgroups:
                    half = (e + qg[0]) % 2
                    WQ = E["VTU"][:, half * 8192:half * 8192 + len(qg) * 1024]
                    kb.wait("Y", "T", wb_cons[half])
                    _wstream_begin(kb, half)
                    re = _wtile_ap(ew[e], qg[0], D, ntile=len(qg))
                    nW = kb.dma("Y", "dZ%d" % half, lambda sy, re=re, WQ=WQ: sy.dma_start(WQ, re))
                    kb.wait("T", "dZ%d" % half, nW)
                    kb.wait("T", "V", nxe)
                    if e == 0 and qg[0] == f0 and f0 == 0:
                        kb.wait("T", "V", ev_n)
                    for fi, ft in enumerate(qg):
                        for kt in range(8):
                            lastT = kb.op("T", lambda t, kt=kt, ft=ft, fi=fi, e=e, xs=xs, f0=f0, WQ=WQ: t.matmul(
                                bank(ft - f0),
                                WQ[:, fi * 1024 + kt * 128: fi * 1024 + kt * 128 + 128],
                                xe_ap(xs, kt), start=(e == 0 and kt == 0), stop=(e == 7 and kt == 7)))
                    wb_cons[half] = lastT
                xe_mm[xs] = lastT
            kb.wait("V", "T", lastT)
            for ft in range(f0, f1):
                ev_n = kb.op("V", lambda v, ft=ft, c=c, f0=f0: v.tensor_tensor(
                    out=X[ft][:, c * CH:(c + 1) * CH], in0=X[ft][:, c * CH:(c + 1) * CH],
                    in1=bank(ft - f0), op=ALU.add))
        kb.full_barrier()
        xs = [X[ft][:, c * CH:(c + 1) * CH] for ft in range(8)]
        ln_stats_apply(kb, E, xs, CH, pp=c % 2)
        kb.full_barrier()


PH_FUNCS["l1a"] = ph_l1a
L1A_SERIAL = False
L1A_V1 = False
L1A_V2 = False
L1A_V3 = False
PH_FUNCS["l1f"] = ph_l1f


def _in_maps_for(w, C, xs):
    m = dict(w)
    m.update({k: C[k] for k in ("ropeC", "ropeS", "ident", "JTB", "RTB", "M0", "M4",
                                 "HS", "HST", "SELS", "xposQC", "xposQS", "xposKC", "xposKS")})
    m["x"] = xs
    return m


def kernel(**inputs):
    C = _consts()
    w = _prep_weights(inputs)
    x = np.ascontiguousarray(np.asarray(inputs["L0_attn_x"], np.float32))
    nc = build_core_program(["l0a", "l0fs", "l0f0", "l0f1", "l0fr", "l1a", "l1f"])
    in_maps = []
    for c in range(8):
        in_maps.append(_in_maps_for(w, C, np.ascontiguousarray(x[2 * c:2 * c + 2])))
    res = run_bass_kernel_spmd(nc, in_maps, core_ids=list(range(8)))
    out = np.concatenate([r["y"] for r in res.results], axis=0)
    return out.astype(np.float32)


# version 20
# speedup vs baseline: 1.0896x; 1.0613x over previous
import numpy as np
from contextlib import ExitStack

import concourse.bass as bass
import concourse.mybir as mybir
from concourse.bass_utils import run_bass_kernel_spmd

F32 = mybir.dt.float32
BF16 = mybir.dt.bfloat16
AF = mybir.ActivationFunctionType
ALU = mybir.AluOpType
AX = mybir.AxisListType

T = 2048
D = 1024
CH = 512
NCH = T // CH
FT = D // 128
QT = T // 128
INV1024 = 1.0 / 1024.0
NEG = -1.0e30

L0F_HEADS = 8
L0F_DH = 64
L0F_INNER = 512
L0F_WIN = 512
HID = 2730
HIDP = 2816
NHT = HIDP // 128


def _host_consts():
    c = {}
    c["ident"] = np.eye(128, dtype=np.float32)
    half = D // 2
    inv = 1.0 / (10000.0 ** (np.arange(half, dtype=np.float64) / half))
    ang = np.arange(T, dtype=np.float64)[:, None] * inv[None, :]
    cosv = np.cos(ang).T
    sinv = np.sin(ang).T
    ropeC = np.repeat(cosv, 2, axis=0).astype(np.float32)
    sgn = np.where(np.arange(D) % 2 == 0, -1.0, 1.0)[:, None]
    ropeS = (np.repeat(sinv, 2, axis=0) * sgn).astype(np.float32)
    c["ropeC"] = np.ascontiguousarray(ropeC)
    c["ropeS"] = np.ascontiguousarray(ropeS)
    P = np.zeros((128, 128), dtype=np.float32)
    i2 = np.arange(0, 128, 2)
    P[i2, i2 + 1] = 1.0
    P[i2 + 1, i2] = 1.0
    c["JTB"] = P
    r = np.arange(128)[:, None]
    cc = np.arange(128)[None, :]
    c["M0"] = np.where(cc > r, NEG, 0.0).astype(np.float32)
    c["M4"] = np.where(cc < r, NEG, 0.0).astype(np.float32)
    dh = L0F_DH
    hi = np.arange(0, dh, 2, dtype=np.float64)
    invf = 1.0 / (10000.0 ** (hi / dh))
    tt = np.arange(T, dtype=np.float64)
    fr32 = tt[:, None] * invf[None, :]
    fr = np.concatenate([fr32, fr32], axis=1)
    base = (hi + 0.4 * dh) / (1.4 * dh)
    power = (tt - T // 2) / float(L0F_WIN // 2)
    sc32 = base[None, :] ** power[:, None]
    sc = np.concatenate([sc32, sc32], axis=1)
    cosf = np.cos(fr)
    sinf = np.sin(fr)
    sgn64 = np.where(np.arange(dh) < dh // 2, -1.0, 1.0)[None, :]
    qc = (cosf * sc * 8.0).T
    qs = (sinf * sc * 8.0 * sgn64).T
    kc = (cosf / sc).T
    ks = (sinf / sc * sgn64).T
    c["xposQC"] = np.ascontiguousarray(np.tile(qc, (2, 1)).astype(np.float32))
    c["xposQS"] = np.ascontiguousarray(np.tile(qs, (2, 1)).astype(np.float32))
    c["xposKC"] = np.ascontiguousarray(np.tile(kc, (2, 1)).astype(np.float32))
    c["xposKS"] = np.ascontiguousarray(np.tile(ks, (2, 1)).astype(np.float32))
    R = np.zeros((128, 128), dtype=np.float32)
    for hh in range(2):
        o = hh * 64
        for rr in range(32):
            R[o + rr, o + rr + 32] = 1.0
            R[o + rr + 32, o + rr] = 1.0
    c["RTB"] = R.T.copy()
    HS = np.zeros((128, 2), dtype=np.float32)
    HS[0:64, 0] = 1.0
    HS[64:128, 1] = 1.0
    c["HS"] = HS
    c["HST"] = np.ascontiguousarray(HS.T)
    SELS = np.zeros((8, 8 * 128), dtype=np.float32)
    for e in range(8):
        SELS[e, e * 128:(e + 1) * 128] = 1.0
    c["SELS"] = SELS
    return c


_CONSTS = None

def _consts():
    global _CONSTS
    if _CONSTS is None:
        _CONSTS = _host_consts()
    return _CONSTS



def _timg(W):
    R, Cc = W.shape
    return np.ascontiguousarray(
        W.reshape(R // 128, 128, Cc // 128, 128).transpose(2, 1, 0, 3).reshape(Cc // 128, 128, R))


def _prep_weights(inp):
    w = {}
    for nm, key in (("wqbig", "L0_attn_w_q"), ("wkbig", "L0_attn_w_k"), ("wvbig", "L0_attn_w_v")):
        src = np.asarray(inp[key])
        big = np.zeros((64, 16, 64, 16), dtype=np.float32)
        hh = np.arange(16)
        big[:, hh, :, hh] = src.transpose(2, 0, 1)
        w[nm] = np.ascontiguousarray(big.reshape(1024, 1024))
    w["l0_wqkv"] = np.ascontiguousarray(np.asarray(inp["L0_ffn_wqkv"], np.float32))
    w["l0_wo"] = np.ascontiguousarray(np.asarray(inp["L0_ffn_wo"], np.float32))
    w1 = np.asarray(inp["L0_ffn_w1"], np.float32)
    b1 = np.asarray(inp["L0_ffn_b1"], np.float32)
    w1re = np.zeros((2, 1024, 2 * HIDP), dtype=np.float32)
    b1re = np.zeros((2, 2 * HIDP), dtype=np.float32)
    for l in range(2):
        a = np.zeros((1024, HIDP), np.float32)
        g = np.zeros((1024, HIDP), np.float32)
        a[:, :HID] = w1[l][:, :HID]
        g[:, :HID] = w1[l][:, HID:]
        inter = np.stack([a.reshape(1024, NHT, 128), g.reshape(1024, NHT, 128)], axis=2)
        w1re[l] = inter.reshape(1024, 2 * HIDP)
        ab = np.zeros(HIDP, np.float32)
        gb = np.zeros(HIDP, np.float32)
        ab[:HID] = b1[l][:HID]
        gb[:HID] = b1[l][HID:]
        interb = np.stack([ab.reshape(NHT, 128), gb.reshape(NHT, 128)], axis=1)
        b1re[l] = interb.reshape(2 * HIDP)
    w["l0_w1re"] = w1re
    w["l0_b1c"] = np.ascontiguousarray(
        np.concatenate([b1re[l].reshape(2 * NHT, 128).T for l in range(2)], axis=1))
    w2 = np.asarray(inp["L0_ffn_w2"], np.float32)
    w2p = np.zeros((2, HIDP, 1024), np.float32)
    w2p[:, :HID, :] = w2
    w["l0_w2p"] = w2p
    w["l0_b2c"] = np.ascontiguousarray(
        np.stack([np.asarray(inp["L0_ffn_b2"], np.float32)[l].reshape(8, 128).T for l in range(2)], axis=1).reshape(128, 16))
    qs = np.asarray(inp["L0_ffn_q_scale"], np.float32)
    ks = np.asarray(inp["L0_ffn_k_scale"], np.float32)
    cols = []
    for l in range(2):
        q1 = np.tile(qs[l], 2)
        q2 = np.tile(np.roll(qs[l], 32), 2)
        k1 = np.tile(ks[l], 2)
        k2 = np.tile(np.roll(ks[l], 32), 2)
        cols += [q1, q2, k1, k2]
    w["l0_qksc"] = np.ascontiguousarray(np.stack(cols, axis=1))
    lng = np.asarray(inp["L0_ffn_ln_g"], np.float32)
    lnb = np.asarray(inp["L0_ffn_ln_b"], np.float32)
    cols = []
    for l in range(2):
        cols.append(lng[l].reshape(8, 128).T)
        cols.append(lnb[l].reshape(8, 128).T)
    w["l0_lnc"] = np.ascontiguousarray(np.concatenate(cols, axis=1))
    w["l1_wqkv"] = np.ascontiguousarray(np.asarray(inp["L1_attn_Wqkv"], np.float32))
    w["l1_wout"] = np.ascontiguousarray(np.asarray(inp["L1_attn_Wout"], np.float32))
    w["l1_bqc"] = np.ascontiguousarray(np.asarray(inp["L1_attn_bqkv"], np.float32).reshape(24, 128).T)
    w["l1_boc"] = np.ascontiguousarray(np.asarray(inp["L1_attn_bout"], np.float32).reshape(8, 128).T)
    w["gate_wT"] = np.ascontiguousarray(np.asarray(inp["L1_ffn_gate_w"], np.float32).T)
    ew = np.asarray(inp["L1_ffn_expert_w"], np.float32).transpose(0, 2, 1)
    for nm in ("wqbig", "wkbig", "wvbig"):
        w[nm] = _timg(w[nm])
    w["l0_wqkv"] = np.stack([_timg(w["l0_wqkv"][l]) for l in range(2)])
    w["l0_wo"] = np.stack([_timg(w["l0_wo"][l]) for l in range(2)])
    w["l0_w1re"] = np.stack([_timg(w["l0_w1re"][l]) for l in range(2)])
    w["l0_w2p"] = np.stack([_timg(w["l0_w2p"][l]) for l in range(2)])
    w["l1_wqkv"] = _timg(w["l1_wqkv"])
    w["l1_wout"] = _timg(w["l1_wout"])
    w["ew_T"] = np.stack([_timg(ew[e]) for e in range(8)])
    return w



class KB:
    EKEYS = ("T", "V", "S", "P", "Y")

    def __init__(self, nc):
        self.nc = nc
        self.q = {k: [] for k in self.EKEYS}
        self.c = dict(T=0, V=0, S=0, P=0, dW0=0, dW1=0, dY0=0, dY1=0, dZ0=0, dZ1=0, dT0=0, dT1=0, dP0=0, dP1=0)
        self.sem = {}

    def dma_begin(self, ek, dk):
        self.wait(ek, dk, self.c[dk])

    def raw(self, ek, fn):
        self.q[ek].append(fn)

    def wait(self, ek, sk, val):
        if val <= 0:
            return
        self.q[ek].append(lambda e, sk=sk, v=val: e.wait_ge(self.sem[sk], v))

    def op(self, ek, fn):
        self.c[ek] += 1
        def run(e, fn=fn, ek=ek):
            fn(e).then_inc(self.sem[ek], 1)
        self.q[ek].append(run)
        return self.c[ek]

    def sop(self, ek, fn):
        self.q[ek].append(lambda e, fn=fn: fn(e))
        return self.c[ek]

    def dma(self, ek, dk, fn):
        self.c[dk] += 16
        def run(e, fn=fn, dk=dk):
            fn(e).then_inc(self.sem[dk], 16)
        self.q[ek].append(run)
        return self.c[dk]

    def snap(self):
        return dict(self.c)

    def barrier_from(self, ek, snap, keys=None):
        for k in (keys or ("T", "V", "S", "P", "dW0", "dW1", "dY0", "dY1", "dZ0", "dZ1", "dT0", "dT1", "dP0", "dP1")):
            self.wait(ek, k, snap[k])

    def full_barrier(self, keys=("T", "V", "S", "P", "Y")):
        s = self.snap()
        for ek in keys:
            self.barrier_from(ek, s)
        return s



def softmax_row(kb, E, psrow, width, pout, exp_scale_key=None, skip_max=False):
    nmx = E["NMX"]
    nms = E["NMS"]
    sms = E["SMS"]
    rcp = E["RCP"]
    if skip_max:
        kb.wait("S", "V", kb.c["V"])
        nS = kb.op("S", lambda s: s.activation(
            out=pout, in_=psrow, func=AF.Exp, accum_out=sms[:, 0:1]))
        kb.wait("V", "S", nS)
        nV = kb.op("V", lambda v: v.reciprocal(out=rcp[:, :], in_=sms[:, :]))
        kb.wait("V", "V", nV)
        nV = kb.op("V", lambda v: v.tensor_scalar(out=pout, in0=pout, scalar1=rcp[:, 0:1],
                                                  scalar2=None, op0=ALU.mult))
        return nV
    nV = kb.op("V", lambda v: v.reduce_max(out=nmx[:, 0:1], in_=psrow, axis=AX.X, negate=True))
    kb.wait("S", "V", nV)
    if exp_scale_key is not None:
        scale = E["consts_f"][exp_scale_key]
        nS = kb.op("S", lambda s, sc=scale: s.mul(out=nms[:, 0:1], in_=nmx[:, 0:1], mul=sc))
        kb.wait("S", "S", nS)
        bias_ap = nms[:, 0:1]
    else:
        scale = 1.0
        bias_ap = nmx[:, 0:1]
    nS = kb.op("S", lambda s, b=bias_ap, sc=scale: s.activation(
        out=pout, in_=psrow, func=AF.Exp, bias=b, scale=sc, accum_out=sms[:, 0:1]))
    kb.wait("V", "S", nS)
    nV = kb.op("V", lambda v: v.reciprocal(out=rcp[:, 0:1], in_=sms[:, 0:1]))
    kb.wait("V", "V", nV)
    nV = kb.op("V", lambda v: v.tensor_scalar(out=pout, in0=pout, scalar1=rcp[:, 0:1],
                                              scalar2=None, op0=ALU.mult))
    return nV


def ln_stats_apply(kb, E, xslices, width, pp=0, gb_cols=None, bf=False, defer=False):
    SCB = E["SCB"]
    SQB = E["SQB"]
    SR = E["SR"]
    PS_S = E["PS_S"]
    ONEC = E["ONEC"]
    ONE1F = E["ONE1F"]
    bA = PS_S[:, 2 * pp * width:(2 * pp + 1) * width]
    bB = PS_S[:, (2 * pp + 1) * width:(2 * pp + 2) * width]
    st = E.setdefault("_lnst", {"apply_v": [0, 0], "bcast_t": [0, 0]})
    tdone = []
    for ft in range(8):
        h = (ft % 2) * width
        if ft >= 2:
            kb.wait("S", "T", tdone[ft - 2])
        if not bf:
            kb.op("S", lambda s, ft=ft, h=h: s.copy(out=SCB[:, h:h + width], in_=xslices[ft]))
            vsrc = SCB[:, h:h + width]
        else:
            vsrc = xslices[ft]
        nS2 = kb.op("S", lambda s, ft=ft, h=h: s.activation(
            out=SQB[:, h:h + width], in_=xslices[ft], func=AF.Square))
        kb.wait("T", "S", nS2)
        if ft == 0:
            kb.wait("T", "V", st["apply_v"][pp])
        kb.op("T", lambda t, ft=ft, vsrc=vsrc: t.matmul(
            bA[0:1, 0:width], ONEC[:, :], vsrc, start=(ft == 0), stop=(ft == 7)))
        n2 = kb.op("T", lambda t, ft=ft, h=h: t.matmul(
            bB[0:1, 0:width], ONEC[:, :], SQB[:, h:h + width], start=(ft == 0), stop=(ft == 7)))
        tdone.append(n2)
    mu = SR[0:1, pp * 1024:pp * 1024 + width]
    tmp = SR[0:1, pp * 1024 + 512:pp * 1024 + 512 + width]
    kb.wait("S", "T", tdone[-1])
    kb.wait("S", "T", st["bcast_t"][pp])
    s_mu = kb.op("S", lambda s: s.activation(out=mu, in_=bA[0:1, 0:width],
                                             func=AF.Copy, scale=E["consts_f"]["inv1024"]))
    kb.wait("S", "S", s_mu)
    s_m2 = kb.op("S", lambda s: s.activation(out=tmp, in_=mu, func=AF.Square))
    kb.wait("V", "S", s_m2)
    v_var = kb.op("V", lambda v: v.scalar_tensor_tensor(
        out=tmp, in0=bB[0:1, 0:width], scalar=E["consts_f"]["inv1024"],
        in1=tmp, op0=ALU.mult, op1=ALU.subtract))
    kb.wait("V", "V", v_var)
    v_ve = kb.op("V", lambda v: v.tensor_scalar(out=tmp, in0=tmp, scalar1=1e-5,
                                                scalar2=None, op0=ALU.add))
    kb.wait("V", "V", v_ve)
    v_rc = kb.op("V", lambda v: v.reciprocal(out=tmp, in_=tmp))
    kb.wait("S", "V", v_rc)
    s_rs = kb.op("S", lambda s: s.activation(out=tmp, in_=tmp, func=AF.Sqrt))

    def finish():
        kb.wait("T", "S", s_rs)
        kb.op("T", lambda t: t.matmul(bA, ONE1F[:, :], mu, start=True, stop=True))
        brs = kb.op("T", lambda t: t.matmul(bB, ONE1F[:, :], tmp, start=True, stop=True))
        st["bcast_t"][pp] = brs
        kb.wait("V", "T", brs)
        nsub = 0
        for ft in range(8):
            nsub = kb.op("V", lambda v, ft=ft: v.tensor_tensor(out=xslices[ft], in0=xslices[ft],
                                                               in1=bA, op=ALU.subtract))
        kb.wait("V", "V", nsub)
        nV = 0
        for ft in range(8):
            nV = kb.op("V", lambda v, ft=ft: v.tensor_tensor(out=xslices[ft], in0=xslices[ft],
                                                             in1=bB, op=ALU.mult))
        if gb_cols is not None:
            kb.wait("V", "V", nV)
            for ft in range(8):
                gcols, bcols = gb_cols
                nV = kb.op("V", lambda v, ft=ft, gcols=gcols, bcols=bcols: v.tensor_scalar(
                    out=xslices[ft], in0=xslices[ft], scalar1=gcols[ft], scalar2=bcols[ft],
                    op0=ALU.mult, op1=ALU.add))
        st["apply_v"][pp] = nV
        return nV

    if defer:
        return finish
    return finish()


def _load_xt(kb, E, b, also_abf=True, also_x=True):
    X, ABF, XR, PS_S = E["X"], E["ABF"], E["XR"], E["PS_S"]
    x_dram = E["x"]
    vdone = [0] * QT
    sdone = [0] * QT
    for tt in range(QT):
        dk = "dY%d" % (tt % 2)
        if tt >= 2:
            kb.wait("Y", "V", vdone[tt - 2])
            kb.wait("Y", "S", sdone[tt - 2])
        kb.dma_begin("Y", dk)
        n = kb.dma("Y", dk, lambda sy, tt=tt: sy.dma_start(
            XR[tt % 2][:, :], x_dram[b, tt * 128:(tt + 1) * 128, :]))
        kb.wait("T", dk, n)
        if tt >= 2:
            kb.wait("T", "V", vdone[tt - 2])
            kb.wait("T", "S", sdone[tt - 2])
        ttr = 0
        for ft in range(8):
            ttr = kb.op("T", lambda t, tt=tt, ft=ft: t.transpose(
                PS_S[:, ft * 128 + (tt % 2) * 1024: ft * 128 + (tt % 2) * 1024 + 128],
                XR[tt % 2][:, ft * 128:(ft + 1) * 128], E["IDF"][:, :]))
        kb.wait("V", "T", ttr)
        kb.wait("S", "T", ttr)
        for ft in range(8):
            src = PS_S[:, ft * 128 + (tt % 2) * 1024: ft * 128 + (tt % 2) * 1024 + 128]
            if ft < 4:
                if also_x:
                    vdone[tt] = kb.op("V", lambda v, ft=ft, tt=tt, src=src: v.tensor_copy(
                        out=X[ft][:, tt * 128:(tt + 1) * 128], in_=src))
                if also_abf:
                    vdone[tt] = kb.op("V", lambda v, ft=ft, tt=tt, src=src: v.tensor_copy(
                        out=ABF[ft][:, tt * 128:(tt + 1) * 128], in_=src))
            else:
                if also_x:
                    sdone[tt] = kb.op("S", lambda s, ft=ft, tt=tt, src=src: s.copy(
                        out=X[ft][:, tt * 128:(tt + 1) * 128], in_=src))
                if also_abf:
                    sdone[tt] = kb.op("S", lambda s, ft=ft, tt=tt, src=src: s.copy(
                        out=ABF[ft][:, tt * 128:(tt + 1) * 128], in_=src))
    kb.full_barrier()


def _wtile_ap(base_ap, j0, rows, ntile=1):
    return bass.AP(base_ap.tensor, base_ap.offset + j0 * 128 * rows,
                   [[rows, 128], [128 * rows, ntile], [1, rows]])


def _stream_w_cols(kb, E, base_ap, col0, rows, ctot, half, dest=None, ncols=128):
    re = _wtile_ap(base_ap, col0 // 128, rows)
    if dest is None:
        dest = E["WB"][:, half * 1024:half * 1024 + rows]
    return kb.dma("Y", "dZ%d" % half, lambda sy, re=re, dest=dest: sy.dma_start(dest, re))


def _wstream_begin(kb, half):
    kb.dma_begin("Y", "dZ%d" % half)



def _rope_pair(kb, E, ps, p, tblC, tblS, out_ap, swap_lhsT):
    SCB = E["SCB"]
    PBF = E["PBF"]
    PS_M = E["PS_M"]
    o = p * CH
    nS = kb.op("S", lambda s: s.copy(out=SCB[:, o:o + CH], in_=ps))
    kb.wait("T", "S", nS)
    nJ = kb.op("T", lambda t: t.matmul(PS_M[:, p * CH:(p + 1) * CH], swap_lhsT, SCB[:, o:o + CH],
                                       start=True, stop=True))
    kb.wait("V", "S", nS)
    nV = kb.op("V", lambda v: v.tensor_tensor(out=PBF[:, o:o + CH], in0=ps, in1=tblC,
                                              op=ALU.mult))
    kb.wait("V", "T", nJ)
    kb.wait("V", "V", nV)
    nV = kb.op("V", lambda v: v.tensor_tensor(out=SCB[:, o:o + CH], in0=PS_M[:, p * CH:(p + 1) * CH],
                                              in1=tblS, op=ALU.mult))
    kb.wait("V", "V", nV)
    nV = kb.op("V", lambda v: v.tensor_tensor(out=out_ap, in0=SCB[:, o:o + CH],
                                              in1=PBF[:, o:o + CH], op=ALU.add))
    return nV, nJ, nS


def ph_l0a(kb, E, b):
    X, ABF, KT, VT, QJ, WB = E["X"], E["ABF"], E["KT"], E["VT"], E["QJ"], E["WB"]
    PBF, PT = E["PBF"], E["PT"]
    TB0, TB1 = E["TB0"], E["TB1"]
    PS_S, PS_T, PS_M = E["PS_S"], E["PS_T"], E["PS_M"]

    kb.full_barrier()
    if b == 0:
        fa = E.get("pc_fenceA")
        if fa:
            kb.wait("Y", "dP0", fa[0])
            kb.wait("Y", "dP1", fa[1])
    _load_xt(kb, E, b)
    if L0A_STAGES < 2:
        return

    rope_v = [0, 0]
    rope_s = [0, 0]
    wb_mm = [0, 0]
    for ft in range(8):
        half = ft % 2
        dk = "dZ%d" % half
        kb.wait("Y", "T", wb_mm[half])
        _wstream_begin(kb, half)
        nW = _stream_w_cols(kb, E, E["wkbig"], ft * 128, D, D, half)
        kb.wait("T", dk, nW)
        for c in range(NCH):
            p = c % 2
            tk = "dT%d" % p
            kb.wait("Y", "V", rope_v[p])
            kb.dma_begin("Y", tk)
            kb.dma("Y", tk, lambda sy, ft=ft, c=c, p=p: sy.dma_start(
                TB0[:, p * CH:(p + 1) * CH],
                E["ropeCb"][ft * 128:(ft + 1) * 128, c * CH:(c + 1) * CH]))
            nTb = kb.dma("Y", tk, lambda sy, ft=ft, c=c, p=p: sy.dma_start(
                TB1[:, p * CH:(p + 1) * CH],
                E["ropeSb"][ft * 128:(ft + 1) * 128, c * CH:(c + 1) * CH]))
            ps = PS_S[:, p * CH:(p + 1) * CH]
            kb.wait("T", "V", rope_v[p])
            kb.wait("T", "S", rope_s[p])
            nT = 0
            for kt in range(8):
                nT = kb.op("T", lambda t, kt=kt, c=c, half=half, ps=ps: t.matmul(
                    ps, WB[:, half * 1024 + kt * 128: half * 1024 + kt * 128 + 128],
                    ABF[kt][:, c * CH:(c + 1) * CH], start=(kt == 0), stop=(kt == 7)))
            wb_mm[half] = nT
            kb.wait("S", "T", nT)
            kb.wait("S", "V", rope_v[p])
            kb.wait("V", tk, nTb)
            nV, nJ, nSc = _rope_pair(kb, E, ps, p, TB0[:, p * CH:(p + 1) * CH],
                                     TB1[:, p * CH:(p + 1) * CH],
                                     KT[ft][:, c * CH:(c + 1) * CH], E["JTBb"][:, :])
            rope_v[p] = nV
            rope_s[p] = nSc
    kb.full_barrier()

    wb_mm = [0, 0]
    ps_ev = [0, 0]
    for ft in range(8):
        half = ft % 2
        dk = "dZ%d" % half
        kb.wait("Y", "T", wb_mm[half])
        _wstream_begin(kb, half)
        nW = _stream_w_cols(kb, E, E["wvbig"], ft * 128, D, D, half)
        kb.wait("T", dk, nW)
        for tt4 in range(4):
            pb = tt4 % 2
            ps = PS_S[:, pb * CH:pb * CH + CH]
            kb.wait("T", "V", ps_ev[pb])
            nT = 0
            for sub in range(4):
                tt = tt4 * 4 + sub
                for kt in range(8):
                    nT = kb.op("T", lambda t, kt=kt, tt=tt, sub=sub, half=half, ps=ps: t.matmul(
                        ps[:, sub * 128:(sub + 1) * 128],
                        ABF[kt][:, tt * 128:(tt + 1) * 128],
                        WB[:, half * 1024 + kt * 128: half * 1024 + kt * 128 + 128],
                        start=(kt == 0), stop=(kt == 7)))
            wb_mm[half] = nT
            kb.wait("V", "T", nT)
            nE = 0
            for sub in range(4):
                tt = tt4 * 4 + sub
                nE = kb.op("V", lambda v, tt=tt, sub=sub, ft=ft, ps=ps: v.tensor_copy(
                    out=VT[tt][:, ft * 128:(ft + 1) * 128], in_=ps[:, sub * 128:(sub + 1) * 128]))
            ps_ev[pb] = nE
    kb.full_barrier()
    if L0A_STAGES < 4:
        return

    for g in range(4):
        jbmax = 4 * (g + 1)
        rope_v = [0, 0]
        rope_s = [0, 0]
        for ft in range(8):
            half = ft % 2
            p = half
            dk = "dZ%d" % half
            kb.wait("Y", "V", rope_v[p])
            _wstream_begin(kb, half)
            nW = _stream_w_cols(kb, E, E["wqbig"], ft * 128, D, D, half)
            nC = kb.dma("Y", dk, lambda sy, ft=ft, g=g, p=p: sy.dma_start(
                TB0[:, p * CH:(p + 1) * CH],
                E["ropeCb"][ft * 128:(ft + 1) * 128, g * CH:(g + 1) * CH]))
            nS_ = kb.dma("Y", dk, lambda sy, ft=ft, g=g, p=p: sy.dma_start(
                TB1[:, p * CH:(p + 1) * CH],
                E["ropeSb"][ft * 128:(ft + 1) * 128, g * CH:(g + 1) * CH]))
            kb.wait("T", dk, nS_)
            kb.wait("T", "V", rope_v[p])
            kb.wait("T", "S", rope_s[p])
            ps = PS_S[:, p * CH:(p + 1) * CH]
            nT = 0
            for kt in range(8):
                nT = kb.op("T", lambda t, kt=kt, g=g, half=half, ps=ps: t.matmul(
                    ps, WB[:, half * 1024 + kt * 128: half * 1024 + kt * 128 + 128],
                    ABF[kt][:, g * CH:(g + 1) * CH], start=(kt == 0), stop=(kt == 7)))
            kb.wait("S", "T", nT)
            kb.wait("S", "V", rope_v[p])
            nV, nJ, nSc = _rope_pair(kb, E, ps, p, TB0[:, p * CH:(p + 1) * CH],
                                     TB1[:, p * CH:(p + 1) * CH], QJ[ft], E["JTBb"][:, :])
            rope_v[p] = nV
            rope_s[p] = nSc
        kb.full_barrier()
        if L0A_S4 < 2:
            continue

        for iq in range(4 * g, 4 * g + 4):
            jext = (iq + 1) * 128
            nchq = (jext + CH - 1) // CH
            kb.wait("T", "S", kb.c["S"])
            kb.wait("T", "V", kb.c["V"])
            nT = 0
            chunks = []
            c0 = 0
            while c0 < jext - 128:
                w_ = min(CH, jext - 128 - c0)
                chunks.append((c0, w_, False))
                c0 += w_
            chunks.append((jext - 128, 128, True))
            for (c0, wdt, diag) in chunks:
                for dt in range(8):
                    nT = kb.op("T", lambda t, c0=c0, dt=dt, iq=iq, g=g, wdt=wdt, diag=diag: t.matmul(
                        PS_S[:, c0:c0 + wdt],
                        QJ[dt][:, (iq - 4 * g) * 128:(iq - 4 * g) * 128 + 128],
                        KT[dt][:, c0 // 128 * 128:c0 + wdt] if False else KT[dt][:, c0:c0 + wdt],
                        start=(dt == 0), stop=(dt == 7 and not diag)))
                if diag:
                    nT = kb.op("T", lambda t, c0=c0: t.matmul(
                        PS_S[:, c0:c0 + 128], E["M4B"][:, :], E["IDB"][:, :],
                        start=False, stop=True))
            kb.wait("V", "T", nT)
            if L0A_S4 < 3:
                continue
            nV = softmax_row(kb, E, PS_S[:, 0:jext], jext, PBF[:, 0:jext], exp_scale_key="inv32")
            if L0A_S4 < 4:
                continue
            kb.wait("T", "V", nV)
            kb.wait("T", "S", kb.c["S"])
            scopy = [0, 0]
            ngrp = (iq + 1 + 3) // 4
            for gb in range(ngrp):
                bank = gb % 2
                j0 = gb * 4
                j1 = min(iq + 1, j0 + 4)
                if gb >= 2:
                    kb.wait("T", "S", scopy[bank])
                nTt = 0
                for jb in range(j0, j1):
                    nTt = kb.op("T", lambda t, jb=jb, bank=bank, j0=j0: t.transpose(
                        PS_T[:, bank * 1024 + (jb - j0) * 128: bank * 1024 + (jb - j0) * 128 + 128],
                        PBF[:, jb * 128:(jb + 1) * 128], E["IDB"][:, :]))
                kb.wait("S", "T", nTt)
                scopy[bank] = kb.op("S", lambda s, j0=j0, j1=j1, bank=bank: s.copy(
                    out=PT[:, j0 * 128:j1 * 128],
                    in_=PS_T[:, bank * 1024: bank * 1024 + (j1 - j0) * 128]))
            if L0A_S4 < 5:
                continue
            kb.wait("T", "S", kb.c["S"])
            aav = []
            for ft in range(8):
                ps = PS_S[:, ft * 128:(ft + 1) * 128]
                nT = 0
                for jb in range(iq + 1):
                    nT = kb.op("T", lambda t, jb=jb, ft=ft, iq=iq, ps=ps: t.matmul(
                        ps, VT[jb][:, ft * 128:(ft + 1) * 128], PT[:, jb * 128:(jb + 1) * 128],
                        start=(jb == 0), stop=(jb == iq)))
                aav.append(nT)
            kb.wait("V", "T", aav[-1])
            for ft in range(8):
                kb.op("V", lambda v, ft=ft, iq=iq: v.tensor_tensor(
                    out=X[ft][:, iq * 128:(iq + 1) * 128], in0=X[ft][:, iq * 128:(iq + 1) * 128],
                    in1=PS_S[:, ft * 128:(ft + 1) * 128], op=ALU.add))
        kb.full_barrier()
        if L0A_S4 < 6:
            continue
        xs = [X[ft][:, g * CH:(g + 1) * CH] for ft in range(8)]
        ln_stats_apply(kb, E, xs, CH, pp=g % 2)
        kb.full_barrier()


def ph_store(kb, E, b):
    X = E["X"]
    PS_S = E["PS_S"]
    XR = E["XR"]
    y = E["y"]
    kb.full_barrier()
    for tt in range(QT):
        nT = 0
        buf = tt % 2
        dk = "dY%d" % buf
        if tt >= 2:
            kb.wait("T", dk, kb.c[dk])
        for ft in range(8):
            nT = kb.op("T", lambda t, tt=tt, ft=ft, buf=buf: t.transpose(
                PS_S[:, ft * 128 + buf * 1024: ft * 128 + buf * 1024 + 128],
                X[ft][:, tt * 128:(tt + 1) * 128], E["IDF"][:, :]))
        kb.wait("V", "T", nT)
        kb.wait("S", "T", nT)
        nV = kb.op("V", lambda v, buf=buf: v.tensor_copy(
            out=XR[buf][:, 0:CH], in_=PS_S[:, buf * 1024:buf * 1024 + CH]))
        nS = kb.op("S", lambda s, buf=buf: s.copy(
            out=XR[buf][:, CH:1024], in_=PS_S[:, buf * 1024 + CH:buf * 1024 + 1024]))
        kb.wait("Y", "V", nV)
        kb.wait("Y", "S", nS)
        kb.dma_begin("Y", dk)
        kb.dma("Y", dk, lambda sy, tt=tt, buf=buf: sy.dma_start(
            y[b, tt * 128:(tt + 1) * 128, :], XR[buf][:, :]))
    kb.full_barrier()


def ph_load(kb, E, b):
    kb.full_barrier()
    _load_xt(kb, E, b)



PH_FUNCS = {}

def build_core_program(phases):
    nc = bass.Bass("TRN2", target_bir_lowering=False)
    C = _consts()

    dr = {}
    dr["x"] = nc.dram_tensor("x", [2, T, D], F32, kind="ExternalInput").ap()
    dr["y"] = nc.dram_tensor("y", [2, T, D], F32, kind="ExternalOutput").ap()
    for nm, shp in [("wqbig", [8, 128, D]), ("wkbig", [8, 128, D]), ("wvbig", [8, 128, D]),
                    ("ropeC", [D, T]), ("ropeS", [D, T]),
                    ("ident", [128, 128]), ("JTB", [128, 128]), ("RTB", [128, 128]),
                    ("M0", [128, 128]), ("M4", [128, 128]),
                    ("HS", [128, 2]), ("HST", [2, 128]), ("SELS", [8, 1024]),
                    ("xposQC", [128, T]), ("xposQS", [128, T]),
                    ("xposKC", [128, T]), ("xposKS", [128, T]),
                    ("l0_wqkv", [2, 12, 128, D]), ("l0_wo", [2, 8, 128, 512]),
                    ("l0_w1re", [2, 44, 128, D]), ("l0_w2p", [2, 8, 128, HIDP]),
                    ("l0_b1c", [128, 4 * NHT]), ("l0_b2c", [128, 16]),
                    ("l0_qksc", [128, 8]), ("l0_lnc", [128, 32]),
                    ("l1_wqkv", [24, 128, D]), ("l1_wout", [8, 128, D]),
                    ("l1_bqc", [128, 24]), ("l1_boc", [128, 8]),
                    ("gate_wT", [D, 8]), ("ew_T", [8, 8, 128, D])]:
        dr[nm] = nc.dram_tensor(nm, shp, F32, kind="ExternalInput").ap()
    dr["xscr"] = nc.dram_tensor("xscr", [8, 128, T], F32).ap()
    MIR = [("wkbig", [8, 128, D]), ("ropeC", [D, T]), ("ropeS", [D, T]),
           ("wvbig", [8, 128, D]), ("wqbig", [8, 128, D]),
           ("l0_wqkv", [2, 12, 128, D]), ("l0_wo", [2, 8, 128, 512]),
           ("l0_w1re", [2, 44, 128, D]), ("l0_w2p", [2, 8, 128, HIDP]),
           ("l1_wqkv", [24, 128, D]), ("l1_wout", [8, 128, D]), ("ew_T", [8, 8, 128, D])]
    mir = {}
    for nm, shp in MIR:
        mir[nm] = nc.dram_tensor(nm + "_bf", shp, BF16).ap()

    ctx = ExitStack()
    sb = lambda n, s, d: ctx.enter_context(nc.sbuf_tensor(n, s, d))
    psm = lambda n, s, d: ctx.enter_context(nc.psum_tensor(n, s, d))

    E = dict(nc=nc)
    E.update(dr)

    XU = sb("XU", [128, 8 * T], F32)
    ABFU = sb("ABFU", [128, 8 * T], BF16)
    KTU = sb("KTU", [128, 8 * T], BF16)
    VTU = sb("VTU", [128, 16 * D], BF16)
    TB0 = sb("TB0", [128, T], BF16)
    TB1 = sb("TB1", [128, T], BF16)
    WBt = sb("WBt", [128, 2048], BF16)
    QJt = sb("QJt", [128, 4096], BF16)
    PBFt = sb("PBFt", [128, T], BF16)
    PTt = sb("PTt", [128, T], BF16)
    SCBt = sb("SCBt", [128, 1024], BF16)
    SQBt = sb("SQBt", [128, 1024], BF16)
    FSCR = sb("FSCR", [128, T], F32)
    OEt = FSCR.bitcast(BF16)

    XUa, ABFa, KTa, VTa = XU.ap(), ABFU.ap(), KTU.ap(), VTU.ap()
    E["X"] = [XUa[:, i * T:(i + 1) * T] for i in range(8)]
    E["ABF"] = [ABFa[:, i * T:(i + 1) * T] for i in range(8)]
    E["KT"] = [KTa[:, i * T:(i + 1) * T] for i in range(8)]
    E["VT"] = [VTa[:, i * D:(i + 1) * D] for i in range(QT)]
    E["KTU"] = KTa
    E["ABFU"] = ABFa
    E["VTU"] = VTa
    E["TB0"], E["TB1"] = TB0.ap(), TB1.ap()
    E["TSF"] = SQBt.bitcast(F32).ap()
    E["WB"] = WBt.ap()
    qa = QJt.ap()
    E["QJ"] = [qa[:, i * CH:(i + 1) * CH] for i in range(8)]
    E["QJU"] = qa
    E["PBF"] = PBFt.ap()
    E["PT"] = PTt.ap()
    oa = OEt.ap()
    E["OE"] = [oa[:, i * CH:(i + 1) * CH] for i in range(8)]
    E["OEU"] = oa
    E["SCB"] = SCBt.ap()
    E["SQB"] = SQBt.ap()
    fa = FSCR.ap()
    E["XR"] = [fa[:, 0:1024], fa[:, 1024:2048]]
    E["SR"] = fa

    for nm, shp, d in [("IDF", [128, 128], F32), ("IDB", [128, 128], BF16),
                       ("JTBb", [128, 128], BF16), ("RTBb", [128, 128], BF16),
                       ("M0c", [128, 128], F32), ("M4c", [128, 128], F32),
                       ("HSb", [128, 2], BF16), ("HSTb", [2, 128], BF16),
                       ("SELSb", [8, 1024], BF16),
                       ("M0B", [128, 128], BF16), ("M4B", [128, 128], BF16),
                       ("BVB", [128, 8], BF16), ("BOC2", [128, 8], F32),
                       ("ONEC", [128, 1], BF16), ("ONE1F", [1, 128], F32),
                       ("ONEB", [1, 128], BF16),
                       ("QKSC", [128, 8], F32), ("LNC", [128, 32], F32),
                       ("B1C", [128, 4 * NHT], F32), ("B2C", [128, 16], F32),
                       ("BQC", [128, 24], F32), ("BOC", [128, 8], F32),
                       ("NMX", [128, 2], F32), ("NMS", [128, 2], F32),
                       ("SMS", [128, 2], F32), ("RCP", [128, 2], F32),
                       ("CP", [128, 4], F32)]:
        E[nm] = sb(nm, shp, d).ap()
    E["M0"], E["M4"] = E["M0c"], E["M4c"]

    E["PS_S"] = psm("PS_S", [128, 2048], F32).ap()
    _pst = psm("PS_T", [128, 2048], BF16)
    E["PS_T"] = _pst.ap()
    E["PS_TF"] = _pst.bitcast(F32).ap()
    E["PS_M"] = psm("PS_M", [128, 1024], F32).ap()

    E["consts_f"] = {"inv32": 1.0 / 32.0, "inv1024": 1.0 / 1024.0}
    nc.const_aps.register(F32, 1.0 / 32.0, E["CP"][:, 0:1])
    nc.const_aps.register(F32, 1.0 / 1024.0, E["CP"][:, 1:2])

    sems = {}
    for k in ("T", "V", "S", "P", "dW0", "dW1", "dY0", "dY1", "dZ0", "dZ1", "dT0", "dT1", "dP0", "dP1"):
        sems[k] = ctx.enter_context(nc.semaphore(f"sem{k}"))
    blk = ctx.enter_context(nc.Block())

    kb = KB(nc)
    kb.sem = sems

    nconst = 0
    for dst, src in [(E["IDF"], dr["ident"]), (E["M0c"], dr["M0"]), (E["M4c"], dr["M4"]),
                     (E["QKSC"], dr["l0_qksc"]), (E["LNC"], dr["l0_lnc"]),
                     (E["B1C"], dr["l0_b1c"]), (E["B2C"], dr["l0_b2c"]),
                     (E["BQC"], dr["l1_bqc"]), (E["BOC"], dr["l1_boc"])]:
        nconst = kb.dma("Y", "dY0", lambda sy, dst=dst, src=src: sy.dma_start(dst[:, :], src))
    nconst2 = 0
    for dst, src in [(E["IDB"], dr["ident"]), (E["JTBb"], dr["JTB"]), (E["RTBb"], dr["RTB"]),
                     (E["HSb"], dr["HS"]), (E["HSTb"], dr["HST"]), (E["SELSb"], dr["SELS"]),
                     (E["M0B"], dr["M0"]), (E["M4B"], dr["M4"])]:
        nconst2 = kb.dma("P", "dW0", lambda g, dst=dst, src=src: g.dma_start(dst[:, :], src))
    kb.sop("P", lambda g: g.memset(E["ONEC"][:, :], 1.0))
    kb.sop("P", lambda g: g.memset(E["ONEB"][:, :], 1.0))
    kb.sop("P", lambda g: g.memset(E["ONE1F"][:, :], 1.0))
    kb.sop("P", lambda g: g.memset(E["CP"][:, 0:1], 1.0 / 32.0))
    nP = kb.op("P", lambda g: g.memset(E["CP"][:, 1:2], 1.0 / 1024.0))
    for ek in ("T", "V", "S"):
        kb.wait(ek, "dY0", nconst)
        kb.wait(ek, "dW0", nconst2)
        kb.wait(ek, "P", nP)
    kb.wait("Y", "dY0", nconst)
    kb.wait("P", "dW0", nconst2)


    KTUa = E["KTU"]
    pc_half = [0, 0]
    hcur = 0
    pc_fence = {}
    for nm, shp in MIR:
        nel = 1
        for s in shp:
            nel *= s
        ctot = nel // 128
        c0 = 0
        while c0 < ctot:
            w = min(8192, ctot - c0)
            h = hcur
            hcur ^= 1
            kb.wait("P", "dP%d" % h, pc_half[h])
            kb.dma_begin("P", "dW%d" % h)
            fin = bass.AP(dr[nm].tensor, c0, [[ctot, 128], [1, w]])
            nin = kb.dma("P", "dW%d" % h, lambda g, fin=fin, h=h, w=w: g.dma_start(
                KTUa[:, h * 8192:h * 8192 + w], fin))
            kb.wait("P", "dW%d" % h, nin)
            fout = bass.AP(mir[nm].tensor, c0, [[ctot, 128], [1, w]])
            pc_half[h] = kb.dma("P", "dP%d" % h, lambda g, fout=fout, h=h, w=w: g.dma_start(
                fout, KTUa[:, h * 8192:h * 8192 + w]))
            c0 += w
        pc_fence[nm] = (kb.c["dP0"], kb.c["dP1"])
    E["pc_fenceA"] = pc_fence["wqbig"]
    E["pc_fenceB"] = pc_fence[MIR[-1][0]]
    for nm, shp in MIR:
        E[nm if nm not in ("ropeC", "ropeS") else nm + "b"] = mir[nm]

    for b in range(2):
        for pi, ph in enumerate(phases):
            if pi == 0 and ph != "l0a":
                ph_load(kb, E, b)
            PH_FUNCS[ph](kb, E, b)
        ph_store(kb, E, b)

    kb.wait("Y", "dY0", kb.c["dY0"])
    kb.wait("Y", "dY1", kb.c["dY1"])

    def mk(ek):
        def sec(eng):
            for fn in kb.q[ek]:
                fn(eng)
        return sec
    blk.tensor(mk("T"))
    blk.vector(mk("V"))
    blk.scalar(mk("S"))
    blk.gpsimd(mk("P"))
    blk.sync(mk("Y"))

    ctx.close()
    return nc


PH_FUNCS["l0a"] = ph_l0a
PH_FUNCS["nop"] = lambda kb, E, b: None


L0A_STAGES = 99
L0A_S4 = 99





def _cast_shift_abf(kb, E, shift):
    X, ABF = E["X"], E["ABF"]
    for ft in range(8):
        ek = "V" if ft < 4 else "S"
        if shift and ft >= 4:
            nz = kb.op("V", lambda v, ft=ft: v.memset(ABF[ft][:, 0:2], 0.0))
            if ek == "V":
                kb.wait("V", "V", nz)
                kb.op("V", lambda v, ft=ft: v.tensor_copy(out=ABF[ft][:, 1:T],
                                                          in_=X[ft][:, 0:T - 1]))
            else:
                kb.wait("S", "V", nz)
                kb.op("S", lambda s, ft=ft: s.copy(out=ABF[ft][:, 1:T], in_=X[ft][:, 0:T - 1]))
        else:
            if ek == "V":
                kb.op("V", lambda v, ft=ft: v.tensor_copy(out=ABF[ft][:, :], in_=X[ft][:, :]))
            else:
                kb.op("S", lambda s, ft=ft: s.copy(out=ABF[ft][:, :], in_=X[ft][:, :]))
    kb.full_barrier()


def _fold_xpos(kb, E, cos_dram, sin_dram, scol, rcol):
    TSF, TB0, TB1 = E["TSF"], E["TB0"], E["TB1"]
    QKSC = E["QKSC"]
    for c in range(NCH):
        for (dram, tb, col) in ((cos_dram, TB0, scol), (sin_dram, TB1, rcol)):
            kb.wait("Y", "V", kb.c["V"])
            kb.dma_begin("Y", "dY0")
            n = kb.dma("Y", "dY0", lambda sy, dram=dram, c=c: sy.dma_start(
                TSF[:, :], dram[:, c * CH:(c + 1) * CH]))
            kb.wait("V", "dY0", n)
            kb.op("V", lambda v, tb=tb, c=c, col=col: v.tensor_scalar(
                out=tb[:, c * CH:(c + 1) * CH], in0=TSF[:, :], scalar1=QKSC[:, col:col + 1],
                scalar2=None, op0=ALU.mult))
    kb.full_barrier()


def _l0f_qk_tiles(kb, E, l, which):
    X, ABF, KT, WB = E["X"], E["ABF"], E["KT"], E["WB"]
    TB0, TB1 = E["TB0"], E["TB1"]
    SCB, SQB, SR, PBF = E["SCB"], E["SQB"], E["SR"], E["PBF"]
    PS_S, PS_M = E["PS_S"], E["PS_M"]
    wbase = E["l0_wqkv"][l]
    units = [(ft, c) for ft in range(4) for c in range(NCH)]
    wb_mm = [0, 0]
    sq_n = {}
    out_n = {}
    qn_n = {}
    t2_n = {}
    recb_n = {}
    st_w = {}

    def emit_tailA(u):
        ft, c = units[u]
        p = u % 2
        kb.wait("T", "S", sq_n[u])
        if u >= 2 and (u - 2) in qn_n:
            kb.wait("T", "V", qn_n[u - 2])
        nrm = kb.op("T", lambda t, p=p: t.matmul(
            PS_M[0:2, p * CH:(p + 1) * CH], E["HSb"][:, :], SQB[:, p * CH:(p + 1) * CH],
            start=True, stop=True))
        kb.wait("S", "T", nrm)
        nsq = kb.op("S", lambda s, p=p: s.activation(
            out=SR[0:2, p * CH:(p + 1) * CH], in_=PS_M[0:2, p * CH:(p + 1) * CH],
            func=AF.Sqrt))
        kb.wait("V", "S", nsq)
        nv = kb.op("V", lambda v, p=p: v.tensor_scalar(
            out=SR[0:2, p * CH:(p + 1) * CH], in0=SR[0:2, p * CH:(p + 1) * CH],
            scalar1=1e-12, scalar2=None, op0=ALU.max))
        kb.wait("V", "V", nv)
        nv = kb.op("V", lambda v, p=p: v.reciprocal(
            out=SR[0:2, p * CH:(p + 1) * CH], in_=SR[0:2, p * CH:(p + 1) * CH]))
        kb.wait("V", "V", nv)
        nrc = kb.op("V", lambda v, p=p: v.tensor_copy(
            out=SQB[0:2, p * CH:p * CH + CH], in_=SR[0:2, p * CH:(p + 1) * CH]))
        recb_n[u] = nrc

    def emit_tailB(u):
        ft, c = units[u]
        p = u % 2
        kb.wait("T", "V", recb_n[u])
        nbc = kb.op("T", lambda t, p=p: t.matmul(
            PS_M[:, p * CH:(p + 1) * CH], E["HSTb"][:, :], SQB[0:2, p * CH:p * CH + CH],
            start=True, stop=True))
        kb.wait("V", "T", nbc)
        if u >= 2 and (u - 2) in t2_n:
            kb.wait("V", "V", t2_n[u - 2])
        nqn = kb.op("V", lambda v, p=p: v.tensor_tensor(
            out=PBF[:, p * CH:(p + 1) * CH], in0=SCB[:, p * CH:(p + 1) * CH],
            in1=PS_M[:, p * CH:(p + 1) * CH], op=ALU.mult))
        qn_n[u] = nqn
        kb.wait("T", "V", nqn)
        if u >= 2 and (u - 2) in t2_n:
            kb.wait("T", "V", t2_n[u - 2])
        nrot = kb.op("T", lambda t, p=p: t.matmul(
            PS_S[:, (2 + p) * CH:(3 + p) * CH], E["RTBb"][:, :], PBF[:, p * CH:(p + 1) * CH],
            start=True, stop=True))
        kb.wait("V", "V", nqn)
        nt1 = kb.op("V", lambda v, c=c, p=p: v.tensor_tensor(
            out=PBF[:, 1024 + p * CH:1024 + (p + 1) * CH], in0=PBF[:, p * CH:(p + 1) * CH],
            in1=TB0[:, c * CH:(c + 1) * CH], op=ALU.mult))
        kb.wait("V", "T", nrot)
        kb.wait("V", "V", nt1)
        nt2 = kb.op("V", lambda v, c=c, p=p: v.tensor_tensor(
            out=SQB[:, p * CH:(p + 1) * CH], in0=PS_S[:, (2 + p) * CH:(3 + p) * CH],
            in1=TB1[:, c * CH:(c + 1) * CH], op=ALU.mult))
        t2_n[u] = nt2
        kb.wait("V", "V", nt2)
        out_n[u] = kb.op("V", lambda v, ft=ft, c=c, which=which, p=p: v.tensor_tensor(
            out=KT[which * 4 + ft][:, c * CH:(c + 1) * CH],
            in0=SQB[:, p * CH:(p + 1) * CH],
            in1=PBF[:, 1024 + p * CH:1024 + (p + 1) * CH], op=ALU.add))

    for u, (ft, c) in enumerate(units):
        p = u % 2
        if c == 0:
            half = ft % 2
            kb.wait("Y", "T", wb_mm[half])
            _wstream_begin(kb, half)
            nW = _stream_w_cols(kb, E, wbase, which * 512 + ft * 128, D, 1536, half)
            st_w[ft] = (half, nW)
        half, nW = st_w[ft]
        if c == 0:
            kb.wait("T", "dZ%d" % half, nW)
        ps = PS_S[:, p * CH:(p + 1) * CH]
        if u >= 2:
            kb.wait("T", "S", sq_n[u - 2])
        nT = 0
        for kt in range(8):
            nT = kb.op("T", lambda t, kt=kt, c=c, half=half, ps=ps: t.matmul(
                ps, WB[:, half * 1024 + kt * 128: half * 1024 + kt * 128 + 128],
                ABF[kt][:, c * CH:(c + 1) * CH], start=(kt == 0), stop=(kt == 7)))
        wb_mm[half] = nT
        kb.wait("S", "T", nT)
        if u >= 2:
            emit_tailB(u - 2)
        if u >= 2 and (u - 2) in qn_n:
            kb.wait("S", "V", qn_n[u - 2])
        kb.op("S", lambda s, ps=ps, p=p: s.copy(out=SCB[:, p * CH:(p + 1) * CH], in_=ps))
        if u >= 2 and (u - 2) in out_n:
            kb.wait("S", "V", out_n[u - 2])
        sq_n[u] = kb.op("S", lambda s, ps=ps, p=p: s.activation(
            out=SQB[:, p * CH:(p + 1) * CH], in_=ps, func=AF.Square))
        if u >= 1:
            emit_tailA(u - 1)
    emit_tailA(len(units) - 1)
    emit_tailB(len(units) - 2)
    emit_tailB(len(units) - 1)
    kb.full_barrier()

def ln_stats_apply_bf(kb, E, bfslices, width, gcols, bcols, pp=0, defer=False):
    return ln_stats_apply(kb, E, bfslices, width, pp=pp, gb_cols=(gcols, bcols), bf=True,
                          defer=defer)

def ph_l0fs(kb, E, b):
    kb.full_barrier()
    if b == 0:
        fb = E.get("pc_fenceB")
        if fb:
            kb.wait("Y", "dP0", fb[0])
            kb.wait("Y", "dP1", fb[1])
    n = 0
    for ft in range(8):
        n = kb.dma("Y", "dY0", lambda sy, ft=ft: sy.dma_start(
            E["xscr"][ft, :, :], E["X"][ft][:, :]))
    kb.wait("Y", "dY0", n)
    kb.full_barrier()


def ph_l0fr(kb, E, b):
    X, XR = E["X"], E["XR"]
    kb.full_barrier()
    nv = [0, 0]
    for i, (ft, hf) in enumerate([(f, h) for f in range(8) for h in range(2)]):
        par = i % 2
        if i >= 2:
            kb.wait("Y", "V", nv[par])
        kb.dma_begin("Y", "dY%d" % par)
        n = kb.dma("Y", "dY%d" % par, lambda sy, ft=ft, hf=hf, par=par: sy.dma_start(
            XR[par][:, :], E["xscr"][ft, :, hf * 1024:(hf + 1) * 1024]))
        kb.wait("V", "dY%d" % par, n)
        nv[par] = kb.op("V", lambda v, ft=ft, hf=hf, par=par: v.tensor_tensor(
            out=X[ft][:, hf * 1024:(hf + 1) * 1024], in0=X[ft][:, hf * 1024:(hf + 1) * 1024],
            in1=XR[par][:, :], op=ALU.add))
    kb.full_barrier()
    prev = None
    for c in range(NCH):
        xs = [X[ft][:, c * CH:(c + 1) * CH] for ft in range(8)]
        f = ln_stats_apply(kb, E, xs, CH, pp=c % 2, defer=True)
        if prev is not None:
            prev()
        prev = f
    prev()
    kb.full_barrier()


def _mk_l0f(l):
    def ph(kb, E, b):
        X, ABF, KT, VT, WB = E["X"], E["ABF"], E["KT"], E["VT"], E["WB"]
        PBF, PT = E["PBF"], E["PT"]
        PS_S, PS_T, PS_M = E["PS_S"], E["PS_T"], E["PS_M"]
        KTU, VTU = E["KTU"], E["VTU"]
        kb.full_barrier()
        _cast_shift_abf(kb, E, shift=True)
        _fold_xpos(kb, E, E["xposQC"], E["xposQS"], 4 * l + 0, 4 * l + 1)
        _l0f_qk_tiles(kb, E, l, 0)
        _fold_xpos(kb, E, E["xposKC"], E["xposKS"], 4 * l + 2, 4 * l + 3)
        _l0f_qk_tiles(kb, E, l, 1)
        wbase = E["l0_wqkv"][l]
        wb_mm = [0, 0]
        ps_ev = [0, 0]
        for f4 in range(4):
            half = f4 % 2
            kb.wait("Y", "T", wb_mm[half])
            _wstream_begin(kb, half)
            nW = _stream_w_cols(kb, E, wbase, 1024 + f4 * 128, D, 1536, half)
            kb.wait("T", "dZ%d" % half, nW)
            for tt4 in range(4):
                pb = tt4 % 2
                ps = PS_S[:, pb * CH:pb * CH + CH]
                kb.wait("T", "V", ps_ev[pb])
                nT = 0
                for sub in range(4):
                    tt = tt4 * 4 + sub
                    for kt in range(8):
                        nT = kb.op("T", lambda t, kt=kt, tt=tt, sub=sub, half=half, ps=ps: t.matmul(
                            ps[:, sub * 128:(sub + 1) * 128],
                            ABF[kt][:, tt * 128:(tt + 1) * 128],
                            WB[:, half * 1024 + kt * 128: half * 1024 + kt * 128 + 128],
                            start=(kt == 0), stop=(kt == 7)))
                wb_mm[half] = nT
                kb.wait("V", "T", nT)
                nE = 0
                for sub in range(4):
                    tt = tt4 * 4 + sub
                    nE = kb.op("V", lambda v, tt=tt, sub=sub, f4=f4, ps=ps: v.tensor_copy(
                        out=VT[tt][:, f4 * 128:(f4 + 1) * 128], in_=ps[:, sub * 128:(sub + 1) * 128]))
                ps_ev[pb] = nE
        kb.full_barrier()
        OT = ABF
        units = [(iq, h) for iq in range(QT) for h in range(L0F_HEADS)]
        sc_n = {}
        exp_n = {}
        aav_n = {}
        evac_n = {}
        ttr_n = {}
        tcp_n = {}
        ABFUa = E["ABFU"]

        def emit_unit_tail(u):
            iq, h = units[u]
            p = u % 2
            tp = iq % 2
            jb0 = max(0, iq - 4)
            nj = iq - jb0 + 1
            boff = p * 1024
            po = PS_M[:, p * 512:p * 512 + 65]
            kb.wait("T", "S", exp_n[u])
            if u >= 2:
                kb.wait("T", "V", evac_n[u - 2])
            nA = 0
            for j in range(nj):
                jb = jb0 + j
                nA = kb.op("T", lambda t, j=j, jb=jb, h=h, boff=boff, po=po: t.matmul(
                    po[:, 0:64], PBF[:, boff + j * 128:boff + j * 128 + 128],
                    VT[jb][:, h * 64:h * 64 + 64], start=(j == 0), stop=False))
                nA = kb.op("T", lambda t, j=j, nj=nj, boff=boff, po=po: t.matmul(
                    po[:, 64:65], PBF[:, boff + j * 128:boff + j * 128 + 128],
                    E["ONEC"][:, :], start=False, stop=(j == nj - 1)))
            aav_n[u] = nA
            kb.wait("V", "T", nA)
            nv = kb.op("V", lambda v, po=po, p=p: v.reciprocal(
                out=E["RCP"][:, p:p + 1], in_=po[:, 64:65]))
            kb.wait("V", "V", nv)
            if h == 0 and iq >= 2:
                kb.wait("V", "T", ttr_n[iq - 2])
            evac_n[u] = kb.op("V", lambda v, po=po, tp=tp, h=h, p=p: v.tensor_scalar(
                out=PT[:, tp * 512 + h * 64:tp * 512 + h * 64 + 64],
                in0=po[:, 0:64], scalar1=E["RCP"][:, p:p + 1], scalar2=None, op0=ALU.mult))

        def emit_tile_tail(iq):
            tp = iq % 2
            kb.wait("T", "V", evac_n[iq * L0F_HEADS + L0F_HEADS - 1])
            if iq >= 2:
                kb.wait("T", "V", tcp_n[iq - 2])
            nTt = 0
            for bqd in range(4):
                nTt = kb.op("T", lambda t, bqd=bqd, tp=tp: t.transpose(
                    PS_T[:, tp * 1024 + bqd * 128:tp * 1024 + bqd * 128 + 128],
                    PT[:, tp * 512 + bqd * 128:tp * 512 + bqd * 128 + 128], E["IDB"][:, :]))
            ttr_n[iq] = nTt
            kb.wait("V", "T", nTt)
            dst = bass.AP(ABFUa.tensor, ABFUa.offset + iq * 128,
                          [[8 * T, 128], [T, 4], [1, 128]])
            tcp_n[iq] = kb.op("V", lambda v, dst=dst, tp=tp: v.tensor_copy(
                out=dst, in_=PS_T[:, tp * 1024:tp * 1024 + 512]))

        for u, (iq, h) in enumerate(units):
            p = u % 2
            jb0 = max(0, iq - 4)
            nj = iq - jb0 + 1
            wj = nj * 128
            boff = p * 1024
            ft = h // 2
            rb = (h % 2) * 64
            if u >= 2:
                kb.wait("T", "S", exp_n[u - 2])
            nT = 0
            for j in range(nj):
                jb = jb0 + j
                need_diag = (jb == iq)
                need_far = (j == 0 and iq >= 4)
                nT = kb.op("T", lambda t, j=j, jb=jb, ft=ft, rb=rb, iq=iq, boff=boff,
                           stop=not (need_diag or need_far): t.matmul(
                    PS_S[:, boff + j * 128:boff + j * 128 + 128],
                    KT[4 + ft][rb:rb + 64, jb * 128:(jb + 1) * 128],
                    KT[ft][rb:rb + 64, iq * 128:(iq + 1) * 128],
                    start=True, stop=stop))
                if need_diag:
                    nT = kb.op("T", lambda t, j=j, boff=boff: t.matmul(
                        PS_S[:, boff + j * 128:boff + j * 128 + 128],
                        E["M0B"][:, :], E["IDB"][:, :], start=False, stop=True))
                if need_far:
                    nT = kb.op("T", lambda t, j=j, boff=boff: t.matmul(
                        PS_S[:, boff + j * 128:boff + j * 128 + 128],
                        E["M4B"][:, :], E["IDB"][:, :], start=False, stop=True))
            sc_n[u] = nT
            kb.wait("S", "T", sc_n[u])
            if u >= 2:
                kb.wait("S", "T", aav_n[u - 2])
            exp_n[u] = kb.op("S", lambda s, wj=wj, boff=boff: s.activation(
                out=PBF[:, boff:boff + wj], in_=PS_S[:, boff:boff + wj], func=AF.Exp))
            if u >= 1:
                emit_unit_tail(u - 1)
            if h == 1 and iq >= 1:
                emit_tile_tail(iq - 1)
        emit_unit_tail(len(units) - 1)
        emit_tile_tail(QT - 1)
        kb.full_barrier()
        wbase = E["l0_wo"][l]
        wb_mm = [0, 0]
        ps_ev = [0, 0]
        for ft in range(8):
            half = ft % 2
            kb.wait("Y", "T", wb_mm[half])
            _wstream_begin(kb, half)
            nW = _stream_w_cols(kb, E, wbase, ft * 128, 512, D, half)
            kb.wait("T", "dZ%d" % half, nW)
            for c in range(NCH):
                pb = c % 2
                ps = PS_S[:, pb * CH:pb * CH + CH]
                kb.wait("T", "V", ps_ev[pb])
                nT = 0
                for kt in range(4):
                    nT = kb.op("T", lambda t, kt=kt, c=c, half=half, ps=ps: t.matmul(
                        ps, WB[:, half * 1024 + kt * 128: half * 1024 + kt * 128 + 128],
                        OT[kt][:, c * CH:(c + 1) * CH], start=(kt == 0), stop=(kt == 3)))
                wb_mm[half] = nT
                kb.wait("V", "T", nT)
                ps_ev[pb] = kb.op("V", lambda v, ft=ft, c=c, ps=ps: v.tensor_tensor(
                    out=X[ft][:, c * CH:(c + 1) * CH], in0=X[ft][:, c * CH:(c + 1) * CH],
                    in1=ps, op=ALU.add))
        kb.full_barrier()
        _cast_shift_abf(kb, E, shift=True)
        gcols = [E["LNC"][:, l * 16 + ft:l * 16 + ft + 1] for ft in range(8)]
        bcols = [E["LNC"][:, l * 16 + 8 + ft:l * 16 + 8 + ft + 1] for ft in range(8)]
        prev = None
        for c in range(NCH):
            bfs = [ABF[ft][:, c * CH:(c + 1) * CH] for ft in range(8)]
            f = ln_stats_apply_bf(kb, E, bfs, CH, gcols, bcols, pp=c % 2, defer=True)
            if prev is not None:
                prev()
            prev = f
        prev()
        kb.full_barrier()
        w1b = E["l0_w1re"][l]
        w2b = E["l0_w2p"][l]
        W1S = [KTU[:, 0:1024], KTU[:, 1024:2048]]
        W2S = [KTU[:, 2048:2048 + 2816], KTU[:, 4864:4864 + 2816]]
        for rnd in range(3):
            fp0 = rnd * 8
            fp1 = min(NHT, fp0 + 8)
            wb_mm = [0, 0]
            h4_v = [0, 0]
            gel_s = [0, 0]
            for fp in range(fp0, fp1):
                half = fp % 2
                kb.wait("Y", "T", wb_mm[half])
                _wstream_begin(kb, half)
                nWa = _stream_w_cols(kb, E, w1b, (2 * fp) * 128, D, 2 * HIDP, half,
                                     dest=W1S[half][:, 0:1024])
                nWg = _stream_w_cols(kb, E, w1b, (2 * fp + 1) * 128, D, 2 * HIDP, half)
                kb.wait("T", "dZ%d" % half, nWg)
                acol = E["B1C"][:, l * 2 * NHT + 2 * fp: l * 2 * NHT + 2 * fp + 1]
                gcol = E["B1C"][:, l * 2 * NHT + 2 * fp + 1: l * 2 * NHT + 2 * fp + 2]
                for c in range(NCH):
                    p = c % 2
                    psa = PS_S[:, 2 * p * CH:(2 * p + 1) * CH]
                    psg = PS_S[:, (2 * p + 1) * CH:(2 * p + 2) * CH]
                    kb.wait("T", "V", h4_v[p])
                    kb.wait("T", "S", gel_s[p])
                    nTa = 0
                    for kt in range(8):
                        nTa = kb.op("T", lambda t, kt=kt, c=c, half=half, psa=psa: t.matmul(
                            psa, W1S[half][:, kt * 128:kt * 128 + 128],
                            ABF[kt][:, c * CH:(c + 1) * CH], start=(kt == 0), stop=(kt == 7)))
                    nTg = 0
                    for kt in range(8):
                        nTg = kb.op("T", lambda t, kt=kt, c=c, half=half, psg=psg: t.matmul(
                            psg, WB[:, half * 1024 + kt * 128: half * 1024 + kt * 128 + 128],
                            ABF[kt][:, c * CH:(c + 1) * CH], start=(kt == 0), stop=(kt == 7)))
                    wb_mm[half] = nTg
                    kb.wait("S", "T", nTg)
                    kb.wait("S", "V", h4_v[p])
                    nG = kb.op("S", lambda s, psg=psg, gcol=gcol, p=p: s.activation(
                        out=E["SQB"][:, p * CH:(p + 1) * CH], in_=psg, func=AF.Gelu,
                        bias=gcol[0:128, 0:1]))
                    gel_s[p] = nG
                    kb.wait("V", "T", nTa)
                    nA = kb.op("V", lambda v, psa=psa, acol=acol, p=p: v.tensor_scalar(
                        out=E["SCB"][:, p * CH:(p + 1) * CH], in0=psa, scalar1=acol[0:128, 0:1],
                        scalar2=None, op0=ALU.add))
                    kb.wait("V", "S", nG)
                    kb.wait("V", "V", nA)
                    h4_v[p] = kb.op("V", lambda v, fp=fp, fp0=fp0, c=c, p=p: v.tensor_tensor(
                        out=VTU[:, ((fp - fp0) * 4 + c) * CH:((fp - fp0) * 4 + c) * CH + CH],
                        in0=E["SCB"][:, p * CH:(p + 1) * CH],
                        in1=E["SQB"][:, p * CH:(p + 1) * CH], op=ALU.mult))
            kb.full_barrier()
            nkt = fp1 - fp0
            wb_mm = [0, 0]
            ps_ev = [0, 0]
            for ft in range(8):
                half = ft % 2
                kb.wait("Y", "T", wb_mm[half])
                _wstream_begin(kb, half)
                w2rows = bass.AP(w2b.tensor, w2b.offset + ft * 128 * HIDP + fp0 * 128,
                                 [[HIDP, 128], [1, nkt * 128]])
                nW = kb.dma("Y", "dZ%d" % half, lambda sy, re=w2rows, half=half, nkt=nkt: sy.dma_start(
                    W2S[half][:, 0:nkt * 128], re))
                kb.wait("T", "dZ%d" % half, nW)
                for c in range(NCH):
                    pb = c % 2
                    ps = PS_S[:, 2 * CH + pb * CH:2 * CH + pb * CH + CH]
                    kb.wait("T", "V", ps_ev[pb])
                    nT = 0
                    for kk in range(nkt):
                        nT = kb.op("T", lambda t, kk=kk, c=c, half=half, ps=ps, nkt=nkt: t.matmul(
                            ps, W2S[half][:, kk * 128:kk * 128 + 128],
                            VTU[:, (kk * 4 + c) * CH:(kk * 4 + c) * CH + CH],
                            start=(kk == 0), stop=(kk == nkt - 1)))
                    wb_mm[half] = nT
                    kb.wait("V", "T", nT)
                    if rnd < 2:
                        ps_ev[pb] = kb.op("V", lambda v, ft=ft, c=c, ps=ps: v.tensor_tensor(
                            out=X[ft][:, c * CH:(c + 1) * CH], in0=X[ft][:, c * CH:(c + 1) * CH],
                            in1=ps, op=ALU.add))
                    else:
                        b2col = E["B2C"][:, l * 8 + ft:l * 8 + ft + 1]
                        ps_ev[pb] = kb.op("V", lambda v, ft=ft, c=c, ps=ps, b2col=b2col: v.scalar_tensor_tensor(
                            out=X[ft][:, c * CH:(c + 1) * CH], in0=ps, scalar=b2col,
                            in1=X[ft][:, c * CH:(c + 1) * CH], op0=ALU.add, op1=ALU.add))
            kb.full_barrier()
    return ph


PH_FUNCS["l0fs"] = ph_l0fs
PH_FUNCS["l0fr"] = ph_l0fr
PH_FUNCS["l0f0"] = _mk_l0f(0)
PH_FUNCS["l0f1"] = _mk_l0f(1)



def ph_l1a(kb, E, b):
    X, ABF, KT, VT, QJ, WB, OE = E["X"], E["ABF"], E["KT"], E["VT"], E["QJ"], E["WB"], E["OE"]
    PBF, PT = E["PBF"], E["PT"]
    PS_S, PS_T, PS_M = E["PS_S"], E["PS_T"], E["PS_M"]
    BQC, BOC = E["BQC"], E["BOC"]
    wqkv = E["l1_wqkv"]
    kb.full_barrier()
    _cast_shift_abf(kb, E, shift=False)
    wb_mm = [0, 0]
    ps_ev = [0, 0]
    for ft in range(8):
        half = ft % 2
        kb.wait("Y", "T", wb_mm[half])
        _wstream_begin(kb, half)
        nW = _stream_w_cols(kb, E, wqkv, 1024 + ft * 128, D, 3 * D, half)
        kb.wait("T", "dZ%d" % half, nW)
        for c in range(NCH):
            pb = c % 2
            ps = PS_S[:, pb * CH:pb * CH + CH]
            kb.wait("T", "V", ps_ev[pb])
            nT = 0
            for kt in range(8):
                nT = kb.op("T", lambda t, kt=kt, c=c, half=half, ps=ps: t.matmul(
                    ps, WB[:, half * 1024 + kt * 128: half * 1024 + kt * 128 + 128],
                    ABF[kt][:, c * CH:(c + 1) * CH], start=(kt == 0), stop=(kt == 7)))
            wb_mm[half] = nT
            kb.wait("V", "T", nT)
            ps_ev[pb] = kb.op("V", lambda v, ft=ft, c=c, ps=ps: v.tensor_scalar(
                out=KT[ft][:, c * CH:(c + 1) * CH], in0=ps,
                scalar1=BQC[:, 8 + ft:9 + ft], scalar2=0.125, op0=ALU.add, op1=ALU.mult))
    kb.full_barrier()
    wb_mm = [0, 0]
    ps_ev = [0, 0]
    for f4 in range(8):
        half = f4 % 2
        kb.wait("Y", "T", wb_mm[half])
        _wstream_begin(kb, half)
        nW = _stream_w_cols(kb, E, wqkv, 2048 + f4 * 128, D, 3 * D, half)
        kb.wait("T", "dZ%d" % half, nW)
        for tt4 in range(4):
            pb = tt4 % 2
            ps = PS_S[:, pb * CH:pb * CH + CH]
            kb.wait("T", "V", ps_ev[pb])
            nT = 0
            for sub in range(4):
                tt = tt4 * 4 + sub
                for kt in range(8):
                    nT = kb.op("T", lambda t, kt=kt, tt=tt, sub=sub, half=half, ps=ps: t.matmul(
                        ps[:, sub * 128:(sub + 1) * 128],
                        ABF[kt][:, tt * 128:(tt + 1) * 128],
                        WB[:, half * 1024 + kt * 128: half * 1024 + kt * 128 + 128],
                        start=(kt == 0), stop=(kt == 7)))
            wb_mm[half] = nT
            kb.wait("V", "T", nT)
            nE = 0
            for sub in range(4):
                tt = tt4 * 4 + sub
                nE = kb.op("V", lambda v, tt=tt, sub=sub, f4=f4, ps=ps: v.tensor_copy(
                    out=VT[tt][:, f4 * 128:(f4 + 1) * 128], in_=ps[:, sub * 128:(sub + 1) * 128]))
            ps_ev[pb] = nE
    kb.full_barrier()
    for g in range(4):
        wb_mm = [0, 0]
        ps_ev = [0, 0]
        for ft in range(8):
            half = ft % 2
            kb.wait("Y", "T", wb_mm[half])
            _wstream_begin(kb, half)
            nW = _stream_w_cols(kb, E, wqkv, ft * 128, D, 3 * D, half)
            kb.wait("T", "dZ%d" % half, nW)
            kb.wait("T", "V", ps_ev[half])
            ps = PS_S[:, (ft % 2) * CH:(ft % 2) * CH + CH]
            nT = 0
            for kt in range(8):
                nT = kb.op("T", lambda t, kt=kt, g=g, half=half, ps=ps: t.matmul(
                    ps, WB[:, half * 1024 + kt * 128: half * 1024 + kt * 128 + 128],
                    ABF[kt][:, g * CH:(g + 1) * CH], start=(kt == 0), stop=(kt == 7)))
            wb_mm[half] = nT
            kb.wait("V", "T", nT)
            ps_ev[half] = kb.op("V", lambda v, ft=ft, ps=ps: v.tensor_scalar(
                out=QJ[ft][:, :], in0=ps, scalar1=BQC[:, ft:ft + 1], scalar2=None, op0=ALU.add))
        kb.full_barrier()
        units = [(iq, h) for iq in range(4 * g, 4 * g + 4) for h in range(16)]
        exp_n = {}
        expa_n = {}
        aav_n = {}
        evac_n = {}
        for u, (iq, h) in enumerate(units):
            if L1A_SERIAL:
                kb.full_barrier()
            jext = (iq + 1) * 128
            dbl = jext <= 1024
            boff = (u % 2) * 1024 if dbl else 0
            ft = h // 2
            rb = (h % 2) * 64
            il = (iq - 4 * g) * 128
            dd = 1 if (L1A_V1 or not dbl) else 2
            if u >= dd:
                key = expa_n.get(u - dd, exp_n[u - dd]) if not dbl else exp_n[u - dd]
                kb.wait("T", "S", key)
            nT = 0
            sc_a = 0
            for jb in range(iq + 1):
                last = (jb == iq)
                if not dbl and jb == 8:
                    sc_a = nT
                    kb.wait("T", "S", exp_n[u - 1] if u >= 1 else 0)
                nT = kb.op("T", lambda t, jb=jb, ft=ft, rb=rb, il=il, boff=boff, last=last: t.matmul(
                    PS_S[:, boff + jb * 128:boff + jb * 128 + 128],
                    KT[ft][rb:rb + 64, jb * 128:(jb + 1) * 128],
                    QJ[ft][rb:rb + 64, il:il + 128],
                    start=True, stop=not last))
            nT = kb.op("T", lambda t, jext=jext, boff=boff: t.matmul(
                PS_S[:, boff + jext - 128:boff + jext],
                E["M0B"][:, :], E["IDB"][:, :], start=False, stop=True))
            sc_n = nT
            if not dbl:
                slot = u % 2
                PTb = E["PT"] if slot == 0 else E["PBF"]
                kb.wait("S", "T", sc_a)
                dd2 = 1 if L1A_V2 else 2
                if u >= dd2 and (u - dd2) in aav_n:
                    kb.wait("S", "T", aav_n[u - dd2])
                expa_n[u] = kb.op("S", lambda s, PTb=PTb: s.activation(
                    out=PTb[:, 0:1024], in_=PS_S[:, 0:1024], func=AF.Exp))
            if u >= 1:
                pq, ph_ = units[u - 1]
                pjext = (pq + 1) * 128
                pslot = (u - 1) % 2
                PTp = E["PT"] if pslot == 0 else E["PBF"]
                po = PS_M[:, pslot * CH:pslot * CH + 65]
                kb.wait("T", "S", exp_n[u - 1])
                dd3 = 2 if L1A_V3 else 3
                if u >= dd3 and (u - dd3) in evac_n:
                    kb.wait("T", "V", evac_n[u - dd3])
                nA = 0
                for jb in range(pq + 1):
                    nA = kb.op("T", lambda t, jb=jb, ph_=ph_, po=po, PTp=PTp, pq=pq: t.matmul(
                        po[:, 0:64], PTp[:, jb * 128:(jb + 1) * 128],
                        VT[jb][:, ph_ * 64:ph_ * 64 + 64],
                        start=(jb == 0), stop=False))
                    nA = kb.op("T", lambda t, jb=jb, po=po, PTp=PTp, pq=pq: t.matmul(
                        po[:, 64:65], PTp[:, jb * 128:(jb + 1) * 128],
                        E["ONEC"][:, :], start=False, stop=(jb == pq)))
                aav_n[u - 1] = nA
                kb.wait("V", "T", nA)
                rp = (u - 1) % 2
                nv = kb.op("V", lambda v, po=po, rp=rp: v.reciprocal(
                    out=E["RCP"][:, rp:rp + 1], in_=po[:, 64:65]))
                kb.wait("V", "V", nv)
                piq = pq
                evac_n[u - 1] = kb.op("V", lambda v, po=po, piq=piq, ph_=ph_, g=g, rp=rp: v.tensor_scalar(
                    out=E["OEU"][:, (piq - 4 * g) * 1024 + ph_ * 64:(piq - 4 * g) * 1024 + ph_ * 64 + 64],
                    in0=po[:, 0:64], scalar1=E["RCP"][:, rp:rp + 1], scalar2=None, op0=ALU.mult))
            slot = u % 2
            PTb = E["PT"] if slot == 0 else E["PBF"]
            kb.wait("S", "T", sc_n)
            dd2 = 1 if L1A_V2 else 2
            if dbl and u >= dd2 and (u - dd2) in aav_n:
                kb.wait("S", "T", aav_n[u - dd2])
            if dbl:
                exp_n[u] = kb.op("S", lambda s, jext=jext, boff=boff, PTb=PTb: s.activation(
                    out=PTb[:, 0:jext], in_=PS_S[:, boff:boff + jext], func=AF.Exp))
            else:
                exp_n[u] = kb.op("S", lambda s, jext=jext, PTb=PTb: s.activation(
                    out=PTb[:, 1024:jext], in_=PS_S[:, 1024:jext], func=AF.Exp))
        u = len(units) - 1
        iq, h = units[u]
        jext = (iq + 1) * 128
        slot = u % 2
        PTb = E["PT"] if slot == 0 else E["PBF"]
        po = PS_M[:, slot * CH:slot * CH + 65]
        kb.wait("T", "S", exp_n[u])
        kb.wait("T", "V", evac_n.get(u - 2, 0))
        nA = 0
        for jb in range(iq + 1):
            nA = kb.op("T", lambda t, jb=jb, h=h, po=po, PTb=PTb, iq=iq: t.matmul(
                po[:, 0:64], PTb[:, jb * 128:(jb + 1) * 128],
                VT[jb][:, h * 64:h * 64 + 64], start=(jb == 0), stop=False))
            nA = kb.op("T", lambda t, jb=jb, po=po, PTb=PTb, iq=iq: t.matmul(
                po[:, 64:65], PTb[:, jb * 128:(jb + 1) * 128],
                E["ONEC"][:, :], start=False, stop=(jb == iq)))
        kb.wait("V", "T", nA)
        rp = u % 2
        nv = kb.op("V", lambda v, po=po, rp=rp: v.reciprocal(
            out=E["RCP"][:, rp:rp + 1], in_=po[:, 64:65]))
        kb.wait("V", "V", nv)
        kb.op("V", lambda v, po=po, iq=iq, h=h, g=g, rp=rp: v.tensor_scalar(
            out=E["OEU"][:, (iq - 4 * g) * 1024 + h * 64:(iq - 4 * g) * 1024 + h * 64 + 64],
            in0=po[:, 0:64], scalar1=E["RCP"][:, rp:rp + 1], scalar2=None, op0=ALU.mult))
        kb.full_barrier()
        QJU = E["QJU"]
        scp = [0, 0]
        for it in range(4):
            for grp in range(2):
                bank = (it * 2 + grp) % 2
                if it * 2 + grp >= 2:
                    kb.wait("T", "S", scp[bank])
                nTt = 0
                for k in range(4):
                    ft = grp * 4 + k
                    nTt = kb.op("T", lambda t, it=it, ft=ft, k=k, bank=bank: t.transpose(
                        PS_T[:, bank * 1024 + k * 128: bank * 1024 + k * 128 + 128],
                        E["OEU"][:, it * 1024 + ft * 128:it * 1024 + ft * 128 + 128], E["IDB"][:, :]))
                kb.wait("S", "T", nTt)
                dst = bass.AP(QJU.tensor, QJU.offset + grp * 4 * CH + it * 128,
                              [[4096, 128], [CH, 4], [1, 128]])
                scp[bank] = kb.op("S", lambda s, dst=dst, bank=bank: s.copy(
                    out=dst, in_=PS_T[:, bank * 1024: bank * 1024 + 512]))
        kb.full_barrier()
        wout = E["l1_wout"]
        if g == 0:
            nbv = kb.op("V", lambda v: v.tensor_copy(out=E["BVB"][:, :], in_=E["BQC"][:, 16:24]))
            kb.wait("T", "V", nbv)
        wb_mm2 = [0, 0]
        ps_ev2 = [0, 0]
        for ft in range(8):
            half = ft % 2
            kb.wait("Y", "T", wb_mm2[half])
            _wstream_begin(kb, half)
            nW = _stream_w_cols(kb, E, wout, ft * 128, D, D, half)
            kb.wait("T", "dZ%d" % half, nW)
            kb.wait("T", "V", ps_ev2[half])
            ps = PS_S[:, (ft % 2) * CH:(ft % 2) * CH + CH]
            nT = 0
            for kt in range(8):
                nT = kb.op("T", lambda t, kt=kt, half=half, ps=ps: t.matmul(
                    ps, WB[:, half * 1024 + kt * 128: half * 1024 + kt * 128 + 128],
                    QJ[kt][:, :], start=(kt == 0), stop=(kt == 7)))
            wb_mm2[half] = nT
            if g == 0:
                kb.wait("T", "V", kb.c["V"])
                nb2 = 0
                for kt in range(8):
                    nb2 = kb.op("T", lambda t, kt=kt, half=half: t.matmul(
                        PS_M[:, 65:66], WB[:, half * 1024 + kt * 128: half * 1024 + kt * 128 + 128],
                        E["BVB"][:, kt:kt + 1],
                        start=(kt == 0), stop=(kt == 7)))
                wb_mm2[half] = nb2
                kb.wait("V", "T", nb2)
                nbb = kb.op("V", lambda v, ft=ft: v.tensor_tensor(
                    out=E["BOC2"][:, ft:ft + 1], in0=PS_M[:, 65:66], in1=E["BOC"][:, ft:ft + 1],
                    op=ALU.add))
                kb.wait("V", "V", nbb)
            kb.wait("V", "T", nT)
            ps_ev2[half] = kb.op("V", lambda v, ft=ft, g=g, ps=ps: v.scalar_tensor_tensor(
                out=X[ft][:, g * CH:(g + 1) * CH], in0=ps, scalar=E["BOC2"][:, ft:ft + 1],
                in1=X[ft][:, g * CH:(g + 1) * CH], op0=ALU.add, op1=ALU.add))
        kb.full_barrier()
        xs = [X[ft][:, g * CH:(g + 1) * CH] for ft in range(8)]
        ln_stats_apply(kb, E, xs, CH, pp=g % 2)
        kb.full_barrier()


def ph_l1f(kb, E, b):
    X, ABF, WB = E["X"], E["ABF"], E["WB"]
    PS_S, PS_T, PS_M = E["PS_S"], E["PS_T"], E["PS_M"]
    QJ = E["QJ"]
    OE = E["OE"]
    MS = E["TB0"]
    BQC = E["BQC"]
    kb.full_barrier()
    _cast_shift_abf(kb, E, shift=False)

    def bank(k):
        if k < 4:
            return PS_S[:, k * CH:(k + 1) * CH]
        return PS_M[:, (k - 4) * CH:(k - 3) * CH]

    GW = E["SR"]
    gw_ap = bass.AP(E["gate_wT"].tensor, E["gate_wT"].offset,
                    [[8, 128], [1024, 8], [1, 8]])
    kb.dma_begin("Y", "dY0")
    ngw = kb.dma("Y", "dY0", lambda sy: sy.dma_start(GW[:, 0:64], gw_ap))
    kb.wait("T", "dY0", ngw)
    for tt in range(QT):
        kb.wait("T", "V", kb.c["V"])
        kb.wait("T", "S", kb.c["S"])
        nT = 0
        for kt in range(8):
            nT = kb.op("T", lambda t, kt=kt, tt=tt: t.matmul(
                PS_M[:, 0:8], X[kt][:, tt * 128:(tt + 1) * 128],
                GW[:, kt * 8:(kt + 1) * 8], start=(kt == 0), stop=(kt == 7)))
        kb.wait("V", "T", nT)
        nV = kb.op("V", lambda v: v.reduce_max(out=E["NMX"][:, 0:1], in_=PS_M[:, 0:8], axis=AX.X))
        kb.wait("V", "V", nV)
        nV = kb.op("V", lambda v: v.tensor_scalar(
            out=E["SCB"][:, 0:8], in0=PS_M[:, 0:8], scalar1=E["NMX"][:, 0:1],
            scalar2=None, op0=ALU.is_ge))
        kb.wait("T", "V", nV)
        nTt = kb.op("T", lambda t, tt=tt: t.transpose(
            PS_T[0:8, (tt % 2) * 1024:(tt % 2) * 1024 + 128], E["SCB"][:, 0:8], E["IDB"][:, :]))
        kb.wait("S", "T", nTt)
        kb.op("S", lambda s, tt=tt: s.copy(
            out=MS[0:8, tt * 128:(tt + 1) * 128],
            in_=PS_T[0:8, (tt % 2) * 1024:(tt % 2) * 1024 + 128]))
    kb.full_barrier()

    ew = E["ew_T"]
    for c in range(NCH):
        prevS = 0
        mask_s = {}
        for e in range(8):
            kb.wait("T", "S", prevS)
            nb = kb.op("T", lambda t, e=e, c=c: t.matmul(
                PS_M[:, CH:2 * CH], E["SELSb"][:, e * 128:(e + 1) * 128],
                MS[0:8, c * CH:(c + 1) * CH], start=True, stop=True))
            kb.wait("S", "T", nb)
            prevS = kb.op("S", lambda s, e=e: s.copy(out=QJ[e][:, :], in_=PS_M[:, CH:2 * CH]))
            mask_s[e] = prevS
        kb.wait("T", "S", prevS)
        PTa, PBFa = E["PT"], E["PBF"]

        def xe_ap(s_, kt):
            if s_ == 0:
                return OE[kt][:, :]
            return (PTa if kt < 4 else PBFa)[:, (kt % 4) * CH:(kt % 4 + 1) * CH]
        ev_n = 0
        lastT = 0
        xe_mm = [0, 0]
        wb_cons = [0, 0]
        for (f0, f1) in ((0, 6), (6, 8)):
            for e in range(8):
                xs = e % 2
                kb.wait("V", "T", xe_mm[xs])
                kb.wait("V", "S", mask_s[e])
                nxe = 0
                for kt in range(8):
                    nxe = kb.op("V", lambda v, kt=kt, c=c, e=e, xs=xs: v.tensor_tensor(
                        out=xe_ap(xs, kt), in0=ABF[kt][:, c * CH:(c + 1) * CH], in1=QJ[e][:, :],
                        op=ALU.mult))
                fts = list(range(f0, f1))
                qgroups = [fts[i:i + 4] for i in range(0, len(fts), 4)]
                qs = E.setdefault("_l1fq", {"k": 0, "ne": 0, "st": {}, "cons": {}})

                def _qdesc(k2):
                    j = k2 % 24
                    if j < 16:
                        return j // 2, ([0, 1, 2, 3] if j % 2 == 0 else [4, 5])
                    return j - 16, [6, 7]

                def _qemit(k2):
                    e2, qg2 = _qdesc(k2)
                    s2 = k2 % 4
                    sem = "dZ%d" % (k2 % 2)
                    if k2 >= 4:
                        kb.wait("Y", "T", qs["cons"][k2 - 4])
                    kb.dma_begin("Y", sem)
                    re = _wtile_ap(ew[e2], qg2[0], D, ntile=len(qg2))
                    dst = E["VTU"][:, s2 * 4096:s2 * 4096 + len(qg2) * 1024]
                    nW = kb.dma("Y", sem, lambda sy, re=re, dst=dst: sy.dma_start(dst, re))
                    qs["st"][k2] = (nW, sem, s2)

                TOTALQ = 24 * NCH * 2
                for qg in qgroups:
                    k = qs["k"]
                    qs["k"] += 1
                    while qs["ne"] <= min(k + 3, (k // 96) * 96 + 95):
                        _qemit(qs["ne"])
                        qs["ne"] += 1
                    nW, sem, s = qs["st"].pop(k)
                    WQ = E["VTU"][:, s * 4096:s * 4096 + len(qg) * 1024]
                    kb.wait("T", sem, nW)
                    kb.wait("T", "V", nxe)
                    if e == 0 and qg[0] == f0 and f0 == 0:
                        kb.wait("T", "V", ev_n)
                    for fi, ft in enumerate(qg):
                        for kt in range(8):
                            lastT = kb.op("T", lambda t, kt=kt, ft=ft, fi=fi, e=e, xs=xs, f0=f0, WQ=WQ: t.matmul(
                                bank(ft - f0),
                                WQ[:, fi * 1024 + kt * 128: fi * 1024 + kt * 128 + 128],
                                xe_ap(xs, kt), start=(e == 0 and kt == 0), stop=(e == 7 and kt == 7)))
                    qs["cons"][k] = lastT
                xe_mm[xs] = lastT
            kb.wait("V", "T", lastT)
            for ft in range(f0, f1):
                ev_n = kb.op("V", lambda v, ft=ft, c=c, f0=f0: v.tensor_tensor(
                    out=X[ft][:, c * CH:(c + 1) * CH], in0=X[ft][:, c * CH:(c + 1) * CH],
                    in1=bank(ft - f0), op=ALU.add))
        kb.full_barrier()
        xs = [X[ft][:, c * CH:(c + 1) * CH] for ft in range(8)]
        ln_stats_apply(kb, E, xs, CH, pp=c % 2)
        kb.full_barrier()


PH_FUNCS["l1a"] = ph_l1a
L1A_SERIAL = False
L1A_V1 = False
L1A_V2 = False
L1A_V3 = False
PH_FUNCS["l1f"] = ph_l1f


def _in_maps_for(w, C, xs):
    m = dict(w)
    m.update({k: C[k] for k in ("ropeC", "ropeS", "ident", "JTB", "RTB", "M0", "M4",
                                 "HS", "HST", "SELS", "xposQC", "xposQS", "xposKC", "xposKS")})
    m["x"] = xs
    return m


def kernel(**inputs):
    C = _consts()
    w = _prep_weights(inputs)
    x = np.ascontiguousarray(np.asarray(inputs["L0_attn_x"], np.float32))
    nc = build_core_program(["l0a", "l0fs", "l0f0", "l0f1", "l0fr", "l1a", "l1f"])
    in_maps = []
    for c in range(8):
        in_maps.append(_in_maps_for(w, C, np.ascontiguousarray(x[2 * c:2 * c + 2])))
    res = run_bass_kernel_spmd(nc, in_maps, core_ids=list(range(8)))
    out = np.concatenate([r["y"] for r in res.results], axis=0)
    return out.astype(np.float32)
